# version 11
# baseline (speedup 1.0000x reference)
import numpy as np
import ml_dtypes
from contextlib import ExitStack
import concourse.bass as bass
import concourse.mybir as mybir
from concourse.bass_utils import run_bass_kernel_spmd



"""Common infrastructure for the mk_prb_d2 megakernel builder.

Layouts (per core, B_loc=2 batches, T=4096 tokens, feature-major):
  - feature-major activation scratch in HBM: [8 dtile, 128, T] (token innermost)
  - token-major input/output: [2, 2048, 1024] (d innermost)
All emit functions append to the same per-engine streams via repeated
@block.<engine> sections; sync via Sem counters tracked in Python.
"""

F32 = mybir.dt.float32
BF16 = mybir.dt.bfloat16
AF = mybir.ActivationFunctionType
ALU = mybir.AluOpType
AX = mybir.AxisListType

D = 1024
NDT = 8
T = 4096
NSEQ = 2048
BL = 2
HID = 2048
NE = 8
I1 = 2816
NIT = 22
LN_EPS = 1e-5
EPS0 = 1e-6
ATTN_EPS = 1e-6


class Sem:
    def __init__(self, nc, ctx, name):
        self.h = ctx.enter_context(nc.semaphore(name))
        self.n = 0

    def inc(self, ins, k=1):
        ins.then_inc(self.h, k)
        self.n += k
        return self.n


class MK:

    def __init__(self, nc, ctx, phase_io=None):
        self.nc = nc
        self.ctx = ctx
        self.phase_io = phase_io
        self.dram = {}
        self.sbuf = {}
        self.guards = []
        self.phase = 0
        self.psum = []
        self.sd = Sem(nc, ctx, "sd")
        self.sg = Sem(nc, ctx, "sg")
        self.sp = Sem(nc, ctx, "sp")
        self.sv = Sem(nc, ctx, "sv")
        self.ss = Sem(nc, ctx, "ss")

    def dram_t(self, name, shape, dtype, kind):
        if name in self.dram:
            return self.dram[name]
        if self.phase_io is not None and kind == "Internal":
            role = self.phase_io.get(name)
            if role == "in":
                kind = "ExternalInput"
            elif role == "out":
                kind = "ExternalOutput"
        ap = self.nc.dram_tensor(name, shape, dtype, kind=kind).ap()
        self.dram[name] = ap
        return ap

    def sb(self, name, shape, dtype=F32, tname=None):
        if name in self.sbuf:
            return self.sbuf[name]
        real = (tname or name) + (f"_ph{self.phase}" if self.phase else "")
        guard = self.nc.sbuf_tensor(real, shape, dtype)
        h = guard.__enter__()
        self.guards.append(guard)
        self.sbuf[name] = h
        return h

    def free_phase_sbuf(self):
        for guard in reversed(self.guards):
            guard.__exit__(None, None, None)
        self.guards = []
        self.sbuf = {}
        self.phase += 1

    def psum_init(self):
        for i in range(8):
            if i == 7:
                self.psum.append(self.ctx.enter_context(
                    self.nc.psum_tensor(f"pb{i}", [128, 1024], BF16)))
            else:
                self.psum.append(self.ctx.enter_context(
                    self.nc.psum_tensor(f"pb{i}", [128, 512], F32)))

    def bank(self, i):
        return self.psum[i]


CONST_SPECS = {
    "c_idf": ([128, 128], "f32"),
    "c_idb": ([128, 128], "bf16"),
    "c_id8f": ([8, 8], "f32"),
    "c_sel8": ([8, 8 * 128], "f32"),
    "c_sel16": ([16, 8 * 128], "f32"),
}


def declare_consts(mk):
    for name, (shape, dt) in CONST_SPECS.items():
        mk.dram_t(name, shape, F32 if dt == "f32" else BF16, "ExternalInput")
        mk.sb(name, shape, F32 if dt == "f32" else BF16, tname=name + "_s")


def emit_load_consts(mk, block):
    @block.sync
    def _(q):
        for name in CONST_SPECS:
            ins = q.dma_start(out=mk.sbuf[name][:], in_=mk.dram[name][:, :])
            mk.sd.inc(ins, 16)
    return mk.sd.n


def build_consts():
    import numpy as np
    c = {}
    c["c_idf"] = np.eye(128, dtype=np.float32)
    import ml_dtypes
    c["c_idb"] = np.eye(128, dtype=np.float32).astype(ml_dtypes.bfloat16)
    c["c_id8f"] = np.eye(8, dtype=np.float32)
    sel8 = np.zeros((8, 8 * 128), dtype=np.float32)
    for e in range(8):
        sel8[e, e * 128:(e + 1) * 128] = 2.0 / 3.0
    c["c_sel8"] = sel8
    sel16 = np.zeros((16, 8 * 128), dtype=np.float32)
    for dt in range(8):
        sel16[2 * dt, dt * 128:dt * 128 + 64] = 1.0
        sel16[2 * dt + 1, dt * 128 + 64:dt * 128 + 128] = 1.0
    c["c_sel16"] = sel16
    return c


TBL_LN_EXP = 6
TBL_SILU = 18


def emit_preload_tbl(mk, block, sid):
    @block.scalar
    def _(s):
        s.add_instruction(mybir.InstLoadActFuncSet(
            name=s.bass.get_next_instruction_name(),
            act_func_set_id=sid, ins=[], outs=[]))


def emit_barrier(mk, block):
    counts = [(mk.sd, mk.sd.n), (mk.sg, mk.sg.n), (mk.sp, mk.sp.n),
              (mk.sv, mk.sv.n), (mk.ss, mk.ss.n)]

    @block.sync
    def _(q):
        for s, n in counts:
            q.wait_ge(s.h, n)

    @block.gpsimd
    def _(g):
        for s, n in counts:
            g.wait_ge(s.h, n)

    @block.tensor
    def _(t):
        for s, n in counts:
            t.wait_ge(s.h, n)

    @block.vector
    def _(v):
        for s, n in counts:
            v.wait_ge(s.h, n)

    @block.scalar
    def _(sc):
        for s, n in counts:
            sc.wait_ge(s.h, n)


def rel(a, b):
    a = np.asarray(a, np.float32); b = np.asarray(b, np.float32)
    return float(np.sqrt(((a - b) ** 2).mean() / max((b ** 2).mean(), 1e-30)))


def fm(x_tok_major):
    B, N, _ = x_tok_major.shape
    return np.ascontiguousarray(
        x_tok_major.reshape(B * N, NDT, 128).transpose(1, 2, 0))


def fm_inv(x_fm):
    return np.ascontiguousarray(
        x_fm.transpose(2, 0, 1).reshape(BL, NSEQ, D))


"""P0(qkv part) + P1a emitter: input transpose, LN0, QKV+decay, KV/Ksum.

Feature-major activations [dtile 8][128 part][tokens]. Chunk = 512 tokens.
"""

CH = 512
NCH = T // CH


def declare_p1a(mk):
    mk.dram_t("x0", [BL, NSEQ, D], F32, "ExternalInput")
    for w in ["W_q", "W_k", "W_v"]:
        mk.dram_t(w, [D, D], F32, "ExternalInput")
    for b in ["b_q", "b_k", "b_v"]:
        mk.dram_t(b, [D], F32, "ExternalInput")
    mk.dram_t("wq16", [3, D, D], BF16, "Internal")
    mk.dram_t("x0f", [NDT, 128, T], F32, "Internal")
    mk.dram_t("xn0", [NDT, 128, T], BF16, "Internal")
    mk.dram_t("qf", [NDT, 128, T], BF16, "Internal")
    mk.dram_t("kvks", [128, 16 * 64 + 16], F32, "Internal")

    mk.sb("wqkvT", [128, 3 * 8 * 1024], BF16)
    mk.sb("bqkv", [128, 3 * 8], F32)
    mk.sb("dec", [128, NSEQ], F32)
    mk.sb("decrow", [1, NSEQ], F32)
    mk.sb("onesf", [1, 128], F32)
    mk.sb("onescol", [128, 1], F32)
    mk.sb("idf", [128, 128], F32)
    mk.sb("idb", [128, 128], BF16)
    mk.sb("x0tm", [128, 4 * 1024], F32)
    mk.sb("x0fc", [128, 8 * CH], F32)
    mk.sb("xsq", [128, 8 * CH], F32)
    mk.sb("stat", [1, 4 * CH], F32)
    mk.sb("xn16", [128, 8 * CH], BF16)
    mk.sb("qrelu", [128, 2 * CH], F32)
    mk.sb("qf16c", [128, 8 * CH], BF16)
    mk.sb("kf16c", [128, 8 * CH], BF16)
    mk.sb("v16c", [128, 8 * CH], BF16)
    mk.sb("kfT", [128, 4 * 128], BF16)
    mk.sb("vT", [128, 4 * 128], BF16)
    mk.sb("red1", [128, 1], F32)
    mk.sb("cst", [1, 4], F32)
    mk.sb("KV", [128, 16 * 64], F32)
    mk.sb("Ksum", [128, 16], F32)


def emit_const_prep(mk, block):
    idf, idb, onesf = mk.sbuf["idf"], mk.sbuf["idb"], mk.sbuf["onesf"]
    onescol = mk.sbuf["onescol"]
    decrow = mk.sbuf["decrow"]

    @block.gpsimd
    def _(g):
        g.memset(idf[:], 0.0)
        g.affine_select(out=idf[:], in_=idf[:], compare_op=ALU.not_equal, fill=1.0,
                        base=0, pattern=[[-1, 128]], channel_multiplier=1)
        g.memset(idb[:], 0.0)
        g.affine_select(out=idb[:], in_=idb[:], compare_op=ALU.not_equal, fill=1.0,
                        base=0, pattern=[[-1, 128]], channel_multiplier=1)
        g.memset(onesf[:], 1.0)
        g.memset(onescol[:], 1.0)
        g.memset(mk.sbuf["cst"][0:1, 0:1], LN_EPS)
        g.memset(mk.sbuf["cst"][0:1, 1:2], 1e-6)
        ins = g.iota(decrow[:], pattern=[[1, NSEQ]], base=0, channel_multiplier=0,
                     allow_small_or_imprecise_dtypes=True)
        mk.sg.inc(ins)

    gp_done = mk.sg.n

    @block.scalar
    def _(s):
        s.wait_ge(mk.sg.h, gp_done)
        ins = s.activation(decrow[:], decrow[:], AF.Exp, bias=0.0, scale=-1.0 / NSEQ)
        mk.ss.inc(ins)


def emit_p0_qkv(mk, block):
    wq16 = mk.dram["wq16"]
    wqkvT = mk.sbuf["wqkvT"]
    bqkv = mk.sbuf["bqkv"]

    @block.gpsimd
    def _(g):
        for i, w in enumerate(["W_q", "W_k", "W_v"]):
            ins = g.dma_start(out=wq16[i, :, :], in_=mk.dram[w][:, :])
            mk.sg.inc(ins, 16)
    casts_done = mk.sg.n

    @block.sync
    def _(q):
        q.wait_ge(mk.sg.h, casts_done)
        for i in range(3):
            for kt in range(NDT):
                ins = q.dma_start_transpose(
                    out=wqkvT[:, (i * 8 + kt) * 1024:(i * 8 + kt + 1) * 1024],
                    in_=wq16[i, :, kt * 128:(kt + 1) * 128])
                mk.sd.inc(ins, 16)
        for i, b in enumerate(["b_q", "b_k", "b_v"]):
            ins = q.dma_start(out=bqkv[:, i * 8:(i + 1) * 8],
                              in_=mk.dram[b].rearrange("(dt p) -> p dt", p=128),
                              allow_slow_non_contiguous=True)
            mk.sd.inc(ins, 16)


def emit_dec_bcast_p1a(mk, block):
    dec, decrow, onesf = mk.sbuf["dec"], mk.sbuf["decrow"], mk.sbuf["onesf"]
    row_ready = mk.ss.n
    pe_hist = []
    sc_hist = []
    for i in range(4):
        @block.tensor
        def _(t, i=i):
            if i == 0:
                t.wait_ge(mk.ss.h, row_ready)
            if i >= 2:
                t.wait_ge(mk.ss.h, sc_hist[i - 2])
            ins = t.matmul(mk.bank(4 + i % 2)[:, 0:512], onesf[:],
                           decrow[:, i * 512:(i + 1) * 512], start=True, stop=True)
            pe_hist.append(mk.sp.inc(ins))

        @block.scalar
        def _(s, i=i):
            s.wait_ge(mk.sp.h, pe_hist[i])
            ins = s.copy(dec[:, i * 512:(i + 1) * 512], mk.bank(4 + i % 2)[:, 0:512])
            sc_hist.append(mk.ss.inc(ins))


def emit_p1a(mk, block):
    B = mk.sbuf
    x0 = mk.dram["x0"]
    x0f_h, xn0_h, qf_h = mk.dram["x0f"], mk.dram["xn0"], mk.dram["qf"]

    x0tm, x0fc, xsq, stat = B["x0tm"], B["x0fc"], B["xsq"], B["stat"]
    xn16, qrelu = B["xn16"], B["qrelu"]
    qf16c, kf16c, v16c = B["qf16c"], B["kf16c"], B["v16c"]
    kfT, vT, red1, KV, Ksum = B["kfT"], B["vT"], B["red1"], B["KV"], B["Ksum"]
    wqkvT, bqkv, dec, onesf, onescol, idf, idb = (
        B["wqkvT"], B["bqkv"], B["dec"], B["onesf"], B["onescol"], B["idf"], B["idb"])

    @block.vector
    def _(v):
        v.memset(KV[:], 0.0)
        ins = v.memset(Ksum[:], 0.0)
        mk.sv.inc(ins)

    prep_d = mk.sd.n
    prep_s = mk.ss.n

    for c in range(NCH):
        b = c // 4
        pos0 = (c % 4) * CH
        decs = dec[:, pos0:pos0 + CH]

        x0tm_free = mk.sp.n

        @block.sync
        def _(q, b=b, pos0=pos0, x0tm_free=x0tm_free):
            q.wait_ge(mk.sp.h, x0tm_free)
            ins = q.dma_start(out=x0tm.rearrange("p (g d) -> p g d", g=4),
                              in_=x0[b, pos0:pos0 + CH, :].rearrange("(g p) d -> p g d", p=128))
            mk.sd.inc(ins, 16)
        ld_x0 = mk.sd.n

        tr_pe = []
        tr_sc = []
        for i in range(32):
            g, dt = divmod(i, NDT)

            @block.tensor
            def _(t, i=i, g=g, dt=dt):
                if i == 0:
                    t.wait_ge(mk.sd.h, ld_x0)
                    t.wait_ge(mk.ss.h, prep_s)
                if i >= 2:
                    t.wait_ge(mk.ss.h, tr_sc[i - 2])
                ins = t.transpose(mk.bank(4 + i % 2)[:, 0:128],
                                  x0tm[:, g * 1024 + dt * 128: g * 1024 + (dt + 1) * 128],
                                  idf[:])
                tr_pe.append(mk.sp.inc(ins))

            @block.scalar
            def _(s, i=i, g=g, dt=dt):
                s.wait_ge(mk.sp.h, tr_pe[i])
                ins = s.copy(x0fc[:, dt * CH + g * 128: dt * CH + (g + 1) * 128],
                             mk.bank(4 + i % 2)[:, 0:128])
                tr_sc.append(mk.ss.inc(ins))

        @block.scalar
        def _(s):
            for dt in range(NDT):
                ins = s.activation(xsq[:, dt * CH:(dt + 1) * CH],
                                   x0fc[:, dt * CH:(dt + 1) * CH], AF.Square)
                mk.ss.inc(ins)
        sq_done = mk.ss.n

        @block.sync
        def _(q, c=c):
            q.wait_ge(mk.ss.h, sq_done)
            ins = q.dma_start(out=x0f_h[:, :, c * CH:(c + 1) * CH].rearrange("dt p c -> p dt c"),
                              in_=x0fc.rearrange("p (dt c) -> p dt c", dt=8))
            mk.sd.inc(ins, 16)

        @block.tensor
        def _(t):
            t.wait_ge(mk.ss.h, sq_done)
            for dt in range(NDT):
                t.matmul(mk.bank(0)[0:1, 0:CH], onescol[:], x0fc[:, dt * CH:(dt + 1) * CH],
                         start=(dt == 0), stop=(dt == 7))
            for dt in range(NDT):
                ins = t.matmul(mk.bank(1)[0:1, 0:CH], onescol[:], xsq[:, dt * CH:(dt + 1) * CH],
                               start=(dt == 0), stop=(dt == 7))
            mk.sp.inc(ins)
        stats_pe = mk.sp.n

        @block.vector
        def _(v):
            v.wait_ge(mk.sp.h, stats_pe)
            v.tensor_scalar_mul(stat[:, 0:CH], mk.bank(0)[0:1, 0:CH], 1.0 / D)
            v.tensor_mul(stat[:, CH:2 * CH], stat[:, 0:CH], stat[:, 0:CH])
            ins = v.scalar_tensor_tensor(stat[:, 2 * CH:3 * CH], mk.bank(1)[0:1, 0:CH],
                                         1.0 / D, stat[:, CH:2 * CH], ALU.mult, ALU.subtract)
            mk.sv.inc(ins)
        var_v = mk.sv.n

        @block.scalar
        def _(s):
            s.wait_ge(mk.sv.h, var_v)
            s.activation(stat[:, 3 * CH:4 * CH], stat[:, 2 * CH:3 * CH], AF.Ln,
                         bias=mk.sbuf["cst"][0:1, 0:1], scale=1.0)
            ins = s.activation(stat[:, 3 * CH:4 * CH], stat[:, 3 * CH:4 * CH], AF.Exp,
                               bias=0.0, scale=-0.5)
            mk.ss.inc(ins)
        rstd_s = mk.ss.n

        @block.tensor
        def _(t):
            t.wait_ge(mk.ss.h, rstd_s)
            t.matmul(mk.bank(2)[:, 0:CH], onesf[:], stat[0:1, 0:CH], start=True, stop=True)
            ins = t.matmul(mk.bank(3)[:, 0:CH], onesf[:], stat[0:1, 3 * CH:4 * CH],
                           start=True, stop=True)
            mk.sp.inc(ins)
        bc_pe = mk.sp.n

        @block.vector
        def _(v):
            v.wait_ge(mk.sp.h, bc_pe)
            for dt in range(NDT):
                v.tensor_tensor(xsq[:, dt * CH:(dt + 1) * CH], x0fc[:, dt * CH:(dt + 1) * CH],
                                mk.bank(2)[:, 0:CH], ALU.subtract)
                ins = v.tensor_tensor(xn16[:, dt * CH:(dt + 1) * CH],
                                      xsq[:, dt * CH:(dt + 1) * CH],
                                      mk.bank(3)[:, 0:CH], ALU.mult)
                mk.sv.inc(ins)
        xn_done = mk.sv.n

        @block.sync
        def _(q, c=c):
            q.wait_ge(mk.sv.h, xn_done)
            ins = q.dma_start(out=xn0_h[:, :, c * CH:(c + 1) * CH].rearrange("dt p c -> p dt c"),
                              in_=xn16.rearrange("p (dt c) -> p dt c", dt=8))
            mk.sd.inc(ins, 16)

        outs = {"k": kf16c, "v": v16c, "q": qf16c}
        wi_of = {"q": 0, "k": 1, "v": 2}
        mm_pe, epi_sc, epi_sv = [], [], []
        idx = 0
        for wname in ["k", "v", "q"]:
            wi = wi_of[wname]
            for mt in range(NDT):
                @block.tensor
                def _(t, wi=wi, mt=mt, idx=idx):
                    if idx == 0:
                        t.wait_ge(mk.sv.h, xn_done)
                        if c == 0:
                            t.wait_ge(mk.sd.h, prep_d)
                    if idx >= 2:
                        t.wait_ge(mk.ss.h, epi_sc[idx - 2])
                    bank = mk.bank(4 + idx % 2)
                    for kt in range(NDT):
                        ins = t.matmul(
                            bank[:, 0:CH],
                            wqkvT[:, (wi * 8 + kt) * 1024 + mt * 128:
                                  (wi * 8 + kt) * 1024 + (mt + 1) * 128],
                            xn16[:, kt * CH:(kt + 1) * CH],
                            start=(kt == 0), stop=(kt == 7))
                    mm_pe.append(mk.sp.inc(ins))

                if wname in ("q", "k"):
                    @block.scalar
                    def _(s, wi=wi, mt=mt, idx=idx):
                        s.wait_ge(mk.sp.h, mm_pe[idx])
                        if idx >= 2 and len(epi_sv) >= idx - 1:
                            pass
                        if idx >= 2:
                            s.wait_ge(mk.sv.h, epi_sv[idx - 2])
                        ins = s.activation(qrelu[:, (idx % 2) * CH:(idx % 2 + 1) * CH],
                                           mk.bank(4 + idx % 2)[:, 0:CH], AF.Relu,
                                           bias=bqkv[:, wi * 8 + mt:wi * 8 + mt + 1], scale=1.0)
                        epi_sc.append(mk.ss.inc(ins))

                    @block.vector
                    def _(v, wname=wname, mt=mt, idx=idx, decs=decs):
                        v.wait_ge(mk.ss.h, epi_sc[idx])
                        ins = v.tensor_tensor(outs[wname][:, mt * CH:(mt + 1) * CH],
                                              qrelu[:, (idx % 2) * CH:(idx % 2 + 1) * CH],
                                              decs, ALU.mult)
                        epi_sv.append(mk.sv.inc(ins))
                else:
                    @block.scalar
                    def _(s, wi=wi, mt=mt, idx=idx):
                        s.wait_ge(mk.sp.h, mm_pe[idx])
                        ins = s.activation(v16c[:, mt * CH:(mt + 1) * CH],
                                           mk.bank(4 + idx % 2)[:, 0:CH],
                                           AF.Identity, bias=bqkv[:, wi * 8 + mt:wi * 8 + mt + 1],
                                           scale=1.0)
                        epi_sc.append(mk.ss.inc(ins))
                    epi_sv.append(mk.sv.n)
                idx += 1

        qkv_done_v = mk.sv.n
        qkv_done_s = mk.ss.n

        @block.sync
        def _(q, c=c):
            q.wait_ge(mk.sv.h, qkv_done_v)
            ins = q.dma_start(out=qf_h[:, :, c * CH:(c + 1) * CH].rearrange("dt p c -> p dt c"),
                              in_=qf16c.rearrange("p (dt c) -> p dt c", dt=8))
            mk.sd.inc(ins, 16)

        for dt in range(NDT):
            slot = dt * BL + b
            prev_v = mk.sv.n

            @block.tensor
            def _(t, dt=dt, prev_v=prev_v):
                t.wait_ge(mk.sv.h, prev_v)
                t.wait_ge(mk.ss.h, qkv_done_s)
                for g in range(4):
                    t.transpose(mk.bank(7)[:, g * 128:(g + 1) * 128],
                                kf16c[:, dt * CH + g * 128:dt * CH + (g + 1) * 128], idb[:])
                for g in range(4):
                    ins = t.transpose(mk.bank(7)[:, 512 + g * 128:512 + (g + 1) * 128],
                                      v16c[:, dt * CH + g * 128:dt * CH + (g + 1) * 128], idb[:])
                mk.sp.inc(ins)
            tr_done = mk.sp.n

            @block.vector
            def _(v, tr_done=tr_done):
                v.wait_ge(mk.sp.h, tr_done)
                v.tensor_copy(kfT[:], mk.bank(7)[:, 0:512])
                ins = v.tensor_copy(vT[:], mk.bank(7)[:, 512:1024])
                mk.sv.inc(ins)
            tr_evac = mk.sv.n

            @block.tensor
            def _(t, tr_evac=tr_evac):
                t.wait_ge(mk.sv.h, tr_evac)
                for g in range(4):
                    t.matmul(mk.bank(6)[0:64, 0:64], kfT[:, g * 128:g * 128 + 64],
                             vT[:, g * 128:g * 128 + 64], start=(g == 0), stop=(g == 3))
                for g in range(4):
                    ins = t.matmul(mk.bank(6)[64:128, 0:64], kfT[:, g * 128 + 64:g * 128 + 128],
                                   vT[:, g * 128 + 64:g * 128 + 128], start=(g == 0), stop=(g == 3))
                mk.sp.inc(ins)
            kv_done = mk.sp.n

            @block.vector
            def _(v, dt=dt, slot=slot, kv_done=kv_done):
                v.wait_ge(mk.sp.h, kv_done)
                v.tensor_tensor(KV[:, slot * 64:(slot + 1) * 64], KV[:, slot * 64:(slot + 1) * 64],
                                mk.bank(6)[:, 0:64], ALU.add)
                ins = v.reduce_sum(red1[:, 0:1], kf16c[:, dt * CH:(dt + 1) * CH], axis=AX.X)
                mk.sv.inc(ins)
                v.wait_ge(mk.sv.h, mk.sv.n)
                ins = v.tensor_tensor(Ksum[:, slot:slot + 1], Ksum[:, slot:slot + 1],
                                      red1[:, 0:1], ALU.add)
                mk.sv.inc(ins)

    kvks = mk.dram["kvks"]
    fin_v = mk.sv.n

    @block.sync
    def _(q):
        q.wait_ge(mk.sv.h, fin_v)
        ins = q.dma_start(out=kvks[:, 0:16 * 64], in_=mk.sbuf["KV"][:])
        mk.sd.inc(ins, 16)
        ins = q.dma_start(out=kvks[:, 16 * 64:16 * 64 + 16], in_=mk.sbuf["Ksum"][:])
        mk.sd.inc(ins, 16)


def emit_final_wait_p1a(mk, block):
    @block.sync
    def _(q):
        q.wait_ge(mk.sd.h, mk.sd.n)
        q.wait_ge(mk.sg.h, mk.sg.n)


"""P1b emitter: router + 8 dense experts + shared expert; x1p = x0f + ffn0.

Chunk = 1024 tokens (4 chunks). Expert weights cast-DMA-streamed from fp32
inputs each chunk (gpsimd queue), load-while-compute (w13(e+1) during y(e),
w2(e+1) during h(e+1)). PSUM banks: 0-3 h-phase, 4-5 y-phase, 6 router/cwb.
Acc in SBUF f32, initialized from x0f chunk.
"""

CB = 1024
NCB = T // CB
NMT = 16


def declare_p1b(mk):
    mk.dram_t("w_router", [D, NE], F32, "ExternalInput")
    mk.dram_t("w1", [NE, D, HID], F32, "ExternalInput")
    mk.dram_t("w3", [NE, D, HID], F32, "ExternalInput")
    mk.dram_t("w2", [NE, HID, D], F32, "ExternalInput")
    mk.dram_t("ws1", [D, HID], F32, "ExternalInput")
    mk.dram_t("ws3", [D, HID], F32, "ExternalInput")
    mk.dram_t("ws2", [HID, D], F32, "ExternalInput")
    mk.dram_t("x0f", [NDT, 128, T], F32, "Internal")
    mk.dram_t("xn0", [NDT, 128, T], BF16, "Internal")
    mk.dram_t("x1p", [NDT, 128, T], F32, "Internal")

    declare_consts(mk)
    mk.sb("onesf", [1, 128], F32)
    mk.sb("wrt", [128, 8 * 8], BF16)
    mk.sb("xnc", [128, 8 * CB], BF16)
    mk.sb("acc", [128, 8 * CB], F32)
    mk.sb("w13", [128, 2 * 8 * HID], BF16)
    mk.sb("w2s", [128, 16 * D], BF16)
    mk.sb("h16", [128, NMT * CB], BF16)
    mk.sb("ssb", [128, 2 * 512], F32)
    mk.sb("tmp", [128, 2 * 512], F32)
    mk.sb("cwb", [128, CB], F32)
    mk.sb("lsb", [8, CB], F32)
    mk.sb("ltm", [128, 64], F32)
    mk.sb("srow", [128, 8], F32)
    mk.sb("cwf", [8, CB], F32)


def emit_p1b(mk, block):
    B = mk.sbuf
    xn0_h, x0f_h, x1p_h = mk.dram["xn0"], mk.dram["x0f"], mk.dram["x1p"]
    xnc, acc, w13, w2s, h16 = B["xnc"], B["acc"], B["w13"], B["w2s"], B["h16"]
    ssb, tmp, cwb, lsb, ltm, srow, cwf = (B["ssb"], B["tmp"], B["cwb"], B["lsb"],
                                          B["ltm"], B["srow"], B["cwf"])
    wrt, onesf = B["wrt"], B["onesf"]
    idf, id8f, sel8 = B["c_idf"], B["c_id8f"], B["c_sel8"]

    consts_d = emit_load_consts(mk, block)

    @block.gpsimd
    def _(g):
        g.memset(onesf[:], 1.0)
        ins = g.dma_start(out=wrt.rearrange("p (kt e) -> p kt e", kt=8),
                          in_=mk.dram["w_router"].rearrange("(kt p) e -> p kt e", p=128))
        mk.sg.inc(ins, 16)
    prolog_g = mk.sg.n

    def stream_w13(e, wait_pe):
        @block.gpsimd
        def _(g):
            g.wait_ge(mk.sp.h, wait_pe)
            src1 = mk.dram["w1"][e] if e < NE else mk.dram["ws1"]
            src3 = mk.dram["w3"][e] if e < NE else mk.dram["ws3"]
            v = w13.rearrange("p (i kt h) -> p i kt h", i=2, kt=8)
            ins = g.dma_start(out=v[:, 0], in_=src1.rearrange("(kt p) h -> p kt h", p=128))
            mk.sg.inc(ins, 16)
            ins = g.dma_start(out=v[:, 1], in_=src3.rearrange("(kt p) h -> p kt h", p=128))
            mk.sg.inc(ins, 16)
        return mk.sg.n

    def stream_w2(e, wait_pe):
        @block.gpsimd
        def _(g):
            g.wait_ge(mk.sp.h, wait_pe)
            src2 = mk.dram["w2"][e] if e < NE else mk.dram["ws2"]
            ins = g.dma_start(out=w2s.rearrange("p (kt d) -> p kt d", kt=16),
                              in_=src2.rearrange("(kt p) d -> p kt d", p=128))
            mk.sg.inc(ins, 16)
        return mk.sg.n

    w13_rdy = stream_w13(0, 0)
    w2_rdy = stream_w2(0, 0)

    h_pe, h_sc, h_sv = [], [], []
    y_pe, y_sv = [], []

    for c in range(NCB):
        prev_pe = mk.sp.n
        prev_v = mk.sv.n
        prev_store = mk.sd.n

        @block.sync
        def _(q):
            q.wait_ge(mk.sp.h, prev_pe)
            q.wait_ge(mk.sv.h, prev_v)
            q.wait_ge(mk.sd.h, prev_store)
            ins = q.dma_start(out=xnc.rearrange("p (dt c) -> p dt c", dt=8),
                              in_=xn0_h[:, :, c * CB:(c + 1) * CB].rearrange("dt p c -> p dt c"))
            mk.sd.inc(ins, 16)
            ins = q.dma_start(out=acc.rearrange("p (dt c) -> p dt c", dt=8),
                              in_=x0f_h[:, :, c * CB:(c + 1) * CB].rearrange("dt p c -> p dt c"))
            mk.sd.inc(ins, 16)
        loads_d = mk.sd.n

        emit_preload_tbl(mk, block, 0)
        for sub in range(2):
            bank6_free = mk.ss.n

            @block.tensor
            def _(t, sub=sub):
                if sub == 0:
                    t.wait_ge(mk.sd.h, loads_d)
                    t.wait_ge(mk.sg.h, prolog_g)
                    t.wait_ge(mk.sd.h, consts_d)
                t.wait_ge(mk.ss.h, bank6_free)
                for kt in range(NDT):
                    ins = t.matmul(mk.bank(6)[0:8, 0:512], wrt[:, kt * 8:(kt + 1) * 8],
                                   xnc[:, kt * CB + sub * 512: kt * CB + sub * 512 + 512],
                                   start=(kt == 0), stop=(kt == 7))
                mk.sp.inc(ins)
            lg_pe = mk.sp.n

            @block.scalar
            def _(s, sub=sub):
                s.wait_ge(mk.sp.h, lg_pe)
                ins = s.copy(lsb[:, sub * 512:(sub + 1) * 512], mk.bank(6)[0:8, 0:512])
                mk.ss.inc(ins)
            lg_sc = mk.ss.n

            @block.tensor
            def _(t, sub=sub):
                t.wait_ge(mk.ss.h, lg_sc)
                t.wait_ge(mk.sv.h, mk.sv.n)
                for g4 in range(4):
                    ins = t.transpose(mk.bank(6)[:, g4 * 8:(g4 + 1) * 8],
                                      lsb[:, sub * 512 + g4 * 128: sub * 512 + (g4 + 1) * 128],
                                      id8f[:])
                mk.sp.inc(ins)
            ltr_pe = mk.sp.n

            @block.vector
            def _(v, sub=sub):
                lt = ltm[:, 0:32].rearrange("p (g e) -> p g e", g=4)
                sc = ltm[:, 32:64].rearrange("p (g e) -> p g e", g=4)
                m1 = srow[:, 0:4]
                m1b = srow[:, 0:4].rearrange("p (a o) -> p a o", o=1).broadcast_to([128, 4, 8])
                v.wait_ge(mk.sp.h, ltr_pe)
                ins = v.tensor_copy(ltm[:, 0:32], mk.bank(6)[:, 0:32])
                mk.sv.inc(ins)
                v.wait_ge(mk.sv.h, mk.sv.n)
                ins = v.reduce_max(m1, lt, axis=AX.X)
                mk.sv.inc(ins)
                v.wait_ge(mk.sv.h, mk.sv.n)
                ins = v.tensor_tensor(lt, lt, m1b, ALU.subtract)
                mk.sv.inc(ins)
            sm_v = mk.sv.n

            @block.scalar
            def _(s, sub=sub):
                s.wait_ge(mk.sv.h, sm_v)
                ins = s.activation(ltm[:, 0:32], ltm[:, 0:32], AF.Exp)
                mk.ss.inc(ins)
            exp_s = mk.ss.n

            @block.vector
            def _(v, sub=sub):
                lt = ltm[:, 0:32].rearrange("p (g e) -> p g e", g=4)
                sc = ltm[:, 32:64].rearrange("p (g e) -> p g e", g=4)
                m1 = srow[:, 0:4]
                m1b = srow[:, 0:4].rearrange("p (a o) -> p a o", o=1).broadcast_to([128, 4, 8])
                m2 = srow[:, 4:8]
                m2b = srow[:, 4:8].rearrange("p (a o) -> p a o", o=1).broadcast_to([128, 4, 8])
                v.wait_ge(mk.ss.h, exp_s)
                ins = v.reduce_sum(m1, lt, axis=AX.X)
                mk.sv.inc(ins); v.wait_ge(mk.sv.h, mk.sv.n)
                ins = v.reciprocal(m1, m1)
                mk.sv.inc(ins); v.wait_ge(mk.sv.h, mk.sv.n)
                v.tensor_tensor(lt, lt, m1b, ALU.mult)
                ins = v.reduce_max(m1, lt, axis=AX.X)
                mk.sv.inc(ins); v.wait_ge(mk.sv.h, mk.sv.n)
                v.tensor_tensor(sc, lt, m1b, ALU.is_lt)
                v.tensor_tensor(sc, sc, lt, ALU.mult)
                ins = v.reduce_max(m2, sc, axis=AX.X)
                mk.sv.inc(ins); v.wait_ge(mk.sv.h, mk.sv.n)
                v.tensor_tensor(sc, lt, m2b, ALU.is_ge)
                ins = v.tensor_tensor(lt, sc, lt, ALU.mult)
                mk.sv.inc(ins)
            cw_v = mk.sv.n

            @block.tensor
            def _(t, sub=sub):
                t.wait_ge(mk.sv.h, cw_v)
                t.wait_ge(mk.ss.h, mk.ss.n)
                for g4 in range(4):
                    ins = t.transpose(mk.bank(6)[0:8, g4 * 128:(g4 + 1) * 128],
                                      ltm[:, g4 * 8:(g4 + 1) * 8], idf[:])
                mk.sp.inc(ins)
            cwt_pe = mk.sp.n

            @block.scalar
            def _(s, sub=sub):
                s.wait_ge(mk.sp.h, cwt_pe)
                ins = s.copy(cwf[:, sub * 512:(sub + 1) * 512], mk.bank(6)[0:8, 0:512])
                mk.ss.inc(ins)

        emit_preload_tbl(mk, block, TBL_SILU)
        last_h_done_pe = None
        last_y_done_pe = None
        for e in range(NE + 1):
            if e < NE:
                for sub in range(2):
                    b6_free = mk.ss.n
                    cwb_free = mk.sv.n

                    @block.tensor
                    def _(t, e=e, sub=sub):
                        t.wait_ge(mk.ss.h, b6_free)
                        ins = t.matmul(mk.bank(6)[:, 0:512], sel8[:, e * 128:(e + 1) * 128],
                                       cwf[0:8, sub * 512:(sub + 1) * 512],
                                       start=True, stop=True)
                        mk.sp.inc(ins)
                    cwb_pe = mk.sp.n

                    @block.scalar
                    def _(s, e=e, sub=sub):
                        s.wait_ge(mk.sp.h, cwb_pe)
                        s.wait_ge(mk.sv.h, cwb_free)
                        ins = s.copy(cwb[:, sub * 512:(sub + 1) * 512], mk.bank(6)[:, 0:512])
                        mk.ss.inc(ins)
            cwb_done = mk.ss.n

            for sub in range(2):
                for mt in range(NMT):
                    gi = len(h_pe)

                    @block.tensor
                    def _(t, e=e, sub=sub, mt=mt, gi=gi, w13_rdy=w13_rdy):
                        if sub == 0 and mt == 0:
                            t.wait_ge(mk.sg.h, w13_rdy)
                        if gi >= 2:
                            t.wait_ge(mk.ss.h, h_sc[gi - 2])
                            t.wait_ge(mk.sv.h, h_sv[gi - 2])
                        for kt in range(NDT):
                            t.matmul(mk.bank(gi % 2)[:, 0:512],
                                     w13[:, kt * HID + mt * 128: kt * HID + (mt + 1) * 128],
                                     xnc[:, kt * CB + sub * 512: kt * CB + sub * 512 + 512],
                                     start=(kt == 0), stop=(kt == 7))
                        for kt in range(NDT):
                            ins = t.matmul(mk.bank(2 + gi % 2)[:, 0:512],
                                           w13[:, (8 + kt) * HID + mt * 128: (8 + kt) * HID + (mt + 1) * 128],
                                           xnc[:, kt * CB + sub * 512: kt * CB + sub * 512 + 512],
                                           start=(kt == 0), stop=(kt == 7))
                        h_pe.append(mk.sp.inc(ins))

                    @block.scalar
                    def _(s, gi=gi):
                        s.wait_ge(mk.sp.h, h_pe[gi])
                        if gi >= 2:
                            s.wait_ge(mk.sv.h, h_sv[gi - 2])
                        ins = s.activation(ssb[:, (gi % 2) * 512:(gi % 2 + 1) * 512],
                                           mk.bank(gi % 2)[:, 0:512], AF.Silu)
                        h_sc.append(mk.ss.inc(ins))

                    @block.vector
                    def _(v, sub=sub, mt=mt, gi=gi):
                        v.wait_ge(mk.ss.h, h_sc[gi])
                        ins = v.tensor_tensor(h16[:, (mt * 2 + sub) * 512:(mt * 2 + sub + 1) * 512],
                                              ssb[:, (gi % 2) * 512:(gi % 2 + 1) * 512],
                                              mk.bank(2 + gi % 2)[:, 0:512], ALU.mult)
                        h_sv.append(mk.sv.inc(ins))
            h_done_pe = mk.sp.n
            h_done_sv = mk.sv.n
            if e == NE:
                last_h_done_pe = h_done_pe

            if e < NE:
                w13_rdy = stream_w13(e + 1, h_done_pe)

            for sub in range(2):
                for dt in range(NDT):
                    gj = len(y_pe)

                    @block.tensor
                    def _(t, e=e, sub=sub, dt=dt, gj=gj, w2_rdy=w2_rdy):
                        if sub == 0 and dt == 0:
                            t.wait_ge(mk.sg.h, w2_rdy)
                            t.wait_ge(mk.sv.h, h_done_sv)
                            t.wait_ge(mk.ss.h, cwb_done)
                        if gj >= 2:
                            t.wait_ge(mk.sv.h, y_sv[gj - 2])
                        for kt in range(NMT):
                            ins = t.matmul(mk.bank(4 + gj % 2)[:, 0:512],
                                           w2s[:, kt * D + dt * 128:kt * D + (dt + 1) * 128],
                                           h16[:, (kt * 2 + sub) * 512:(kt * 2 + sub + 1) * 512],
                                           start=(kt == 0), stop=(kt == 15))
                        y_pe.append(mk.sp.inc(ins))

                    @block.vector
                    def _(v, e=e, sub=sub, dt=dt, gj=gj):
                        v.wait_ge(mk.sp.h, y_pe[gj])
                        aslice = acc[:, dt * CB + sub * 512: dt * CB + (sub + 1) * 512]
                        if e < NE:
                            v.tensor_tensor(tmp[:, (gj % 2) * 512:(gj % 2 + 1) * 512],
                                            mk.bank(4 + gj % 2)[:, 0:512],
                                            cwb[:, sub * 512:(sub + 1) * 512], ALU.mult)
                            ins = v.tensor_tensor(aslice, aslice,
                                                  tmp[:, (gj % 2) * 512:(gj % 2 + 1) * 512],
                                                  ALU.add)
                        else:
                            ins = v.scalar_tensor_tensor(aslice, mk.bank(4 + gj % 2)[:, 0:512],
                                                         1.0 / 3.0, aslice, ALU.mult, ALU.add)
                        y_sv.append(mk.sv.inc(ins))
            y_done_pe = mk.sp.n
            if e == NE:
                last_y_done_pe = y_done_pe

            if e < NE:
                w2_rdy = stream_w2(e + 1, y_done_pe)

        acc_done = mk.sv.n

        @block.sync
        def _(q, c=c):
            q.wait_ge(mk.sv.h, acc_done)
            ins = q.dma_start(out=x1p_h[:, :, c * CB:(c + 1) * CB].rearrange("dt p c -> p dt c"),
                              in_=acc.rearrange("p (dt c) -> p dt c", dt=8))
            mk.sd.inc(ins, 16)

        if c < NCB - 1:
            w13_rdy = stream_w13(0, last_h_done_pe)
            w2_rdy = stream_w2(0, last_y_done_pe)


def emit_final_wait_p1b(mk, block):
    @block.sync
    def _(q):
        q.wait_ge(mk.sd.h, mk.sd.n)
        q.wait_ge(mk.sg.h, mk.sg.n)


"""P2 emitter: L0 attn output (num/den via block-diag KV), O-proj, x1 = x1p + attn0,
LN1 -> xn1 (bf16) + m1r1 rows.

Chunk = 512 tokens (8 chunks). Banks: 0 den/stats, 1-2 num rotate + stats, 3 rden
bcast, 4-5 O-proj + LN bcast reuse. kvks layout from P1a.
"""

CH = 512
NCH = T // CH


def declare_p2(mk):
    mk.dram_t("W_o", [D, D], F32, "ExternalInput")
    mk.dram_t("b_o", [D], F32, "ExternalInput")
    mk.dram_t("qf", [NDT, 128, T], BF16, "Internal")
    mk.dram_t("x1p", [NDT, 128, T], F32, "Internal")
    mk.dram_t("kvks", [128, 16 * 64 + 16], F32, "Internal")
    mk.dram_t("wo16", [D, D], BF16, "Internal")
    mk.dram_t("x1f", [NDT, 128, T], F32, "Internal")
    mk.dram_t("xn1", [NDT, 128, T], BF16, "Internal")
    mk.dram_t("m1r1", [2, T], F32, "Internal")

    declare_consts(mk)
    mk.sb("onesf", [1, 128], F32)
    mk.sb("onescol", [128, 1], F32)
    mk.sb("cst", [1, 4], F32)
    mk.sb("woT", [128, 8 * 1024], BF16)
    mk.sb("bo", [128, 8], F32)
    mk.sb("KVsb", [128, 16 * 64], F32)
    mk.sb("Kssb", [128, 16], F32)
    mk.sb("KVst", [128, 16 * 128], BF16)
    mk.sb("Ksst", [128, 16 * 16], BF16)
    mk.sb("qfc", [128, 8 * CH], BF16)
    mk.sb("x1pc", [128, 8 * CH], F32)
    mk.sb("den", [16, CH], F32)
    mk.sb("num16", [128, 2 * CH], BF16)
    mk.sb("att16", [128, 8 * CH], BF16)
    mk.sb("x1c", [128, 8 * CH], F32)
    mk.sb("xsq2", [128, 8 * CH], F32)
    mk.sb("stat", [1, 4 * CH], F32)
    mk.sb("xn16", [128, 8 * CH], BF16)
    mk.sb("mrow", [1, T], F32)
    mk.sb("rrow", [1, T], F32)


def emit_p2(mk, block):
    B = mk.sbuf
    qf_h, x1p_h, kvks_h = mk.dram["qf"], mk.dram["x1p"], mk.dram["kvks"]
    x1f_h, xn1_h, m1r1_h = mk.dram["x1f"], mk.dram["xn1"], mk.dram["m1r1"]
    wo16 = mk.dram["wo16"]
    woT, bo = B["woT"], B["bo"]
    KVsb, Kssb, KVst, Ksst = B["KVsb"], B["Kssb"], B["KVst"], B["Ksst"]
    qfc, x1pc, den, num16, att16 = B["qfc"], B["x1pc"], B["den"], B["num16"], B["att16"]
    x1c, xsq, stat, xn16 = B["x1c"], B["xsq2"], B["stat"], B["xn16"]
    mrow, rrow = B["mrow"], B["rrow"]
    onesf, onescol, cst = B["onesf"], B["onescol"], B["cst"]
    sel16 = B["c_sel16"]

    consts_d = emit_load_consts(mk, block)

    @block.gpsimd
    def _(g):
        g.memset(onesf[:], 1.0)
        g.memset(onescol[:], 1.0)
        g.memset(cst[0:1, 0:1], LN_EPS)
        g.memset(cst[0:1, 1:2], EPS0)
        g.memset(KVst[:], 0.0)
        g.memset(Ksst[:], 0.0)
        ins = g.dma_start(out=wo16[:, :], in_=mk.dram["W_o"][:, :])
        mk.sg.inc(ins, 16)
    gp_pre = mk.sg.n

    @block.sync
    def _(q):
        ins = q.dma_start(out=KVsb[:], in_=kvks_h[:, 0:16 * 64])
        mk.sd.inc(ins, 16)
        ins = q.dma_start(out=Kssb[:], in_=kvks_h[:, 16 * 64:16 * 64 + 16])
        mk.sd.inc(ins, 16)
        ins = q.dma_start(out=bo[:, 0:8],
                          in_=mk.dram["b_o"].rearrange("(dt p) -> p dt", p=128),
                          allow_slow_non_contiguous=True)
        mk.sd.inc(ins, 16)
        q.wait_ge(mk.sg.h, gp_pre)
        for kt in range(NDT):
            ins = q.dma_start_transpose(out=woT[:, kt * 1024:(kt + 1) * 1024],
                                        in_=wo16[:, kt * 128:(kt + 1) * 128])
            mk.sd.inc(ins, 16)
    pre_d = mk.sd.n

    @block.vector
    def _(v):
        v.wait_ge(mk.sd.h, pre_d)
        v.wait_ge(mk.sg.h, gp_pre)
        for b in range(BL):
            for dt in range(NDT):
                slot = dt * BL + b
                tile = (b * NDT + dt) * 128
                v.tensor_copy(KVst[0:64, tile:tile + 64], KVsb[0:64, slot * 64:(slot + 1) * 64])
                v.tensor_copy(KVst[64:128, tile + 64:tile + 128],
                              KVsb[64:128, slot * 64:(slot + 1) * 64])
                kt16 = (b * NDT + dt) * 16
                v.tensor_copy(Ksst[0:64, kt16 + 2 * dt:kt16 + 2 * dt + 1],
                              Kssb[0:64, slot:slot + 1])
                v.tensor_copy(Ksst[64:128, kt16 + 2 * dt + 1:kt16 + 2 * dt + 2],
                              Kssb[64:128, slot:slot + 1])
        ins = v.engine_nop()
        mk.sv.inc(ins)
    stats_built = mk.sv.n

    for c in range(NCH):
        b = c // 4

        prev_pe = mk.sp.n
        prev_sd = mk.sd.n

        @block.sync
        def _(q, c=c):
            q.wait_ge(mk.sp.h, prev_pe)
            q.wait_ge(mk.sd.h, prev_sd)
            ins = q.dma_start(out=qfc.rearrange("p (dt c) -> p dt c", dt=8),
                              in_=qf_h[:, :, c * CH:(c + 1) * CH].rearrange("dt p c -> p dt c"))
            mk.sd.inc(ins, 16)
            ins = q.dma_start(out=x1pc.rearrange("p (dt c) -> p dt c", dt=8),
                              in_=x1p_h[:, :, c * CH:(c + 1) * CH].rearrange("dt p c -> p dt c"))
            mk.sd.inc(ins, 16)
        loads_d = mk.sd.n

        den_read = mk.sv.n

        @block.tensor
        def _(t, c=c, b=b):
            t.wait_ge(mk.sd.h, loads_d)
            t.wait_ge(mk.sv.h, stats_built if c == 0 else den_read)
            for dt in range(NDT):
                ins = t.matmul(mk.bank(0)[0:16, 0:CH],
                               Ksst[:, (b * NDT + dt) * 16:(b * NDT + dt + 1) * 16],
                               qfc[:, dt * CH:(dt + 1) * CH],
                               start=(dt == 0), stop=(dt == 7))
            mk.sp.inc(ins)
        den_pe = mk.sp.n

        @block.vector
        def _(v):
            v.wait_ge(mk.sp.h, den_pe)
            ins = v.tensor_scalar_add(den[:], mk.bank(0)[0:16, 0:CH], EPS0)
            mk.sv.inc(ins)
            v.wait_ge(mk.sv.h, mk.sv.n)
            ins = v.reciprocal(den[:], den[:])
            mk.sv.inc(ins)
            v.wait_ge(mk.sv.h, mk.sv.n)
            ins = v.engine_nop()
            mk.sv.inc(ins)
        rden_v = mk.sv.n

        num_pe, num_sc, bc_pe, att_sv = [], [], [], []
        for dt in range(NDT):
            @block.tensor
            def _(t, dt=dt, b=b):
                if dt >= 2:
                    t.wait_ge(mk.ss.h, num_sc[dt - 2])
                if dt >= 1:
                    t.wait_ge(mk.sv.h, att_sv[dt - 1])
                else:
                    t.wait_ge(mk.sv.h, rden_v)
                ins = t.matmul(mk.bank(1 + dt % 2)[:, 0:CH],
                               KVst[:, (b * NDT + dt) * 128:(b * NDT + dt + 1) * 128],
                               qfc[:, dt * CH:(dt + 1) * CH], start=True, stop=True)
                num_pe.append(mk.sp.inc(ins))
                ins = t.matmul(mk.bank(3)[:, 0:CH], sel16[:, dt * 128:(dt + 1) * 128],
                               den[:, :], start=True, stop=True)
                bc_pe.append(mk.sp.inc(ins))

            @block.scalar
            def _(s, dt=dt):
                s.wait_ge(mk.sp.h, num_pe[dt])
                if dt >= 2:
                    s.wait_ge(mk.sv.h, att_sv[dt - 2])
                ins = s.copy(num16[:, (dt % 2) * CH:(dt % 2 + 1) * CH],
                             mk.bank(1 + dt % 2)[:, 0:CH])
                num_sc.append(mk.ss.inc(ins))

            @block.vector
            def _(v, dt=dt):
                v.wait_ge(mk.ss.h, num_sc[dt])
                v.wait_ge(mk.sp.h, bc_pe[dt])
                ins = v.tensor_tensor(att16[:, dt * CH:(dt + 1) * CH],
                                      num16[:, (dt % 2) * CH:(dt % 2 + 1) * CH],
                                      mk.bank(3)[:, 0:CH], ALU.mult)
                att_sv.append(mk.sv.inc(ins))
        att_done = mk.sv.n

        o_pe, o_sv = [], []
        for mt in range(NDT):
            @block.tensor
            def _(t, mt=mt, c=c):
                if mt == 0:
                    t.wait_ge(mk.sv.h, att_done)
                    if c == 0:
                        t.wait_ge(mk.sd.h, pre_d)
                if mt >= 2:
                    t.wait_ge(mk.sv.h, o_sv[mt - 2])
                for kt in range(NDT):
                    ins = t.matmul(mk.bank(4 + mt % 2)[:, 0:CH],
                                   woT[:, kt * 1024 + mt * 128:kt * 1024 + (mt + 1) * 128],
                                   att16[:, kt * CH:(kt + 1) * CH],
                                   start=(kt == 0), stop=(kt == 7))
                o_pe.append(mk.sp.inc(ins))

            @block.vector
            def _(v, mt=mt):
                v.wait_ge(mk.sp.h, o_pe[mt])
                ins = v.scalar_tensor_tensor(x1c[:, mt * CH:(mt + 1) * CH],
                                             mk.bank(4 + mt % 2)[:, 0:CH],
                                             bo[:, mt:mt + 1],
                                             x1pc[:, mt * CH:(mt + 1) * CH],
                                             ALU.add, ALU.add)
                o_sv.append(mk.sv.inc(ins))
        x1_done = mk.sv.n

        @block.sync
        def _(q, c=c):
            q.wait_ge(mk.sv.h, x1_done)
            ins = q.dma_start(out=x1f_h[:, :, c * CH:(c + 1) * CH].rearrange("dt p c -> p dt c"),
                              in_=x1c.rearrange("p (dt c) -> p dt c", dt=8))
            mk.sd.inc(ins, 16)

        @block.scalar
        def _(s):
            s.wait_ge(mk.sv.h, x1_done)
            for dt in range(NDT):
                ins = s.activation(xsq[:, dt * CH:(dt + 1) * CH],
                                   x1c[:, dt * CH:(dt + 1) * CH], AF.Square)
                mk.ss.inc(ins)
        sq_done = mk.ss.n

        @block.tensor
        def _(t):
            t.wait_ge(mk.ss.h, sq_done)
            for dt in range(NDT):
                t.matmul(mk.bank(0)[0:1, 0:CH], onescol[:], x1c[:, dt * CH:(dt + 1) * CH],
                         start=(dt == 0), stop=(dt == 7))
            for dt in range(NDT):
                ins = t.matmul(mk.bank(1)[0:1, 0:CH], onescol[:], xsq[:, dt * CH:(dt + 1) * CH],
                               start=(dt == 0), stop=(dt == 7))
            mk.sp.inc(ins)
        stats_pe = mk.sp.n

        @block.vector
        def _(v):
            v.wait_ge(mk.sp.h, stats_pe)
            v.tensor_scalar_mul(stat[:, 0:CH], mk.bank(0)[0:1, 0:CH], 1.0 / D)
            v.tensor_mul(stat[:, CH:2 * CH], stat[:, 0:CH], stat[:, 0:CH])
            ins = v.scalar_tensor_tensor(stat[:, 2 * CH:3 * CH], mk.bank(1)[0:1, 0:CH],
                                         1.0 / D, stat[:, CH:2 * CH], ALU.mult, ALU.subtract)
            mk.sv.inc(ins)
        var_v = mk.sv.n

        @block.scalar
        def _(s):
            s.wait_ge(mk.sv.h, var_v)
            s.activation(stat[:, 3 * CH:4 * CH], stat[:, 2 * CH:3 * CH], AF.Ln,
                         bias=cst[0:1, 0:1], scale=1.0)
            ins = s.activation(stat[:, 3 * CH:4 * CH], stat[:, 3 * CH:4 * CH], AF.Exp,
                               bias=0.0, scale=-0.5)
            mk.ss.inc(ins)
        rstd_s = mk.ss.n

        @block.vector
        def _(v, c=c):
            v.wait_ge(mk.ss.h, rstd_s)
            v.tensor_copy(mrow[:, c * CH:(c + 1) * CH], stat[:, 0:CH])
            ins = v.tensor_copy(rrow[:, c * CH:(c + 1) * CH], stat[:, 3 * CH:4 * CH])
            mk.sv.inc(ins)

        @block.tensor
        def _(t):
            t.wait_ge(mk.ss.h, rstd_s)
            t.matmul(mk.bank(2)[:, 0:CH], onesf[:], stat[0:1, 0:CH], start=True, stop=True)
            ins = t.matmul(mk.bank(3)[:, 0:CH], onesf[:], stat[0:1, 3 * CH:4 * CH],
                           start=True, stop=True)
            mk.sp.inc(ins)
        bc_pe2 = mk.sp.n

        @block.vector
        def _(v):
            v.wait_ge(mk.sp.h, bc_pe2)
            for dt in range(NDT):
                v.tensor_tensor(xsq[:, dt * CH:(dt + 1) * CH], x1c[:, dt * CH:(dt + 1) * CH],
                                mk.bank(2)[:, 0:CH], ALU.subtract)
                ins = v.tensor_tensor(xn16[:, dt * CH:(dt + 1) * CH],
                                      xsq[:, dt * CH:(dt + 1) * CH],
                                      mk.bank(3)[:, 0:CH], ALU.mult)
                mk.sv.inc(ins)
        xn_done = mk.sv.n

        @block.sync
        def _(q, c=c):
            q.wait_ge(mk.sv.h, xn_done)
            ins = q.dma_start(out=xn1_h[:, :, c * CH:(c + 1) * CH].rearrange("dt p c -> p dt c"),
                              in_=xn16.rearrange("p (dt c) -> p dt c", dt=8))
            mk.sd.inc(ins, 16)

    fin_v = mk.sv.n

    @block.sync
    def _(q):
        q.wait_ge(mk.sv.h, fin_v)
        ins = q.dma_start(out=m1r1_h[0:1, :], in_=mrow[0:1, :])
        mk.sd.inc(ins, 16)
        ins = q.dma_start(out=m1r1_h[1:2, :], in_=rrow[0:1, :])
        mk.sd.inc(ins, 16)



"""P3 emitter: L1_attn (2-level dwconv wavelet + linear attention + LN(g,b)),
x2p = x1 + attn1.

Per batch (2): convs per dt (scalar first tap + vector stt taps), lin-attn
len 512 via PE transposes + block-diag matmuls, folded upsample-conv
synthesis, final LN(gamma,beta) per 512-chunk. y streamed via HBM scratch.
"""

CH = 512


def declare_p3(mk):
    mk.dram_t("h_filter", [4, D], F32, "ExternalInput")
    mk.dram_t("g_filter", [4, D], F32, "ExternalInput")
    mk.dram_t("h_synth", [4, D], F32, "ExternalInput")
    mk.dram_t("g_synth", [4, D], F32, "ExternalInput")
    mk.dram_t("gamma", [D], F32, "ExternalInput")
    mk.dram_t("beta", [D], F32, "ExternalInput")
    mk.dram_t("xn1", [NDT, 128, T], BF16, "Internal")
    mk.dram_t("x1f", [NDT, 128, T], F32, "Internal")
    mk.dram_t("yh", [NDT, 128, T], F32, "Internal")
    mk.dram_t("x2p", [NDT, 128, T], F32, "Internal")

    declare_consts(mk)
    mk.sb("onesf", [1, 128], F32)
    mk.sb("onescol", [128, 1], F32)
    mk.sb("cst", [1, 4], F32)
    mk.sb("gbuf", [128, 16], F32)
    mk.sb("fbuf", [128, 4 * 32], F32)
    mk.sb("fcomb", [128, 8 * 10], F32)
    mk.sb("xnb", [128, 8 * NSEQ], BF16)
    mk.sb("px", [128, NSEQ + 4], F32)
    mk.sb("pl", [128, 1024 + 4], F32)
    mk.sb("ps1", [128, 512 + 2], F32)
    mk.sb("ps2", [128, 512 + 2], F32)
    mk.sb("pt1", [128, 1024 + 2], F32)
    mk.sb("pt2", [128, 1024 + 2], F32)
    mk.sb("csc1", [128, 1024], F32)
    mk.sb("h1b", [128, 8 * 1024], BF16)
    mk.sb("h2b", [128, 8 * 512], BF16)
    mk.sb("l2b", [128, 8 * 512], F32)
    mk.sb("qe", [128, 8 * 512], F32)
    mk.sb("qe16", [128, 8 * 512], BF16)
    mk.sb("v16b", [128, 8 * 512], BF16)
    mk.sb("zsb", [128, 8], F32)
    mk.sb("kfT2", [128, 512], BF16)
    mk.sb("vT2", [128, 512], BF16)
    mk.sb("kvat", [128, 8 * 64], F32)
    mk.sb("KVst2", [128, 8 * 128], BF16)
    mk.sb("Zst2", [128, 8 * 16], BF16)
    mk.sb("latb", [128, 8 * 512], F32)
    mk.sb("low1", [128, 1024], F32)
    mk.sb("ydt", [128, NSEQ], F32)
    mk.sb("den2", [16, 512], F32)
    mk.sb("num16b", [128, 2 * 512], BF16)
    mk.sb("xsq3", [128, 8 * CH], F32)
    mk.sb("stat3", [1, 4 * CH], F32)


def emit_p3(mk, block):
    B = mk.sbuf
    xn1_h, x1f_h, yh_h, x2p_h = (mk.dram["xn1"], mk.dram["x1f"], mk.dram["yh"],
                                 mk.dram["x2p"])
    onesf, onescol, cst = B["onesf"], B["onescol"], B["cst"]
    gbuf, fbuf, fcomb = B["gbuf"], B["fbuf"], B["fcomb"]
    xnb, px, pl = B["xnb"], B["px"], B["pl"]
    ps1, ps2, pt1, pt2, cs1 = B["ps1"], B["ps2"], B["pt1"], B["pt2"], B["csc1"]
    h1b, h2b, l2b = B["h1b"], B["h2b"], B["l2b"]
    qe, qe16, v16 = B["qe"], B["qe16"], B["v16b"]
    zsb, kfT2, vT2, kvat = B["zsb"], B["kfT2"], B["vT2"], B["kvat"]
    KVst2, Zst2, latb, low1, ydt = B["KVst2"], B["Zst2"], B["latb"], B["low1"], B["ydt"]
    den2, num16 = B["den2"], B["num16b"]
    xsq, stat = B["xsq3"], B["stat3"]
    ycc, x1cc, x2pc = B["qe"], B["l2b"], B["latb"]
    idb, sel16 = B["c_idb"], B["c_sel16"]

    consts_d = emit_load_consts(mk, block)

    @block.gpsimd
    def _(g):
        g.memset(onesf[:], 1.0)
        g.memset(onescol[:], 1.0)
        g.memset(cst[0:1, 0:1], LN_EPS)
        g.memset(cst[0:1, 1:2], ATTN_EPS)
        g.memset(px[:], 0.0)
        g.memset(pl[:], 0.0)
        g.memset(ps1[:], 0.0)
        g.memset(ps2[:], 0.0)
        g.memset(pt1[:], 0.0)
        g.memset(pt2[:], 0.0)
        g.memset(KVst2[:], 0.0)
        ins = g.memset(Zst2[:], 0.0)
        mk.sg.inc(ins)
    gp_pre = mk.sg.n

    @block.sync
    def _(q):
        for i, nm in enumerate(["h_filter", "g_filter", "h_synth", "g_synth"]):
            ins = q.dma_start(out=fbuf[:, i * 32:(i + 1) * 32].rearrange("p (k dt) -> p k dt", k=4),
                              in_=mk.dram[nm].rearrange("k (dt p) -> p k dt", p=128),
                              allow_slow_non_contiguous=True)
            mk.sd.inc(ins, 16)
        ins = q.dma_start(out=gbuf[:, 0:8],
                          in_=mk.dram["gamma"].rearrange("(dt p) -> p dt", p=128),
                          allow_slow_non_contiguous=True)
        mk.sd.inc(ins, 16)
        ins = q.dma_start(out=gbuf[:, 8:16],
                          in_=mk.dram["beta"].rearrange("(dt p) -> p dt", p=128),
                          allow_slow_non_contiguous=True)
        mk.sd.inc(ins, 16)
    filt_d = mk.sd.n

    @block.vector
    def _(v):
        v.wait_ge(mk.sd.h, filt_d)
        for dt in range(NDT):
            for fi, base in [(2, 0), (3, 5)]:
                f0 = fbuf[:, fi * 32 + 0 * 8 + dt: fi * 32 + 0 * 8 + dt + 1]
                f1 = fbuf[:, fi * 32 + 1 * 8 + dt: fi * 32 + 1 * 8 + dt + 1]
                f2 = fbuf[:, fi * 32 + 2 * 8 + dt: fi * 32 + 2 * 8 + dt + 1]
                f3 = fbuf[:, fi * 32 + 3 * 8 + dt: fi * 32 + 3 * 8 + dt + 1]
                o = fcomb[:, dt * 10 + base: dt * 10 + base + 5]
                v.tensor_copy(o[:, 0:1], f0)
                v.tensor_add(o[:, 1:2], f1, f2)
                v.tensor_copy(o[:, 2:3], f3)
                v.tensor_add(o[:, 3:4], f0, f1)
                v.tensor_add(o[:, 4:5], f2, f3)
        ins = v.engine_nop()
        mk.sv.inc(ins)
    fcomb_v = mk.sv.n

    def tapsum(out_ap, terms, wait_sv=None, wait_ss=None):
        in0, c0 = terms[0]

        @block.scalar
        def _(s):
            if wait_sv is not None:
                s.wait_ge(mk.sv.h, wait_sv)
            if wait_ss is not None:
                s.wait_ge(mk.ss.h, wait_ss)
            ins = s.activation(out_ap, in0, AF.Identity, bias=0.0, scale=c0)
            mk.ss.inc(ins)
        t0 = mk.ss.n

        @block.vector
        def _(v):
            v.wait_ge(mk.ss.h, t0)
            for in_ap, c_ap in terms[1:]:
                ins = v.scalar_tensor_tensor(out_ap, in_ap, c_ap, out_ap,
                                             ALU.mult, ALU.add)
            mk.sv.inc(ins)
        return mk.sv.n

    def ftap(f, dt, i):
        return fbuf[:, f * 32 + i * 8 + dt: f * 32 + i * 8 + dt + 1]

    def fc(dt, j):
        return fcomb[:, dt * 10 + j: dt * 10 + j + 1]

    for b in range(BL):
        n0 = b * NSEQ
        prev_pe = mk.sp.n
        prev_sv = mk.sv.n
        prev_ss = mk.ss.n

        @block.sync
        def _(q, b=b, n0=n0):
            q.wait_ge(mk.sp.h, prev_pe)
            q.wait_ge(mk.sv.h, prev_sv)
            q.wait_ge(mk.ss.h, prev_ss)
            ins = q.dma_start(out=xnb.rearrange("p (dt c) -> p dt c", dt=8),
                              in_=xn1_h[:, :, n0:n0 + NSEQ].rearrange("dt p c -> p dt c"))
            mk.sd.inc(ins, 16)
        xnb_d = mk.sd.n

        for dt in range(NDT):
            @block.vector
            def _(v, dt=dt):
                if dt == 0:
                    v.wait_ge(mk.sd.h, xnb_d)
                    v.wait_ge(mk.sv.h, fcomb_v)
                    v.wait_ge(mk.sg.h, gp_pre)
                    v.wait_ge(mk.sd.h, consts_d)
                v.wait_ge(mk.ss.h, mk.ss.n)
                ins = v.tensor_copy(px[:, 1:NSEQ + 1], xnb[:, dt * NSEQ:(dt + 1) * NSEQ])
                mk.sv.inc(ins)
            pxv = mk.sv.n

            tapsum(pl[:, 1:1025],
                   [(px[:, i:i + NSEQ:2], ftap(0, dt, i)) for i in range(4)],
                   wait_sv=pxv)
            h1_sv = tapsum(cs1[:, 0:1024],
                           [(px[:, i:i + NSEQ:2], ftap(1, dt, i)) for i in range(4)],
                           wait_sv=pxv)

            @block.scalar
            def _(s, dt=dt):
                s.wait_ge(mk.sv.h, h1_sv)
                ins = s.copy(h1b[:, dt * 1024:(dt + 1) * 1024], cs1[:, 0:1024])
                mk.ss.inc(ins)

            l1_sv = mk.sv.n
            l2_sv = tapsum(l2b[:, dt * 512:(dt + 1) * 512],
                           [(pl[:, i:i + 1024:2], ftap(0, dt, i)) for i in range(4)],
                           wait_sv=l1_sv)
            h2_sv = tapsum(cs1[:, 0:512],
                           [(pl[:, i:i + 1024:2], ftap(1, dt, i)) for i in range(4)],
                           wait_sv=None, wait_ss=None)

            @block.scalar
            def _(s, dt=dt):
                s.wait_ge(mk.sv.h, h2_sv)
                ins = s.copy(h2b[:, dt * 512:(dt + 1) * 512], cs1[:, 0:512])
                mk.ss.inc(ins)
        convs_sv = mk.sv.n
        convs_ss = mk.ss.n

        @block.vector
        def _(v):
            v.wait_ge(mk.ss.h, convs_ss)
            ins = v.tensor_scalar_min(qe[:], l2b[:], 0.0)
            mk.sv.inc(ins)
        qe_min = mk.sv.n

        @block.scalar
        def _(s):
            s.wait_ge(mk.sv.h, qe_min)
            ins = s.activation(qe[:], qe[:], AF.Exp)
            mk.ss.inc(ins)
        qe_exp = mk.ss.n

        @block.vector
        def _(v):
            v.wait_ge(mk.ss.h, qe_exp)
            ins = v.scalar_tensor_tensor(qe[:], l2b[:], 0.0, qe[:], ALU.max, ALU.add)
            mk.sv.inc(ins)
            v.wait_ge(mk.sv.h, mk.sv.n)
            v.tensor_copy(qe16[:], qe[:])
            ins = v.tensor_copy(v16[:], l2b[:])
            mk.sv.inc(ins)
            for dt in range(NDT):
                ins = v.reduce_sum(zsb[:, dt:dt + 1], qe[:, dt * 512:(dt + 1) * 512],
                                   axis=AX.X)
            mk.sv.inc(ins)
        qe_done = mk.sv.n

        for dt in range(NDT):
            prev_kv_v = mk.sv.n

            @block.tensor
            def _(t, dt=dt):
                t.wait_ge(mk.sv.h, prev_kv_v if dt > 0 else qe_done)
                for g4 in range(4):
                    t.transpose(mk.bank(7)[:, g4 * 128:(g4 + 1) * 128],
                                qe16[:, dt * 512 + g4 * 128:dt * 512 + (g4 + 1) * 128], idb[:])
                for g4 in range(4):
                    ins = t.transpose(mk.bank(7)[:, 512 + g4 * 128:512 + (g4 + 1) * 128],
                                      v16[:, dt * 512 + g4 * 128:dt * 512 + (g4 + 1) * 128], idb[:])
                mk.sp.inc(ins)
            tr_pe = mk.sp.n

            @block.vector
            def _(v, dt=dt):
                v.wait_ge(mk.sp.h, tr_pe)
                v.tensor_copy(kfT2[:], mk.bank(7)[:, 0:512])
                ins = v.tensor_copy(vT2[:], mk.bank(7)[:, 512:1024])
                mk.sv.inc(ins)
            tr_ev = mk.sv.n

            @block.tensor
            def _(t, dt=dt):
                t.wait_ge(mk.sv.h, tr_ev)
                for g4 in range(4):
                    t.matmul(mk.bank(6)[0:64, 0:64], kfT2[:, g4 * 128:g4 * 128 + 64],
                             vT2[:, g4 * 128:g4 * 128 + 64], start=(g4 == 0), stop=(g4 == 3))
                for g4 in range(4):
                    ins = t.matmul(mk.bank(6)[64:128, 0:64],
                                   kfT2[:, g4 * 128 + 64:g4 * 128 + 128],
                                   vT2[:, g4 * 128 + 64:g4 * 128 + 128],
                                   start=(g4 == 0), stop=(g4 == 3))
                mk.sp.inc(ins)
            kv_pe = mk.sp.n

            @block.vector
            def _(v, dt=dt):
                v.wait_ge(mk.sp.h, kv_pe)
                ins = v.tensor_copy(kvat[:, dt * 64:(dt + 1) * 64], mk.bank(6)[:, 0:64])
                mk.sv.inc(ins)
        kv_done_v = mk.sv.n

        @block.vector
        def _(v):
            v.wait_ge(mk.sv.h, kv_done_v)
            v.wait_ge(mk.sg.h, gp_pre)
            for dt in range(NDT):
                v.tensor_copy(KVst2[0:64, dt * 128:dt * 128 + 64],
                              kvat[0:64, dt * 64:(dt + 1) * 64])
                v.tensor_copy(KVst2[64:128, dt * 128 + 64:dt * 128 + 128],
                              kvat[64:128, dt * 64:(dt + 1) * 64])
                v.tensor_copy(Zst2[0:64, dt * 16 + 2 * dt:dt * 16 + 2 * dt + 1],
                              zsb[0:64, dt:dt + 1])
                v.tensor_copy(Zst2[64:128, dt * 16 + 2 * dt + 1:dt * 16 + 2 * dt + 2],
                              zsb[64:128, dt:dt + 1])
            ins = v.engine_nop()
            mk.sv.inc(ins)
        st_built = mk.sv.n

        @block.tensor
        def _(t):
            t.wait_ge(mk.sv.h, st_built)
            for dt in range(NDT):
                ins = t.matmul(mk.bank(0)[0:16, 0:512],
                               Zst2[:, dt * 16:(dt + 1) * 16],
                               qe16[:, dt * 512:(dt + 1) * 512],
                               start=(dt == 0), stop=(dt == 7))
            mk.sp.inc(ins)
        den_pe = mk.sp.n

        @block.vector
        def _(v):
            v.wait_ge(mk.sp.h, den_pe)
            ins = v.tensor_scalar_add(den2[:], mk.bank(0)[0:16, 0:512], ATTN_EPS)
            mk.sv.inc(ins); v.wait_ge(mk.sv.h, mk.sv.n)
            ins = v.reciprocal(den2[:], den2[:])
            mk.sv.inc(ins); v.wait_ge(mk.sv.h, mk.sv.n)
            ins = v.engine_nop()
            mk.sv.inc(ins)
        rden_v = mk.sv.n

        num_pe, num_sc, bc_pe, att_sv = [], [], [], []
        for dt in range(NDT):
            @block.tensor
            def _(t, dt=dt):
                if dt >= 2:
                    t.wait_ge(mk.ss.h, num_sc[dt - 2])
                if dt >= 1:
                    t.wait_ge(mk.sv.h, att_sv[dt - 1])
                else:
                    t.wait_ge(mk.sv.h, rden_v)
                ins = t.matmul(mk.bank(1 + dt % 2)[:, 0:512],
                               KVst2[:, dt * 128:(dt + 1) * 128],
                               qe16[:, dt * 512:(dt + 1) * 512], start=True, stop=True)
                num_pe.append(mk.sp.inc(ins))
                ins = t.matmul(mk.bank(3)[:, 0:512], sel16[:, dt * 128:(dt + 1) * 128],
                               den2[:, :], start=True, stop=True)
                bc_pe.append(mk.sp.inc(ins))

            @block.scalar
            def _(s, dt=dt):
                s.wait_ge(mk.sp.h, num_pe[dt])
                if dt >= 2:
                    s.wait_ge(mk.sv.h, att_sv[dt - 2])
                ins = s.copy(num16[:, (dt % 2) * 512:(dt % 2 + 1) * 512],
                             mk.bank(1 + dt % 2)[:, 0:512])
                num_sc.append(mk.ss.inc(ins))

            @block.vector
            def _(v, dt=dt):
                v.wait_ge(mk.ss.h, num_sc[dt])
                v.wait_ge(mk.sp.h, bc_pe[dt])
                ins = v.tensor_tensor(latb[:, dt * 512:(dt + 1) * 512],
                                      num16[:, (dt % 2) * 512:(dt % 2 + 1) * 512],
                                      mk.bank(3)[:, 0:512], ALU.mult)
                att_sv.append(mk.sv.inc(ins))
        lat_done = mk.sv.n

        for dt in range(NDT):
            @block.vector
            def _(v, dt=dt):
                if dt == 0:
                    v.wait_ge(mk.sv.h, lat_done)
                v.wait_ge(mk.ss.h, mk.ss.n)
                v.tensor_copy(ps1[:, 1:513], latb[:, dt * 512:(dt + 1) * 512])
                ins = v.tensor_copy(ps2[:, 1:513], h2b[:, dt * 512:(dt + 1) * 512])
                mk.sv.inc(ins)
            pads1 = mk.sv.n

            tapsum(low1[:, 0:1024:2],
                   [(ps1[:, 0:512], fc(dt, 0)), (ps1[:, 1:513], fc(dt, 1)),
                    (ps1[:, 2:514], fc(dt, 2)),
                    (ps2[:, 0:512], fc(dt, 5)), (ps2[:, 1:513], fc(dt, 6)),
                    (ps2[:, 2:514], fc(dt, 7))],
                   wait_sv=pads1)
            s1_sv = tapsum(low1[:, 1:1024:2],
                           [(ps1[:, 1:513], fc(dt, 3)), (ps1[:, 2:514], fc(dt, 4)),
                            (ps2[:, 1:513], fc(dt, 8)), (ps2[:, 2:514], fc(dt, 9))],
                           wait_sv=None)

            @block.vector
            def _(v, dt=dt):
                v.wait_ge(mk.sv.h, s1_sv)
                v.wait_ge(mk.ss.h, mk.ss.n)
                v.wait_ge(mk.sd.h, mk.sd.n)
                v.tensor_copy(pt1[:, 1:1025], low1[:, 0:1024])
                ins = v.tensor_copy(pt2[:, 1:1025], h1b[:, dt * 1024:(dt + 1) * 1024])
                mk.sv.inc(ins)
            pads2 = mk.sv.n

            tapsum(ydt[:, 0:2048:2],
                   [(pt1[:, 0:1024], fc(dt, 0)), (pt1[:, 1:1025], fc(dt, 1)),
                    (pt1[:, 2:1026], fc(dt, 2)),
                    (pt2[:, 0:1024], fc(dt, 5)), (pt2[:, 1:1025], fc(dt, 6)),
                    (pt2[:, 2:1026], fc(dt, 7))],
                   wait_sv=pads2)
            s2_sv = tapsum(ydt[:, 1:2048:2],
                           [(pt1[:, 1:1025], fc(dt, 3)), (pt1[:, 2:1026], fc(dt, 4)),
                            (pt2[:, 1:1025], fc(dt, 8)), (pt2[:, 2:1026], fc(dt, 9))],
                           wait_sv=None)

            @block.vector
            def _(v, dt=dt):
                v.wait_ge(mk.sv.h, s2_sv)
                ins = v.tensor_tensor(ydt[:], ydt[:], xnb[:, dt * NSEQ:(dt + 1) * NSEQ],
                                      ALU.add)
                mk.sv.inc(ins)
            y_v = mk.sv.n

            @block.sync
            def _(q, dt=dt, n0=n0):
                q.wait_ge(mk.sv.h, y_v)
                ins = q.dma_start(out=yh_h[dt, :, n0:n0 + NSEQ], in_=ydt[:])
                mk.sd.inc(ins, 16)
        y_stored = mk.sd.n

        for cc in range(4):
            c = b * 4 + cc
            prev2_pe = mk.sp.n
            prev2_sd = mk.sd.n

            @block.sync
            def _(q, c=c):
                q.wait_ge(mk.sv.h, mk.sv.n)
                q.wait_ge(mk.ss.h, mk.ss.n)
                q.wait_ge(mk.sd.h, prev2_sd)
                ins = q.dma_start(out=ycc.rearrange("p (dt c) -> p dt c", dt=8),
                                  in_=yh_h[:, :, c * CH:(c + 1) * CH].rearrange("dt p c -> p dt c"))
                mk.sd.inc(ins, 16)
                ins = q.dma_start(out=x1cc.rearrange("p (dt c) -> p dt c", dt=8),
                                  in_=x1f_h[:, :, c * CH:(c + 1) * CH].rearrange("dt p c -> p dt c"))
                mk.sd.inc(ins, 16)
            ld2 = mk.sd.n

            @block.scalar
            def _(s):
                s.wait_ge(mk.sd.h, ld2)
                for dt in range(NDT):
                    ins = s.activation(xsq[:, dt * CH:(dt + 1) * CH],
                                       ycc[:, dt * CH:(dt + 1) * CH], AF.Square)
                    mk.ss.inc(ins)
            sq_done = mk.ss.n

            @block.tensor
            def _(t):
                t.wait_ge(mk.ss.h, sq_done)
                t.wait_ge(mk.sv.h, mk.sv.n)
                for dt in range(NDT):
                    t.matmul(mk.bank(0)[0:1, 0:CH], onescol[:], ycc[:, dt * CH:(dt + 1) * CH],
                             start=(dt == 0), stop=(dt == 7))
                for dt in range(NDT):
                    ins = t.matmul(mk.bank(1)[0:1, 0:CH], onescol[:], xsq[:, dt * CH:(dt + 1) * CH],
                                   start=(dt == 0), stop=(dt == 7))
                mk.sp.inc(ins)
            stats_pe = mk.sp.n

            @block.vector
            def _(v):
                v.wait_ge(mk.sp.h, stats_pe)
                v.tensor_scalar_mul(stat[:, 0:CH], mk.bank(0)[0:1, 0:CH], 1.0 / D)
                v.tensor_mul(stat[:, CH:2 * CH], stat[:, 0:CH], stat[:, 0:CH])
                ins = v.scalar_tensor_tensor(stat[:, 2 * CH:3 * CH], mk.bank(1)[0:1, 0:CH],
                                             1.0 / D, stat[:, CH:2 * CH], ALU.mult, ALU.subtract)
                mk.sv.inc(ins)
            var_v = mk.sv.n

            @block.scalar
            def _(s):
                s.wait_ge(mk.sv.h, var_v)
                s.activation(stat[:, 3 * CH:4 * CH], stat[:, 2 * CH:3 * CH], AF.Ln,
                             bias=cst[0:1, 0:1], scale=1.0)
                ins = s.activation(stat[:, 3 * CH:4 * CH], stat[:, 3 * CH:4 * CH], AF.Exp,
                                   bias=0.0, scale=-0.5)
                mk.ss.inc(ins)
            rstd_s = mk.ss.n

            @block.tensor
            def _(t):
                t.wait_ge(mk.ss.h, rstd_s)
                t.matmul(mk.bank(2)[:, 0:CH], onesf[:], stat[0:1, 0:CH], start=True, stop=True)
                ins = t.matmul(mk.bank(3)[:, 0:CH], onesf[:], stat[0:1, 3 * CH:4 * CH],
                               start=True, stop=True)
                mk.sp.inc(ins)
            bc2_pe = mk.sp.n

            @block.vector
            def _(v, c=c):
                v.wait_ge(mk.sp.h, bc2_pe)
                for dt in range(NDT):
                    v.tensor_tensor(xsq[:, dt * CH:(dt + 1) * CH], ycc[:, dt * CH:(dt + 1) * CH],
                                    mk.bank(2)[:, 0:CH], ALU.subtract)
                    v.tensor_tensor(xsq[:, dt * CH:(dt + 1) * CH], xsq[:, dt * CH:(dt + 1) * CH],
                                    mk.bank(3)[:, 0:CH], ALU.mult)
                    v.tensor_scalar(xsq[:, dt * CH:(dt + 1) * CH], xsq[:, dt * CH:(dt + 1) * CH],
                                    gbuf[:, dt:dt + 1], gbuf[:, 8 + dt:9 + dt],
                                    ALU.mult, ALU.add)
                    ins = v.tensor_tensor(x2pc[:, dt * CH:(dt + 1) * CH],
                                          x1cc[:, dt * CH:(dt + 1) * CH],
                                          xsq[:, dt * CH:(dt + 1) * CH], ALU.add)
                    mk.sv.inc(ins)
            x2p_v = mk.sv.n

            @block.sync
            def _(q, c=c):
                q.wait_ge(mk.sv.h, x2p_v)
                ins = q.dma_start(out=x2p_h[:, :, c * CH:(c + 1) * CH].rearrange("dt p c -> p dt c"),
                                  in_=x2pc.rearrange("p (dt c) -> p dt c", dt=8))
                mk.sd.inc(ins, 16)


"""P4 emitter: L1_ffn fp4 quant-dequant SwiGLU + final output.

Weight prep: quantize w_gate/w_up/w_down (blocks of 16 along last dim),
bf16 -> HBM; wgT/wuT via transpose-DMA -> HBM (streamed per I-chunk);
wdT via transpose-DMA -> resident SBUF.
Main loop per 512-chunk: xn1f=(x1-m)r, fp32-transpose, x-quant, bf16-transpose
back; gate/up with xq stationary (token-major hidden); h-quant; bf16-transpose;
down-proj; + x2p; fp32-transpose out; store token-major output.
"""

CH = 512
NCH = T // CH
DEBUG = False
ICS = [512, 512, 512, 512, 512, 256]


def declare_p4(mk):
    mk.dram_t("w_gate", [I1, D], F32, "ExternalInput")
    mk.dram_t("w_up", [I1, D], F32, "ExternalInput")
    mk.dram_t("w_down", [D, I1], F32, "ExternalInput")
    mk.dram_t("x1f", [NDT, 128, T], F32, "Internal")
    mk.dram_t("x2p", [NDT, 128, T], F32, "Internal")
    mk.dram_t("m1r1", [2, T], F32, "Internal")
    mk.dram_t("wgq", [I1, D], BF16, "Internal")
    mk.dram_t("wuq", [I1, D], BF16, "Internal")
    mk.dram_t("wdq", [D, I1], BF16, "Internal")
    mk.dram_t("wgT", [NDT, 128, I1], BF16, "Internal")
    mk.dram_t("wuT", [NDT, 128, I1], BF16, "Internal")
    mk.dram_t("out", [BL, NSEQ, D], F32, "ExternalOutput")
    if DEBUG:
        mk.dram_t("dbg_xq", [128, 8 * CH], BF16, "ExternalOutput")
        mk.dram_t("dbg_hq", [128, NIT * CH], BF16, "ExternalOutput")
        mk.dram_t("dbg_xn", [128, 8 * CH], F32, "ExternalOutput")
        mk.dram_t("dbg_xtm", [128, 1024], F32, "ExternalOutput")
        mk.dram_t("dbg_xqtm", [128, 1024], BF16, "ExternalOutput")

    declare_consts(mk)
    mk.sb("onesf", [1, 128], F32)
    mk.sb("wdT", [128, NIT * 1024], BF16)
    mk.sb("wgub", [128, 2 * 2 * 8 * 512], BF16)
    mk.sb("x1fc", [128, 8 * CH], F32)
    mk.sb("xn1f", [128, 8 * CH], F32)
    mk.sb("x2pc", [128, 8 * CH], F32)
    mk.sb("mr", [1, 2 * CH], F32)
    mk.sb("xtm", [128, 1024], F32)
    mk.sb("xqtm", [128, 1024], BF16)
    mk.sb("xqfm", [128, 8 * CH], BF16)
    mk.sb("htm", [128, 2 * 512], F32)
    mk.sb("hqtm", [128, 2 * 512], BF16)
    mk.sb("hqfm", [128, NIT * CH], BF16)
    mk.sb("sgb", [128, 2 * 512], F32)
    mk.sb("xab", [128, 1024], F32)
    mk.sb("qb", [128, 1024], F32)
    mk.sb("c2b", [128, 1024], F32)
    mk.sb("sgn", [128, 1024], F32)
    mk.sb("amax", [128, 64], F32)
    mk.sb("rs", [128, 64], F32)
    mk.sb("sc2", [128, 64], F32)
    mk.sb("otm", [128, 1024], F32)
    mk.sbuf["wq_ld"] = mk.sbuf["x1fc"]
    mk.sbuf["wq_16"] = mk.sbuf["xqfm"]


def emit_quant(mk, block, in_ap, out_ap, L, wait_sv=None, wait_ss=None, wait_sd=None,
               out_wait_sp=None):
    B = mk.sbuf
    xab, qb, c2b, sgn = B["xab"], B["qb"], B["c2b"], B["sgn"]
    amax, rs, sc2 = B["amax"], B["rs"], B["sc2"]
    nb = L // 16
    scratch_free = mk.sv.n

    @block.scalar
    def _(s):
        s.wait_ge(mk.sv.h, scratch_free)
        if wait_ss is not None:
            s.wait_ge(mk.ss.h, wait_ss)
        if wait_sv is not None:
            s.wait_ge(mk.sv.h, wait_sv)
        if wait_sd is not None:
            s.wait_ge(mk.sd.h, wait_sd)
        s.activation(xab[:, 0:L], in_ap, AF.Abs)
        ins = s.sign(sgn[:, 0:L], in_ap)
        mk.ss.inc(ins)
    abs_s = mk.ss.n

    @block.vector
    def _(v):
        v.wait_ge(mk.ss.h, abs_s)
        ins = v.reduce_max(amax[:, 0:nb], xab[:, 0:L].rearrange("p (b k) -> p b k", k=16),
                           axis=AX.X)
        mk.sv.inc(ins)
        v.wait_ge(mk.sv.h, mk.sv.n)
        ins = v.tensor_scalar_max(amax[:, 0:nb], amax[:, 0:nb], 6e-12)
        ins = v.reciprocal(rs[:, 0:nb], amax[:, 0:nb])
        mk.sv.inc(ins)
        v.wait_ge(mk.sv.h, mk.sv.n)
        v.tensor_scalar_mul(rs[:, 0:nb], rs[:, 0:nb], 6.0)
        ins = v.tensor_scalar_mul(sc2[:, 0:nb], amax[:, 0:nb], 1.0 / 12.0)
        a3 = xab[:, 0:L].rearrange("p (b k) -> p b k", k=16)
        rs3 = rs[:, 0:nb].rearrange("p (b o) -> p b o", o=1).broadcast_to([128, nb, 16])
        sc3 = sc2[:, 0:nb].rearrange("p (b o) -> p b o", o=1).broadcast_to([128, nb, 16])
        v.tensor_tensor(a3, a3, rs3, ALU.mult)
        a = xab[:, 0:L]
        q = qb[:, 0:L]
        c2 = c2b[:, 0:L]
        v.tensor_single_scalar(q, a, 0.25, ALU.is_gt)
        v.scalar_tensor_tensor(q, a, 0.75, q, ALU.is_gt, ALU.add)
        v.scalar_tensor_tensor(q, a, 1.25, q, ALU.is_gt, ALU.add)
        v.scalar_tensor_tensor(q, a, 1.75, q, ALU.is_gt, ALU.add)
        v.tensor_single_scalar(c2, a, 2.5, ALU.is_gt)
        v.scalar_tensor_tensor(c2, a, 3.5, c2, ALU.is_gt, ALU.add)
        v.scalar_tensor_tensor(q, c2, 2.0, q, ALU.mult, ALU.add)
        v.tensor_single_scalar(c2, a, 5.0, ALU.is_gt)
        v.scalar_tensor_tensor(q, c2, 4.0, q, ALU.mult, ALU.add)
        v.tensor_tensor(q, q, sgn[:, 0:L], ALU.mult)
        if out_wait_sp is not None:
            v.wait_ge(mk.sp.h, out_wait_sp)
        q3 = qb[:, 0:L].rearrange("p (b k) -> p b k", k=16)
        o3 = out_ap.rearrange("p (b k) -> p b k", k=16)
        ins = v.tensor_tensor(o3, q3, sc3, ALU.mult)
        mk.sv.inc(ins)
    return mk.sv.n, abs_s


def emit_p4_weights(mk, block):
    B = mk.sbuf
    wq_ld, wq_16 = B["wq_ld"], B["wq_16"]
    wdT = B["wdT"]

    for wi, (src, dst) in enumerate([("w_gate", "wgq"), ("w_up", "wuq"),
                                     ("w_down", "wdq")]):
        nrow_t = NIT if wi < 2 else NDT
        L = D if wi < 2 else I1
        for it in range(nrow_t):
            rows = 128
            prev_q = mk.sv.n

            @block.sync
            def _(q, src=src, it=it, rows=rows, L=L):
                q.wait_ge(mk.sv.h, prev_q)
                q.wait_ge(mk.sd.h, mk.sd.n)
                ins = q.dma_start(out=wq_ld[:, 0:L],
                                  in_=mk.dram[src][it * 128:(it + 1) * 128, :])
                mk.sd.inc(ins, 16)
            ld_d = mk.sd.n

            qdone = None
            for off in range(0, L, 1024):
                LL = min(1024, L - off)
                qdone, _ = emit_quant(mk, block, wq_ld[:, off:off + LL],
                                      wq_16[:, off:off + LL], LL, wait_sd=ld_d)

            @block.sync
            def _(q, dst=dst, it=it, L=L, qdone=qdone):
                q.wait_ge(mk.sv.h, qdone)
                ins = q.dma_start(out=mk.dram[dst][it * 128:(it + 1) * 128, :],
                                  in_=wq_16[:, 0:L])
                mk.sd.inc(ins, 16)
    wq_done = mk.sd.n

    @block.sync
    def _(q):
        q.wait_ge(mk.sd.h, wq_done)
        for kt in range(NDT):
            ins = q.dma_start_transpose(out=mk.sbuf["wq_16"][:, 0:I1],
                                        in_=mk.dram["wgq"][:, kt * 128:(kt + 1) * 128])
            mk.sd.inc(ins, 16)
            q.wait_ge(mk.sd.h, mk.sd.n)
            ins = q.dma_start(out=mk.dram["wgT"][kt], in_=mk.sbuf["wq_16"][:, 0:I1])
            mk.sd.inc(ins, 16)
            q.wait_ge(mk.sd.h, mk.sd.n)
            ins = q.dma_start_transpose(out=mk.sbuf["wq_16"][:, 0:I1],
                                        in_=mk.dram["wuq"][:, kt * 128:(kt + 1) * 128])
            mk.sd.inc(ins, 16)
            q.wait_ge(mk.sd.h, mk.sd.n)
            ins = q.dma_start(out=mk.dram["wuT"][kt], in_=mk.sbuf["wq_16"][:, 0:I1])
            mk.sd.inc(ins, 16)
            q.wait_ge(mk.sd.h, mk.sd.n)
        for kt in range(NIT):
            ins = q.dma_start_transpose(out=wdT[:, kt * 1024:(kt + 1) * 1024],
                                        in_=mk.dram["wdq"][:, kt * 128:(kt + 1) * 128])
            mk.sd.inc(ins, 16)
    return mk.sd.n


def emit_p4(mk, block):
    B = mk.sbuf
    x1f_h, x2p_h, m1r1_h, out_h = (mk.dram["x1f"], mk.dram["x2p"],
                                   mk.dram["m1r1"], mk.dram["out"])
    wdT, wgub = B["wdT"], B["wgub"]
    x1fc, xn1f, x2pc, mr = B["x1fc"], B["xn1f"], B["x2pc"], B["mr"]
    xtm, xqtm, xqfm = B["xtm"], B["xqtm"], B["xqfm"]
    htm, hqtm, hqfm, sgb = B["htm"], B["hqtm"], B["hqfm"], B["sgb"]
    otm = B["otm"]
    onesf = B["onesf"]
    idf, idb = B["c_idf"], B["c_idb"]
    outc = xn1f

    @block.gpsimd
    def _(g):
        ins = g.memset(onesf[:], 1.0)
        mk.sg.inc(ins)
    ones_g = mk.sg.n

    wprep_d = emit_p4_weights(mk, block)

    def stream_wgu(ic, buf, wait_pe):
        @block.sync
        def _(q, ic=ic, buf=buf):
            q.wait_ge(mk.sp.h, wait_pe)
            L = ICS[ic]
            i0 = ic * 512
            ins = q.dma_start(
                out=wgub[:, buf * 4096:(buf + 1) * 4096].rearrange("p (kt l) -> p kt l", kt=8)[:, :, 0:L],
                in_=mk.dram["wgT"][:, :, i0:i0 + L].rearrange("kt p l -> p kt l"))
            mk.sd.inc(ins, 16)
            ins = q.dma_start(
                out=wgub[:, 8192 + buf * 4096:8192 + (buf + 1) * 4096].rearrange("p (kt l) -> p kt l", kt=8)[:, :, 0:L],
                in_=mk.dram["wuT"][:, :, i0:i0 + L].rearrange("kt p l -> p kt l"))
            mk.sd.inc(ins, 16)
        return mk.sd.n

    def wg_ap(buf, kt, L):
        return wgub[:, buf * 4096 + kt * 512: buf * 4096 + kt * 512 + L]

    def wu_ap(buf, kt, L):
        return wgub[:, 8192 + buf * 4096 + kt * 512: 8192 + buf * 4096 + kt * 512 + L]

    wgu_rdy = stream_wgu(0, 0, 0)

    for c in range(NCH):
        b = c // 4
        prev_pe = mk.sp.n
        prev_sd = mk.sd.n
        prev_sv = mk.sv.n

        @block.sync
        def _(q, c=c):
            q.wait_ge(mk.sp.h, prev_pe)
            q.wait_ge(mk.sv.h, prev_sv)
            q.wait_ge(mk.sd.h, prev_sd)
            ins = q.dma_start(out=x1fc.rearrange("p (dt c) -> p dt c", dt=8),
                              in_=x1f_h[:, :, c * CH:(c + 1) * CH].rearrange("dt p c -> p dt c"))
            mk.sd.inc(ins, 16)
            ins = q.dma_start(out=x2pc.rearrange("p (dt c) -> p dt c", dt=8),
                              in_=x2p_h[:, :, c * CH:(c + 1) * CH].rearrange("dt p c -> p dt c"))
            mk.sd.inc(ins, 16)
            ins = q.dma_start(out=mr[0:1, 0:CH], in_=m1r1_h[0:1, c * CH:(c + 1) * CH])
            mk.sd.inc(ins, 16)
            ins = q.dma_start(out=mr[0:1, CH:2 * CH], in_=m1r1_h[1:2, c * CH:(c + 1) * CH])
            mk.sd.inc(ins, 16)
        loads_d = mk.sd.n

        @block.tensor
        def _(t, c=c):
            t.wait_ge(mk.sd.h, loads_d)
            if c == 0:
                t.wait_ge(mk.sg.h, ones_g)
            t.wait_ge(mk.sv.h, mk.sv.n)
            t.wait_ge(mk.ss.h, mk.ss.n)
            t.matmul(mk.bank(4)[:, 0:CH], onesf[:], mr[0:1, 0:CH], start=True, stop=True)
            ins = t.matmul(mk.bank(5)[:, 0:CH], onesf[:], mr[0:1, CH:2 * CH],
                           start=True, stop=True)
            mk.sp.inc(ins)
        mrb_pe = mk.sp.n

        @block.vector
        def _(v):
            v.wait_ge(mk.sp.h, mrb_pe)
            for dt in range(NDT):
                v.tensor_tensor(xn1f[:, dt * CH:(dt + 1) * CH], x1fc[:, dt * CH:(dt + 1) * CH],
                                mk.bank(4)[:, 0:CH], ALU.subtract)
                ins = v.tensor_tensor(xn1f[:, dt * CH:(dt + 1) * CH],
                                      xn1f[:, dt * CH:(dt + 1) * CH],
                                      mk.bank(5)[:, 0:CH], ALU.mult)
                mk.sv.inc(ins)
        xn_v = mk.sv.n

        trx_pe_l, trx_sc_l = [], []
        for g in range(4):
            for dt in range(8):
                gi = g * 8 + dt

                @block.tensor
                def _(t, g=g, dt=dt, gi=gi):
                    if gi == 0:
                        t.wait_ge(mk.sv.h, xn_v)
                        t.wait_ge(mk.ss.h, mk.ss.n)
                    if gi >= 2:
                        t.wait_ge(mk.ss.h, trx_sc_l[gi - 2])
                    ins = t.transpose(mk.bank(4 + gi % 2)[:, 0:128],
                                      xn1f[:, dt * CH + g * 128:dt * CH + (g + 1) * 128],
                                      idf[:])
                    trx_pe_l.append(mk.sp.inc(ins))

                @block.scalar
                def _(s, g=g, dt=dt, gi=gi):
                    s.wait_ge(mk.sp.h, trx_pe_l[gi])
                    ins = s.copy(xtm[:, dt * 128:(dt + 1) * 128], mk.bank(4 + gi % 2)[:, 0:128])
                    trx_sc_l.append(mk.ss.inc(ins))
            xtm_s = mk.ss.n

            q_sv, _ = emit_quant(mk, block, xtm[:, 0:1024], xqtm[:, 0:1024], 1024,
                                 wait_ss=xtm_s)
            if DEBUG and c == 0 and g == 0:
                @block.sync
                def _(q):
                    q.wait_ge(mk.sv.h, q_sv)
                    ins = q.dma_start(out=mk.dram["dbg_xn"][:, :], in_=xn1f[:])
                    mk.sd.inc(ins, 16)
                    ins = q.dma_start(out=mk.dram["dbg_xtm"][:, :], in_=xtm[:])
                    mk.sd.inc(ins, 16)
                    ins = q.dma_start(out=mk.dram["dbg_xqtm"][:, :], in_=xqtm[:])
                    mk.sd.inc(ins, 16)

            @block.tensor
            def _(t, g=g):
                t.wait_ge(mk.sv.h, q_sv)
                t.wait_ge(mk.sv.h, mk.sv.n)
                for dt in range(8):
                    ins = t.transpose(mk.bank(7)[:, dt * 128:(dt + 1) * 128],
                                      xqtm[:, dt * 128:(dt + 1) * 128], idb[:])
                mk.sp.inc(ins)
            trq_pe = mk.sp.n

            @block.vector
            def _(v, g=g):
                v.wait_ge(mk.sp.h, trq_pe)
                for kt in range(8):
                    ins = v.tensor_copy(xqfm[:, kt * CH + g * 128:kt * CH + (g + 1) * 128],
                                        mk.bank(7)[:, kt * 128:(kt + 1) * 128])
                mk.sv.inc(ins)
        xq_done = mk.sv.n

        GRP = [(ic, g) for ic in range(6) for g in range(4)]
        gu_pe_l, silu_l, hmul_l, absm_l, qm_l, trm_l, cpm_l = {}, {}, {}, {}, {}, {}, {}

        def emit_tr(j):
            ic_, g_ = GRP[j]
            nt_ = ICS[ic_] // 128
            p_ = j % 2

            @block.tensor
            def _(t, j=j, nt_=nt_, p_=p_):
                t.wait_ge(mk.sv.h, qm_l[j])
                if j >= 2:
                    t.wait_ge(mk.sv.h, cpm_l[j - 2])
                for jj in range(nt_):
                    ins = t.transpose(mk.bank(7)[:, jj * 128:(jj + 1) * 128],
                                      hqtm[:, p_ * 512 + jj * 128:p_ * 512 + (jj + 1) * 128],
                                      idb[:])
                trm_l[j] = mk.sp.inc(ins)

            @block.vector
            def _(v, j=j, ic_=ic_, g_=g_, nt_=nt_):
                v.wait_ge(mk.sp.h, trm_l[j])
                for jj in range(nt_):
                    it = ic_ * 4 + jj
                    ins = v.tensor_copy(hqfm[:, it * CH + g_ * 128:it * CH + (g_ + 1) * 128],
                                        mk.bank(7)[:, jj * 128:(jj + 1) * 128])
                cpm_l[j] = mk.sv.inc(ins)

        for i, (ic, g) in enumerate(GRP):
            L = ICS[ic]
            p = i % 2

            @block.tensor
            def _(t, i=i, ic=ic, g=g, L=L, wgu_rdy=wgu_rdy):
                if g == 0:
                    t.wait_ge(mk.sd.h, wgu_rdy)
                    if ic == 0:
                        t.wait_ge(mk.sv.h, xq_done)
                if i >= 1:
                    t.wait_ge(mk.ss.h, silu_l[i - 1])
                    t.wait_ge(mk.sv.h, hmul_l[i - 1])
                buf = ic % 2
                for kt in range(8):
                    t.matmul(mk.bank(0)[:, 0:L], xqfm[:, kt * CH + g * 128:kt * CH + (g + 1) * 128],
                             wg_ap(buf, kt, L), start=(kt == 0), stop=(kt == 7))
                for kt in range(8):
                    ins = t.matmul(mk.bank(1)[:, 0:L], xqfm[:, kt * CH + g * 128:kt * CH + (g + 1) * 128],
                                   wu_ap(buf, kt, L), start=(kt == 0), stop=(kt == 7))
                gu_pe_l[i] = mk.sp.inc(ins)

            if i >= 2:
                emit_tr(i - 2)

            @block.scalar
            def _(s, i=i, L=L, p=p):
                s.wait_ge(mk.sp.h, gu_pe_l[i])
                if i >= 2:
                    s.wait_ge(mk.sv.h, hmul_l[i - 2])
                ins = s.activation(sgb[:, p * 512:p * 512 + L], mk.bank(0)[:, 0:L],
                                   AF.Silu)
                silu_l[i] = mk.ss.inc(ins)

            @block.vector
            def _(v, i=i, L=L, p=p):
                v.wait_ge(mk.ss.h, silu_l[i])
                if i >= 2:
                    v.wait_ge(mk.ss.h, absm_l[i - 2])
                ins = v.tensor_tensor(htm[:, p * 512:p * 512 + L],
                                      sgb[:, p * 512:p * 512 + L],
                                      mk.bank(1)[:, 0:L], ALU.mult)
                hmul_l[i] = mk.sv.inc(ins)

            qm_l[i], absm_l[i] = emit_quant(
                mk, block, htm[:, p * 512:p * 512 + L],
                hqtm[:, p * 512:p * 512 + L], L, wait_sv=hmul_l[i],
                out_wait_sp=(trm_l[i - 2] if i >= 2 else None))

            if g == 3:
                if ic < 5:
                    wgu_rdy = stream_wgu(ic + 1, (ic + 1) % 2, mk.sp.n)
                elif c < NCH - 1:
                    wgu_rdy = stream_wgu(0, 0, mk.sp.n)
        emit_tr(22)
        emit_tr(23)
        hq_done = mk.sv.n

        if DEBUG and c == 0:
            @block.sync
            def _(q):
                q.wait_ge(mk.sv.h, hq_done)
                ins = q.dma_start(out=mk.dram["dbg_xq"][:, :], in_=xqfm[:])
                mk.sd.inc(ins, 16)
                ins = q.dma_start(out=mk.dram["dbg_hq"][:, :], in_=hqfm[:])
                mk.sd.inc(ins, 16)

        d_pe, d_sv = [], []
        for mt in range(NDT):
            @block.tensor
            def _(t, mt=mt):
                if mt == 0:
                    t.wait_ge(mk.sv.h, hq_done)
                    t.wait_ge(mk.sd.h, wprep_d)
                if mt >= 2:
                    t.wait_ge(mk.sv.h, d_sv[mt - 2])
                for kt in range(NIT):
                    ins = t.matmul(mk.bank(2 + mt % 2)[:, 0:CH],
                                   wdT[:, kt * 1024 + mt * 128:kt * 1024 + (mt + 1) * 128],
                                   hqfm[:, kt * CH:(kt + 1) * CH],
                                   start=(kt == 0), stop=(kt == NIT - 1))
                d_pe.append(mk.sp.inc(ins))

            @block.vector
            def _(v, mt=mt):
                v.wait_ge(mk.sp.h, d_pe[mt])
                ins = v.tensor_tensor(outc[:, mt * CH:(mt + 1) * CH],
                                      mk.bank(2 + mt % 2)[:, 0:CH],
                                      x2pc[:, mt * CH:(mt + 1) * CH], ALU.add)
                d_sv.append(mk.sv.inc(ins))
        out_done = mk.sv.n

        tro_pe_l, tro_sc_l = [], []
        for g in range(4):
            for dt in range(8):
                gi = g * 8 + dt

                @block.tensor
                def _(t, g=g, dt=dt, gi=gi):
                    if gi == 0:
                        t.wait_ge(mk.sv.h, out_done)
                    if gi >= 2:
                        t.wait_ge(mk.ss.h, tro_sc_l[gi - 2])
                    ins = t.transpose(mk.bank(4 + gi % 2)[:, 128:256],
                                      outc[:, dt * CH + g * 128:dt * CH + (g + 1) * 128],
                                      idf[:])
                    tro_pe_l.append(mk.sp.inc(ins))

                @block.scalar
                def _(s, g=g, dt=dt, gi=gi):
                    s.wait_ge(mk.sp.h, tro_pe_l[gi])
                    if dt == 0:
                        s.wait_ge(mk.sd.h, mk.sd.n)
                    ins = s.copy(otm[:, dt * 128:(dt + 1) * 128],
                                 mk.bank(4 + gi % 2)[:, 128:256])
                    tro_sc_l.append(mk.ss.inc(ins))
            otm_s = mk.ss.n

            @block.sync
            def _(q, c=c, g=g, b=b):
                q.wait_ge(mk.ss.h, otm_s)
                n0 = (c % 4) * CH + g * 128
                ins = q.dma_start(out=out_h[b, n0:n0 + 128, :], in_=otm[:])
                mk.sd.inc(ins, 16)




_NCORES = 8


def emit_final_wait_all(mk, block):
    @block.sync
    def _(q):
        q.wait_ge(mk.sd.h, mk.sd.n)
        q.wait_ge(mk.sg.h, mk.sg.n)

    @block.gpsimd
    def _(g):
        g.wait_ge(mk.sd.h, mk.sd.n)
        g.wait_ge(mk.sg.h, mk.sg.n)


def _build_fused():
    nc = bass.Bass("TRN2", target_bir_lowering=False)
    ctx = ExitStack()
    mk = MK(nc, ctx, phase_io=None)
    with ctx:
        mk.psum_init()
        block = ctx.enter_context(nc.Block())
        declare_p1a(mk)
        emit_preload_tbl(mk, block, TBL_LN_EXP)
        emit_const_prep(mk, block)
        emit_p0_qkv(mk, block)
        emit_dec_bcast_p1a(mk, block)
        emit_p1a(mk, block)
        emit_barrier(mk, block)
        mk.free_phase_sbuf()
        declare_p1b(mk)
        emit_preload_tbl(mk, block, TBL_SILU)
        emit_p1b(mk, block)
        emit_barrier(mk, block)
        mk.free_phase_sbuf()
        declare_p2(mk)
        emit_preload_tbl(mk, block, TBL_LN_EXP)
        emit_p2(mk, block)
        emit_barrier(mk, block)
        mk.free_phase_sbuf()
        declare_p3(mk)
        emit_preload_tbl(mk, block, TBL_LN_EXP)
        emit_p3(mk, block)
        emit_barrier(mk, block)
        mk.free_phase_sbuf()
        declare_p4(mk)
        emit_preload_tbl(mk, block, TBL_SILU)
        emit_load_consts(mk, block)
        emit_p4(mk, block)
        emit_final_wait_all(mk, block)
    return nc


def kernel(**inputs):
    consts = build_consts()
    f32 = lambda a: np.ascontiguousarray(a, dtype=np.float32)
    x_full = f32(inputs["L0_attn_x"])
    nc = _build_fused()
    im = []
    for i in range(_NCORES):
        d = {
            "x0": np.ascontiguousarray(x_full[2 * i:2 * i + 2]),
            "W_q": f32(inputs["L0_attn_W_q"]), "W_k": f32(inputs["L0_attn_W_k"]),
            "W_v": f32(inputs["L0_attn_W_v"]), "W_o": f32(inputs["L0_attn_W_o"]),
            "b_q": f32(inputs["L0_attn_b_q"]), "b_k": f32(inputs["L0_attn_b_k"]),
            "b_v": f32(inputs["L0_attn_b_v"]), "b_o": f32(inputs["L0_attn_b_o"]),
            "w_router": f32(inputs["L0_ffn_w_router"]),
            "w1": f32(inputs["L0_ffn_w1"]), "w3": f32(inputs["L0_ffn_w3"]),
            "w2": f32(inputs["L0_ffn_w2"]),
            "ws1": f32(inputs["L0_ffn_ws1"]), "ws3": f32(inputs["L0_ffn_ws3"]),
            "ws2": f32(inputs["L0_ffn_ws2"]),
            "h_filter": f32(inputs["L1_attn_h_filter"]),
            "g_filter": f32(inputs["L1_attn_g_filter"]),
            "h_synth": f32(inputs["L1_attn_h_synth"]),
            "g_synth": f32(inputs["L1_attn_g_synth"]),
            "gamma": f32(inputs["L1_attn_norm_gamma"]),
            "beta": f32(inputs["L1_attn_norm_beta"]),
            "w_gate": f32(inputs["L1_ffn_w_gate"]),
            "w_up": f32(inputs["L1_ffn_w_up"]),
            "w_down": f32(inputs["L1_ffn_w_down"]),
        }
        d.update(consts)
        im.append(d)
    res = run_bass_kernel_spmd(nc, im, core_ids=list(range(_NCORES)))
    out = np.concatenate([np.asarray(r["out"]) for r in res.results], axis=0)
    return out.astype(np.float32)


# version 13
# speedup vs baseline: 1.0430x; 1.0430x over previous
import numpy as np
import ml_dtypes
from contextlib import ExitStack
import concourse.bass as bass
import concourse.mybir as mybir
from concourse.bass_utils import run_bass_kernel_spmd



"""Common infrastructure for the mk_prb_d2 megakernel builder.

Layouts (per core, B_loc=2 batches, T=4096 tokens, feature-major):
  - feature-major activation scratch in HBM: [8 dtile, 128, T] (token innermost)
  - token-major input/output: [2, 2048, 1024] (d innermost)
All emit functions append to the same per-engine streams via repeated
@block.<engine> sections; sync via Sem counters tracked in Python.
"""

F32 = mybir.dt.float32
BF16 = mybir.dt.bfloat16
AF = mybir.ActivationFunctionType
ALU = mybir.AluOpType
AX = mybir.AxisListType

D = 1024
NDT = 8
T = 4096
NSEQ = 2048
BL = 2
HID = 2048
NE = 8
I1 = 2816
NIT = 22
LN_EPS = 1e-5
EPS0 = 1e-6
ATTN_EPS = 1e-6


class Sem:
    def __init__(self, nc, ctx, name):
        self.h = ctx.enter_context(nc.semaphore(name))
        self.n = 0

    def inc(self, ins, k=1):
        ins.then_inc(self.h, k)
        self.n += k
        return self.n


class MK:

    def __init__(self, nc, ctx, phase_io=None):
        self.nc = nc
        self.ctx = ctx
        self.phase_io = phase_io
        self.dram = {}
        self.sbuf = {}
        self.guards = []
        self.phase = 0
        self.psum = []
        self.sd = Sem(nc, ctx, "sd")
        self.sg = Sem(nc, ctx, "sg")
        self.sp = Sem(nc, ctx, "sp")
        self.sv = Sem(nc, ctx, "sv")
        self.ss = Sem(nc, ctx, "ss")

    def dram_t(self, name, shape, dtype, kind):
        if name in self.dram:
            return self.dram[name]
        if self.phase_io is not None and kind == "Internal":
            role = self.phase_io.get(name)
            if role == "in":
                kind = "ExternalInput"
            elif role == "out":
                kind = "ExternalOutput"
        ap = self.nc.dram_tensor(name, shape, dtype, kind=kind).ap()
        self.dram[name] = ap
        return ap

    def sb(self, name, shape, dtype=F32, tname=None):
        if name in self.sbuf:
            return self.sbuf[name]
        real = (tname or name) + (f"_ph{self.phase}" if self.phase else "")
        guard = self.nc.sbuf_tensor(real, shape, dtype)
        h = guard.__enter__()
        self.guards.append(guard)
        self.sbuf[name] = h
        return h

    def free_phase_sbuf(self):
        for guard in reversed(self.guards):
            guard.__exit__(None, None, None)
        self.guards = []
        self.sbuf = {}
        self.phase += 1

    def psum_init(self):
        for i in range(8):
            if i == 7:
                self.psum.append(self.ctx.enter_context(
                    self.nc.psum_tensor(f"pb{i}", [128, 1024], BF16)))
            else:
                self.psum.append(self.ctx.enter_context(
                    self.nc.psum_tensor(f"pb{i}", [128, 512], F32)))

    def bank(self, i):
        return self.psum[i]


CONST_SPECS = {
    "c_idf": ([128, 128], "f32"),
    "c_idb": ([128, 128], "bf16"),
    "c_id8f": ([8, 8], "f32"),
    "c_sel8": ([8, 8 * 128], "f32"),
    "c_sel16": ([16, 8 * 128], "f32"),
}


def declare_consts(mk):
    for name, (shape, dt) in CONST_SPECS.items():
        mk.dram_t(name, shape, F32 if dt == "f32" else BF16, "ExternalInput")
        mk.sb(name, shape, F32 if dt == "f32" else BF16, tname=name + "_s")


def emit_load_consts(mk, block):
    @block.sync
    def _(q):
        for name in CONST_SPECS:
            ins = q.dma_start(out=mk.sbuf[name][:], in_=mk.dram[name][:, :])
            mk.sd.inc(ins, 16)
    return mk.sd.n


def build_consts():
    import numpy as np
    c = {}
    c["c_idf"] = np.eye(128, dtype=np.float32)
    import ml_dtypes
    c["c_idb"] = np.eye(128, dtype=np.float32).astype(ml_dtypes.bfloat16)
    c["c_id8f"] = np.eye(8, dtype=np.float32)
    sel8 = np.zeros((8, 8 * 128), dtype=np.float32)
    for e in range(8):
        sel8[e, e * 128:(e + 1) * 128] = 2.0 / 3.0
    c["c_sel8"] = sel8
    sel16 = np.zeros((16, 8 * 128), dtype=np.float32)
    for dt in range(8):
        sel16[2 * dt, dt * 128:dt * 128 + 64] = 1.0
        sel16[2 * dt + 1, dt * 128 + 64:dt * 128 + 128] = 1.0
    c["c_sel16"] = sel16
    return c


TBL_LN_EXP = 6
TBL_SILU = 18


def emit_preload_tbl(mk, block, sid):
    @block.scalar
    def _(s):
        s.add_instruction(mybir.InstLoadActFuncSet(
            name=s.bass.get_next_instruction_name(),
            act_func_set_id=sid, ins=[], outs=[]))


def emit_barrier(mk, block):
    counts = [(mk.sd, mk.sd.n), (mk.sg, mk.sg.n), (mk.sp, mk.sp.n),
              (mk.sv, mk.sv.n), (mk.ss, mk.ss.n)]

    @block.sync
    def _(q):
        for s, n in counts:
            q.wait_ge(s.h, n)

    @block.gpsimd
    def _(g):
        for s, n in counts:
            g.wait_ge(s.h, n)

    @block.tensor
    def _(t):
        for s, n in counts:
            t.wait_ge(s.h, n)

    @block.vector
    def _(v):
        for s, n in counts:
            v.wait_ge(s.h, n)

    @block.scalar
    def _(sc):
        for s, n in counts:
            sc.wait_ge(s.h, n)


def rel(a, b):
    a = np.asarray(a, np.float32); b = np.asarray(b, np.float32)
    return float(np.sqrt(((a - b) ** 2).mean() / max((b ** 2).mean(), 1e-30)))


def fm(x_tok_major):
    B, N, _ = x_tok_major.shape
    return np.ascontiguousarray(
        x_tok_major.reshape(B * N, NDT, 128).transpose(1, 2, 0))


def fm_inv(x_fm):
    return np.ascontiguousarray(
        x_fm.transpose(2, 0, 1).reshape(BL, NSEQ, D))


"""P0(qkv part) + P1a emitter: input transpose, LN0, QKV+decay, KV/Ksum.

Feature-major activations [dtile 8][128 part][tokens]. Chunk = 512 tokens.
"""

CH = 512
NCH = T // CH


def declare_p1a(mk):
    mk.dram_t("x0", [BL, NSEQ, D], F32, "ExternalInput")
    for w in ["W_q", "W_k", "W_v"]:
        mk.dram_t(w, [D, D], F32, "ExternalInput")
    for b in ["b_q", "b_k", "b_v"]:
        mk.dram_t(b, [D], F32, "ExternalInput")
    mk.dram_t("wq16", [3, D, D], BF16, "Internal")
    mk.dram_t("x0f", [NDT, 128, T], F32, "Internal")
    mk.dram_t("xn0", [NDT, 128, T], BF16, "Internal")
    mk.dram_t("qf", [NDT, 128, T], BF16, "Internal")
    mk.dram_t("kvks", [128, 16 * 64 + 16], F32, "Internal")

    mk.sb("wqkvT", [128, 3 * 8 * 1024], BF16)
    mk.sb("bqkv", [128, 3 * 8], F32)
    mk.sb("dec", [128, NSEQ], F32)
    mk.sb("decrow", [1, NSEQ], F32)
    mk.sb("onesf", [1, 128], F32)
    mk.sb("onescol", [128, 1], F32)
    mk.sb("idf", [128, 128], F32)
    mk.sb("idb", [128, 128], BF16)
    mk.sb("x0tm", [128, 4 * 1024], F32)
    mk.sb("x0fc", [128, 8 * CH], F32)
    mk.sb("xsq", [128, 8 * CH], F32)
    mk.sb("stat", [1, 4 * CH], F32)
    mk.sb("xn16", [128, 8 * CH], BF16)
    mk.sb("qrelu", [128, 2 * CH], F32)
    mk.sb("qf16c", [128, 8 * CH], BF16)
    mk.sb("kf16c", [128, 8 * CH], BF16)
    mk.sb("v16c", [128, 8 * CH], BF16)
    mk.sb("kfT", [128, 4 * 128], BF16)
    mk.sb("vT", [128, 4 * 128], BF16)
    mk.sb("red1", [128, 1], F32)
    mk.sb("cst", [1, 4], F32)
    mk.sb("KV", [128, 16 * 64], F32)
    mk.sb("Ksum", [128, 16], F32)


def emit_const_prep(mk, block):
    idf, idb, onesf = mk.sbuf["idf"], mk.sbuf["idb"], mk.sbuf["onesf"]
    onescol = mk.sbuf["onescol"]
    decrow = mk.sbuf["decrow"]

    @block.gpsimd
    def _(g):
        g.memset(idf[:], 0.0)
        g.affine_select(out=idf[:], in_=idf[:], compare_op=ALU.not_equal, fill=1.0,
                        base=0, pattern=[[-1, 128]], channel_multiplier=1)
        g.memset(idb[:], 0.0)
        g.affine_select(out=idb[:], in_=idb[:], compare_op=ALU.not_equal, fill=1.0,
                        base=0, pattern=[[-1, 128]], channel_multiplier=1)
        g.memset(onesf[:], 1.0)
        g.memset(onescol[:], 1.0)
        g.memset(mk.sbuf["cst"][0:1, 0:1], LN_EPS)
        g.memset(mk.sbuf["cst"][0:1, 1:2], 1e-6)
        ins = g.iota(decrow[:], pattern=[[1, NSEQ]], base=0, channel_multiplier=0,
                     allow_small_or_imprecise_dtypes=True)
        mk.sg.inc(ins)

    gp_done = mk.sg.n

    @block.scalar
    def _(s):
        s.wait_ge(mk.sg.h, gp_done)
        ins = s.activation(decrow[:], decrow[:], AF.Exp, bias=0.0, scale=-1.0 / NSEQ)
        mk.ss.inc(ins)


def emit_p0_qkv(mk, block):
    wq16 = mk.dram["wq16"]
    wqkvT = mk.sbuf["wqkvT"]
    bqkv = mk.sbuf["bqkv"]

    @block.gpsimd
    def _(g):
        for i, w in enumerate(["W_q", "W_k", "W_v"]):
            ins = g.dma_start(out=wq16[i, :, :], in_=mk.dram[w][:, :])
            mk.sg.inc(ins, 16)
    casts_done = mk.sg.n

    @block.sync
    def _(q):
        q.wait_ge(mk.sg.h, casts_done)
        for i in range(3):
            for kt in range(NDT):
                ins = q.dma_start_transpose(
                    out=wqkvT[:, (i * 8 + kt) * 1024:(i * 8 + kt + 1) * 1024],
                    in_=wq16[i, :, kt * 128:(kt + 1) * 128])
                mk.sd.inc(ins, 16)
        for i, b in enumerate(["b_q", "b_k", "b_v"]):
            ins = q.dma_start(out=bqkv[:, i * 8:(i + 1) * 8],
                              in_=mk.dram[b].rearrange("(dt p) -> p dt", p=128),
                              allow_slow_non_contiguous=True)
            mk.sd.inc(ins, 16)


def emit_dec_bcast_p1a(mk, block):
    dec, decrow, onesf = mk.sbuf["dec"], mk.sbuf["decrow"], mk.sbuf["onesf"]
    row_ready = mk.ss.n
    pe_hist = []
    sc_hist = []
    for i in range(4):
        @block.tensor
        def _(t, i=i):
            if i == 0:
                t.wait_ge(mk.ss.h, row_ready)
            if i >= 2:
                t.wait_ge(mk.ss.h, sc_hist[i - 2])
            ins = t.matmul(mk.bank(4 + i % 2)[:, 0:512], onesf[:],
                           decrow[:, i * 512:(i + 1) * 512], start=True, stop=True)
            pe_hist.append(mk.sp.inc(ins))

        @block.scalar
        def _(s, i=i):
            s.wait_ge(mk.sp.h, pe_hist[i])
            ins = s.copy(dec[:, i * 512:(i + 1) * 512], mk.bank(4 + i % 2)[:, 0:512])
            sc_hist.append(mk.ss.inc(ins))


def emit_p1a(mk, block):
    B = mk.sbuf
    x0 = mk.dram["x0"]
    x0f_h, xn0_h, qf_h = mk.dram["x0f"], mk.dram["xn0"], mk.dram["qf"]

    x0tm, x0fc, xsq, stat = B["x0tm"], B["x0fc"], B["xsq"], B["stat"]
    xn16, qrelu = B["xn16"], B["qrelu"]
    qf16c, kf16c, v16c = B["qf16c"], B["kf16c"], B["v16c"]
    kfT, vT, red1, KV, Ksum = B["kfT"], B["vT"], B["red1"], B["KV"], B["Ksum"]
    wqkvT, bqkv, dec, onesf, onescol, idf, idb = (
        B["wqkvT"], B["bqkv"], B["dec"], B["onesf"], B["onescol"], B["idf"], B["idb"])

    @block.vector
    def _(v):
        v.memset(KV[:], 0.0)
        ins = v.memset(Ksum[:], 0.0)
        mk.sv.inc(ins)

    prep_d = mk.sd.n
    prep_s = mk.ss.n

    for c in range(NCH):
        b = c // 4
        pos0 = (c % 4) * CH
        decs = dec[:, pos0:pos0 + CH]

        x0tm_free = mk.sp.n

        @block.sync
        def _(q, b=b, pos0=pos0, x0tm_free=x0tm_free):
            q.wait_ge(mk.sp.h, x0tm_free)
            ins = q.dma_start(out=x0tm.rearrange("p (g d) -> p g d", g=4),
                              in_=x0[b, pos0:pos0 + CH, :].rearrange("(g p) d -> p g d", p=128))
            mk.sd.inc(ins, 16)
        ld_x0 = mk.sd.n

        tr_pe = []
        tr_sc = []
        for i in range(32):
            g, dt = divmod(i, NDT)

            @block.tensor
            def _(t, i=i, g=g, dt=dt):
                if i == 0:
                    t.wait_ge(mk.sd.h, ld_x0)
                    t.wait_ge(mk.ss.h, prep_s)
                if i >= 2:
                    t.wait_ge(mk.ss.h, tr_sc[i - 2])
                ins = t.transpose(mk.bank(4 + i % 2)[:, 0:128],
                                  x0tm[:, g * 1024 + dt * 128: g * 1024 + (dt + 1) * 128],
                                  idf[:])
                tr_pe.append(mk.sp.inc(ins))

            @block.scalar
            def _(s, i=i, g=g, dt=dt):
                s.wait_ge(mk.sp.h, tr_pe[i])
                ins = s.copy(x0fc[:, dt * CH + g * 128: dt * CH + (g + 1) * 128],
                             mk.bank(4 + i % 2)[:, 0:128])
                tr_sc.append(mk.ss.inc(ins))

        @block.scalar
        def _(s):
            for dt in range(NDT):
                ins = s.activation(xsq[:, dt * CH:(dt + 1) * CH],
                                   x0fc[:, dt * CH:(dt + 1) * CH], AF.Square)
                mk.ss.inc(ins)
        sq_done = mk.ss.n

        @block.scalar
        def _(s, c=c):
            ins = s.dma_start(out=x0f_h[:, :, c * CH:(c + 1) * CH].rearrange("dt p c -> p dt c"),
                              in_=x0fc.rearrange("p (dt c) -> p dt c", dt=8))
            mk.sd.inc(ins, 16)

        @block.tensor
        def _(t):
            t.wait_ge(mk.ss.h, sq_done)
            for dt in range(NDT):
                t.matmul(mk.bank(0)[0:1, 0:CH], onescol[:], x0fc[:, dt * CH:(dt + 1) * CH],
                         start=(dt == 0), stop=(dt == 7))
            for dt in range(NDT):
                ins = t.matmul(mk.bank(1)[0:1, 0:CH], onescol[:], xsq[:, dt * CH:(dt + 1) * CH],
                               start=(dt == 0), stop=(dt == 7))
            mk.sp.inc(ins)
        stats_pe = mk.sp.n

        @block.vector
        def _(v):
            v.wait_ge(mk.sp.h, stats_pe)
            v.tensor_scalar_mul(stat[:, 0:CH], mk.bank(0)[0:1, 0:CH], 1.0 / D)
            v.tensor_mul(stat[:, CH:2 * CH], stat[:, 0:CH], stat[:, 0:CH])
            ins = v.scalar_tensor_tensor(stat[:, 2 * CH:3 * CH], mk.bank(1)[0:1, 0:CH],
                                         1.0 / D, stat[:, CH:2 * CH], ALU.mult, ALU.subtract)
            mk.sv.inc(ins)
        var_v = mk.sv.n

        @block.scalar
        def _(s):
            s.wait_ge(mk.sv.h, var_v)
            s.activation(stat[:, 3 * CH:4 * CH], stat[:, 2 * CH:3 * CH], AF.Ln,
                         bias=mk.sbuf["cst"][0:1, 0:1], scale=1.0)
            ins = s.activation(stat[:, 3 * CH:4 * CH], stat[:, 3 * CH:4 * CH], AF.Exp,
                               bias=0.0, scale=-0.5)
            mk.ss.inc(ins)
        rstd_s = mk.ss.n

        @block.tensor
        def _(t):
            t.wait_ge(mk.ss.h, rstd_s)
            t.matmul(mk.bank(2)[:, 0:CH], onesf[:], stat[0:1, 0:CH], start=True, stop=True)
            ins = t.matmul(mk.bank(3)[:, 0:CH], onesf[:], stat[0:1, 3 * CH:4 * CH],
                           start=True, stop=True)
            mk.sp.inc(ins)
        bc_pe = mk.sp.n

        @block.vector
        def _(v):
            v.wait_ge(mk.sp.h, bc_pe)
            for dt in range(NDT):
                v.tensor_tensor(xsq[:, dt * CH:(dt + 1) * CH], x0fc[:, dt * CH:(dt + 1) * CH],
                                mk.bank(2)[:, 0:CH], ALU.subtract)
                ins = v.tensor_tensor(xn16[:, dt * CH:(dt + 1) * CH],
                                      xsq[:, dt * CH:(dt + 1) * CH],
                                      mk.bank(3)[:, 0:CH], ALU.mult)
                mk.sv.inc(ins)
        xn_done = mk.sv.n

        @block.gpsimd
        def _(g, c=c):
            g.wait_ge(mk.sv.h, xn_done)
            ins = g.dma_start(out=xn0_h[:, :, c * CH:(c + 1) * CH].rearrange("dt p c -> p dt c"),
                              in_=xn16.rearrange("p (dt c) -> p dt c", dt=8))
            mk.sd.inc(ins, 16)

        outs = {"k": kf16c, "v": v16c, "q": qf16c}
        wi_of = {"q": 0, "k": 1, "v": 2}
        mm_pe, epi_sc, epi_sv = [], [], []
        idx = 0
        for wname in ["k", "v", "q"]:
            wi = wi_of[wname]
            for mt in range(NDT):
                @block.tensor
                def _(t, wi=wi, mt=mt, idx=idx):
                    if idx == 0:
                        t.wait_ge(mk.sv.h, xn_done)
                        if c == 0:
                            t.wait_ge(mk.sd.h, prep_d)
                    if idx >= 2:
                        t.wait_ge(mk.ss.h, epi_sc[idx - 2])
                    bank = mk.bank(4 + idx % 2)
                    for kt in range(NDT):
                        ins = t.matmul(
                            bank[:, 0:CH],
                            wqkvT[:, (wi * 8 + kt) * 1024 + mt * 128:
                                  (wi * 8 + kt) * 1024 + (mt + 1) * 128],
                            xn16[:, kt * CH:(kt + 1) * CH],
                            start=(kt == 0), stop=(kt == 7))
                    mm_pe.append(mk.sp.inc(ins))

                if wname in ("q", "k"):
                    @block.scalar
                    def _(s, wi=wi, mt=mt, idx=idx):
                        s.wait_ge(mk.sp.h, mm_pe[idx])
                        if idx >= 2 and len(epi_sv) >= idx - 1:
                            pass
                        if idx >= 2:
                            s.wait_ge(mk.sv.h, epi_sv[idx - 2])
                        ins = s.activation(qrelu[:, (idx % 2) * CH:(idx % 2 + 1) * CH],
                                           mk.bank(4 + idx % 2)[:, 0:CH], AF.Relu,
                                           bias=bqkv[:, wi * 8 + mt:wi * 8 + mt + 1], scale=1.0)
                        epi_sc.append(mk.ss.inc(ins))

                    @block.vector
                    def _(v, wname=wname, mt=mt, idx=idx, decs=decs):
                        v.wait_ge(mk.ss.h, epi_sc[idx])
                        ins = v.tensor_tensor(outs[wname][:, mt * CH:(mt + 1) * CH],
                                              qrelu[:, (idx % 2) * CH:(idx % 2 + 1) * CH],
                                              decs, ALU.mult)
                        epi_sv.append(mk.sv.inc(ins))
                else:
                    @block.scalar
                    def _(s, wi=wi, mt=mt, idx=idx):
                        s.wait_ge(mk.sp.h, mm_pe[idx])
                        ins = s.activation(v16c[:, mt * CH:(mt + 1) * CH],
                                           mk.bank(4 + idx % 2)[:, 0:CH],
                                           AF.Identity, bias=bqkv[:, wi * 8 + mt:wi * 8 + mt + 1],
                                           scale=1.0)
                        epi_sc.append(mk.ss.inc(ins))
                    epi_sv.append(mk.sv.n)
                idx += 1

        qkv_done_v = mk.sv.n
        qkv_done_s = mk.ss.n

        @block.gpsimd
        def _(g, c=c):
            g.wait_ge(mk.sv.h, qkv_done_v)
            ins = g.dma_start(out=qf_h[:, :, c * CH:(c + 1) * CH].rearrange("dt p c -> p dt c"),
                              in_=qf16c.rearrange("p (dt c) -> p dt c", dt=8))
            mk.sd.inc(ins, 16)

        for dt in range(NDT):
            slot = dt * BL + b
            prev_v = mk.sv.n

            @block.tensor
            def _(t, dt=dt, prev_v=prev_v):
                t.wait_ge(mk.sv.h, prev_v)
                t.wait_ge(mk.ss.h, qkv_done_s)
                for g in range(4):
                    t.transpose(mk.bank(7)[:, g * 128:(g + 1) * 128],
                                kf16c[:, dt * CH + g * 128:dt * CH + (g + 1) * 128], idb[:])
                for g in range(4):
                    ins = t.transpose(mk.bank(7)[:, 512 + g * 128:512 + (g + 1) * 128],
                                      v16c[:, dt * CH + g * 128:dt * CH + (g + 1) * 128], idb[:])
                mk.sp.inc(ins)
            tr_done = mk.sp.n

            @block.vector
            def _(v, tr_done=tr_done):
                v.wait_ge(mk.sp.h, tr_done)
                v.tensor_copy(kfT[:], mk.bank(7)[:, 0:512])
                ins = v.tensor_copy(vT[:], mk.bank(7)[:, 512:1024])
                mk.sv.inc(ins)
            tr_evac = mk.sv.n

            @block.tensor
            def _(t, tr_evac=tr_evac):
                t.wait_ge(mk.sv.h, tr_evac)
                for g in range(4):
                    t.matmul(mk.bank(6)[0:64, 0:64], kfT[:, g * 128:g * 128 + 64],
                             vT[:, g * 128:g * 128 + 64], start=(g == 0), stop=(g == 3))
                for g in range(4):
                    ins = t.matmul(mk.bank(6)[64:128, 0:64], kfT[:, g * 128 + 64:g * 128 + 128],
                                   vT[:, g * 128 + 64:g * 128 + 128], start=(g == 0), stop=(g == 3))
                mk.sp.inc(ins)
            kv_done = mk.sp.n

            @block.vector
            def _(v, dt=dt, slot=slot, kv_done=kv_done):
                v.wait_ge(mk.sp.h, kv_done)
                v.tensor_tensor(KV[:, slot * 64:(slot + 1) * 64], KV[:, slot * 64:(slot + 1) * 64],
                                mk.bank(6)[:, 0:64], ALU.add)
                ins = v.reduce_sum(red1[:, 0:1], kf16c[:, dt * CH:(dt + 1) * CH], axis=AX.X)
                mk.sv.inc(ins)
                v.wait_ge(mk.sv.h, mk.sv.n)
                ins = v.tensor_tensor(Ksum[:, slot:slot + 1], Ksum[:, slot:slot + 1],
                                      red1[:, 0:1], ALU.add)
                mk.sv.inc(ins)

    kvks = mk.dram["kvks"]
    fin_v = mk.sv.n

    @block.sync
    def _(q):
        q.wait_ge(mk.sv.h, fin_v)
        ins = q.dma_start(out=kvks[:, 0:16 * 64], in_=mk.sbuf["KV"][:])
        mk.sd.inc(ins, 16)
        ins = q.dma_start(out=kvks[:, 16 * 64:16 * 64 + 16], in_=mk.sbuf["Ksum"][:])
        mk.sd.inc(ins, 16)


def emit_final_wait_p1a(mk, block):
    @block.sync
    def _(q):
        q.wait_ge(mk.sd.h, mk.sd.n)
        q.wait_ge(mk.sg.h, mk.sg.n)


"""P1b emitter: router + 8 dense experts + shared expert; x1p = x0f + ffn0.

Chunk = 1024 tokens (4 chunks). Expert weights cast-DMA-streamed from fp32
inputs each chunk (gpsimd queue), load-while-compute (w13(e+1) during y(e),
w2(e+1) during h(e+1)). PSUM banks: 0-3 h-phase, 4-5 y-phase, 6 router/cwb.
Acc in SBUF f32, initialized from x0f chunk.
"""

CB = 1024
NCB = T // CB
NMT = 16


def declare_p1b(mk):
    mk.dram_t("w_router", [D, NE], F32, "ExternalInput")
    mk.dram_t("w1", [NE, D, HID], F32, "ExternalInput")
    mk.dram_t("w3", [NE, D, HID], F32, "ExternalInput")
    mk.dram_t("w2", [NE, HID, D], F32, "ExternalInput")
    mk.dram_t("ws1", [D, HID], F32, "ExternalInput")
    mk.dram_t("ws3", [D, HID], F32, "ExternalInput")
    mk.dram_t("ws2", [HID, D], F32, "ExternalInput")
    mk.dram_t("x0f", [NDT, 128, T], F32, "Internal")
    mk.dram_t("xn0", [NDT, 128, T], BF16, "Internal")
    mk.dram_t("x1p", [NDT, 128, T], F32, "Internal")

    declare_consts(mk)
    mk.sb("onesf", [1, 128], F32)
    mk.sb("wrt", [128, 8 * 8], BF16)
    mk.sb("xnc", [128, 8 * CB], BF16)
    mk.sb("acc", [128, 8 * CB], F32)
    mk.sb("w13", [128, 2 * 8 * HID], BF16)
    mk.sb("w2s", [128, 16 * D], BF16)
    mk.sb("h16", [128, NMT * CB], BF16)
    mk.sb("ssb", [128, 2 * 512], F32)
    mk.sb("tmp", [128, 2 * 512], F32)
    mk.sb("cwb", [128, CB], F32)
    mk.sb("lsb", [8, CB], F32)
    mk.sb("ltm", [128, 64], F32)
    mk.sb("srow", [128, 8], F32)
    mk.sb("cwf", [8, CB], F32)


def emit_p1b(mk, block):
    B = mk.sbuf
    xn0_h, x0f_h, x1p_h = mk.dram["xn0"], mk.dram["x0f"], mk.dram["x1p"]
    xnc, acc, w13, w2s, h16 = B["xnc"], B["acc"], B["w13"], B["w2s"], B["h16"]
    ssb, tmp, cwb, lsb, ltm, srow, cwf = (B["ssb"], B["tmp"], B["cwb"], B["lsb"],
                                          B["ltm"], B["srow"], B["cwf"])
    wrt, onesf = B["wrt"], B["onesf"]
    idf, id8f, sel8 = B["c_idf"], B["c_id8f"], B["c_sel8"]

    consts_d = emit_load_consts(mk, block)

    @block.gpsimd
    def _(g):
        g.memset(onesf[:], 1.0)
        ins = g.dma_start(out=wrt.rearrange("p (kt e) -> p kt e", kt=8),
                          in_=mk.dram["w_router"].rearrange("(kt p) e -> p kt e", p=128))
        mk.sg.inc(ins, 16)
    prolog_g = mk.sg.n

    def stream_w13(e, wait_pe):
        @block.gpsimd
        def _(g):
            g.wait_ge(mk.sp.h, wait_pe)
            src1 = mk.dram["w1"][e] if e < NE else mk.dram["ws1"]
            src3 = mk.dram["w3"][e] if e < NE else mk.dram["ws3"]
            v = w13.rearrange("p (i kt h) -> p i kt h", i=2, kt=8)
            ins = g.dma_start(out=v[:, 0], in_=src1.rearrange("(kt p) h -> p kt h", p=128))
            mk.sg.inc(ins, 16)
            ins = g.dma_start(out=v[:, 1], in_=src3.rearrange("(kt p) h -> p kt h", p=128))
            mk.sg.inc(ins, 16)
        return mk.sg.n

    def stream_w2(e, wait_pe):
        @block.gpsimd
        def _(g):
            g.wait_ge(mk.sp.h, wait_pe)
            src2 = mk.dram["w2"][e] if e < NE else mk.dram["ws2"]
            ins = g.dma_start(out=w2s.rearrange("p (kt d) -> p kt d", kt=16),
                              in_=src2.rearrange("(kt p) d -> p kt d", p=128))
            mk.sg.inc(ins, 16)
        return mk.sg.n

    w13_rdy = stream_w13(0, 0)
    w2_rdy = stream_w2(0, 0)

    h_pe, h_sc, h_sv = [], [], []
    y_pe, y_sv = [], []

    for c in range(NCB):
        prev_pe = mk.sp.n
        prev_v = mk.sv.n
        prev_store = mk.sd.n

        @block.sync
        def _(q):
            q.wait_ge(mk.sp.h, prev_pe)
            q.wait_ge(mk.sv.h, prev_v)
            q.wait_ge(mk.sd.h, prev_store)
            ins = q.dma_start(out=xnc.rearrange("p (dt c) -> p dt c", dt=8),
                              in_=xn0_h[:, :, c * CB:(c + 1) * CB].rearrange("dt p c -> p dt c"))
            mk.sd.inc(ins, 16)
            ins = q.dma_start(out=acc.rearrange("p (dt c) -> p dt c", dt=8),
                              in_=x0f_h[:, :, c * CB:(c + 1) * CB].rearrange("dt p c -> p dt c"))
            mk.sd.inc(ins, 16)
        loads_d = mk.sd.n

        emit_preload_tbl(mk, block, 0)
        for sub in range(2):
            bank6_free = mk.ss.n

            @block.tensor
            def _(t, sub=sub):
                if sub == 0:
                    t.wait_ge(mk.sd.h, loads_d)
                    t.wait_ge(mk.sg.h, prolog_g)
                    t.wait_ge(mk.sd.h, consts_d)
                t.wait_ge(mk.ss.h, bank6_free)
                for kt in range(NDT):
                    ins = t.matmul(mk.bank(6)[0:8, 0:512], wrt[:, kt * 8:(kt + 1) * 8],
                                   xnc[:, kt * CB + sub * 512: kt * CB + sub * 512 + 512],
                                   start=(kt == 0), stop=(kt == 7))
                mk.sp.inc(ins)
            lg_pe = mk.sp.n

            @block.scalar
            def _(s, sub=sub):
                s.wait_ge(mk.sp.h, lg_pe)
                ins = s.copy(lsb[:, sub * 512:(sub + 1) * 512], mk.bank(6)[0:8, 0:512])
                mk.ss.inc(ins)
            lg_sc = mk.ss.n

            @block.tensor
            def _(t, sub=sub):
                t.wait_ge(mk.ss.h, lg_sc)
                t.wait_ge(mk.sv.h, mk.sv.n)
                for g4 in range(4):
                    ins = t.transpose(mk.bank(6)[:, g4 * 8:(g4 + 1) * 8],
                                      lsb[:, sub * 512 + g4 * 128: sub * 512 + (g4 + 1) * 128],
                                      id8f[:])
                mk.sp.inc(ins)
            ltr_pe = mk.sp.n

            @block.vector
            def _(v, sub=sub):
                lt = ltm[:, 0:32].rearrange("p (g e) -> p g e", g=4)
                sc = ltm[:, 32:64].rearrange("p (g e) -> p g e", g=4)
                m1 = srow[:, 0:4]
                m1b = srow[:, 0:4].rearrange("p (a o) -> p a o", o=1).broadcast_to([128, 4, 8])
                v.wait_ge(mk.sp.h, ltr_pe)
                ins = v.tensor_copy(ltm[:, 0:32], mk.bank(6)[:, 0:32])
                mk.sv.inc(ins)
                v.wait_ge(mk.sv.h, mk.sv.n)
                ins = v.reduce_max(m1, lt, axis=AX.X)
                mk.sv.inc(ins)
                v.wait_ge(mk.sv.h, mk.sv.n)
                ins = v.tensor_tensor(lt, lt, m1b, ALU.subtract)
                mk.sv.inc(ins)
            sm_v = mk.sv.n

            @block.scalar
            def _(s, sub=sub):
                s.wait_ge(mk.sv.h, sm_v)
                ins = s.activation(ltm[:, 0:32], ltm[:, 0:32], AF.Exp)
                mk.ss.inc(ins)
            exp_s = mk.ss.n

            @block.vector
            def _(v, sub=sub):
                lt = ltm[:, 0:32].rearrange("p (g e) -> p g e", g=4)
                sc = ltm[:, 32:64].rearrange("p (g e) -> p g e", g=4)
                m1 = srow[:, 0:4]
                m1b = srow[:, 0:4].rearrange("p (a o) -> p a o", o=1).broadcast_to([128, 4, 8])
                m2 = srow[:, 4:8]
                m2b = srow[:, 4:8].rearrange("p (a o) -> p a o", o=1).broadcast_to([128, 4, 8])
                v.wait_ge(mk.ss.h, exp_s)
                ins = v.reduce_sum(m1, lt, axis=AX.X)
                mk.sv.inc(ins); v.wait_ge(mk.sv.h, mk.sv.n)
                ins = v.reciprocal(m1, m1)
                mk.sv.inc(ins); v.wait_ge(mk.sv.h, mk.sv.n)
                v.tensor_tensor(lt, lt, m1b, ALU.mult)
                ins = v.reduce_max(m1, lt, axis=AX.X)
                mk.sv.inc(ins); v.wait_ge(mk.sv.h, mk.sv.n)
                v.tensor_tensor(sc, lt, m1b, ALU.is_lt)
                v.tensor_tensor(sc, sc, lt, ALU.mult)
                ins = v.reduce_max(m2, sc, axis=AX.X)
                mk.sv.inc(ins); v.wait_ge(mk.sv.h, mk.sv.n)
                v.tensor_tensor(sc, lt, m2b, ALU.is_ge)
                ins = v.tensor_tensor(lt, sc, lt, ALU.mult)
                mk.sv.inc(ins)
            cw_v = mk.sv.n

            @block.tensor
            def _(t, sub=sub):
                t.wait_ge(mk.sv.h, cw_v)
                t.wait_ge(mk.ss.h, mk.ss.n)
                for g4 in range(4):
                    ins = t.transpose(mk.bank(6)[0:8, g4 * 128:(g4 + 1) * 128],
                                      ltm[:, g4 * 8:(g4 + 1) * 8], idf[:])
                mk.sp.inc(ins)
            cwt_pe = mk.sp.n

            @block.scalar
            def _(s, sub=sub):
                s.wait_ge(mk.sp.h, cwt_pe)
                ins = s.copy(cwf[:, sub * 512:(sub + 1) * 512], mk.bank(6)[0:8, 0:512])
                mk.ss.inc(ins)

        emit_preload_tbl(mk, block, TBL_SILU)
        last_h_done_pe = None
        last_y_done_pe = None
        for e in range(NE + 1):
            if e < NE:
                for sub in range(2):
                    b6_free = mk.ss.n
                    cwb_free = mk.sv.n

                    @block.tensor
                    def _(t, e=e, sub=sub):
                        t.wait_ge(mk.ss.h, b6_free)
                        ins = t.matmul(mk.bank(6)[:, 0:512], sel8[:, e * 128:(e + 1) * 128],
                                       cwf[0:8, sub * 512:(sub + 1) * 512],
                                       start=True, stop=True)
                        mk.sp.inc(ins)
                    cwb_pe = mk.sp.n

                    @block.scalar
                    def _(s, e=e, sub=sub):
                        s.wait_ge(mk.sp.h, cwb_pe)
                        s.wait_ge(mk.sv.h, cwb_free)
                        ins = s.copy(cwb[:, sub * 512:(sub + 1) * 512], mk.bank(6)[:, 0:512])
                        mk.ss.inc(ins)
            cwb_done = mk.ss.n

            for sub in range(2):
                for mt in range(NMT):
                    gi = len(h_pe)

                    @block.tensor
                    def _(t, e=e, sub=sub, mt=mt, gi=gi, w13_rdy=w13_rdy):
                        if sub == 0 and mt == 0:
                            t.wait_ge(mk.sg.h, w13_rdy)
                        if gi >= 2:
                            t.wait_ge(mk.ss.h, h_sc[gi - 2])
                            t.wait_ge(mk.sv.h, h_sv[gi - 2])
                        for kt in range(NDT):
                            t.matmul(mk.bank(gi % 2)[:, 0:512],
                                     w13[:, kt * HID + mt * 128: kt * HID + (mt + 1) * 128],
                                     xnc[:, kt * CB + sub * 512: kt * CB + sub * 512 + 512],
                                     start=(kt == 0), stop=(kt == 7))
                        for kt in range(NDT):
                            ins = t.matmul(mk.bank(2 + gi % 2)[:, 0:512],
                                           w13[:, (8 + kt) * HID + mt * 128: (8 + kt) * HID + (mt + 1) * 128],
                                           xnc[:, kt * CB + sub * 512: kt * CB + sub * 512 + 512],
                                           start=(kt == 0), stop=(kt == 7))
                        h_pe.append(mk.sp.inc(ins))

                    @block.scalar
                    def _(s, gi=gi):
                        s.wait_ge(mk.sp.h, h_pe[gi])
                        if gi >= 2:
                            s.wait_ge(mk.sv.h, h_sv[gi - 2])
                        ins = s.activation(ssb[:, (gi % 2) * 512:(gi % 2 + 1) * 512],
                                           mk.bank(gi % 2)[:, 0:512], AF.Silu)
                        h_sc.append(mk.ss.inc(ins))

                    @block.vector
                    def _(v, sub=sub, mt=mt, gi=gi):
                        v.wait_ge(mk.ss.h, h_sc[gi])
                        ins = v.tensor_tensor(h16[:, (mt * 2 + sub) * 512:(mt * 2 + sub + 1) * 512],
                                              ssb[:, (gi % 2) * 512:(gi % 2 + 1) * 512],
                                              mk.bank(2 + gi % 2)[:, 0:512], ALU.mult)
                        h_sv.append(mk.sv.inc(ins))
            h_done_pe = mk.sp.n
            h_done_sv = mk.sv.n
            if e == NE:
                last_h_done_pe = h_done_pe

            if e < NE:
                w13_rdy = stream_w13(e + 1, h_done_pe)

            for sub in range(2):
                for dt in range(NDT):
                    gj = len(y_pe)

                    @block.tensor
                    def _(t, e=e, sub=sub, dt=dt, gj=gj, w2_rdy=w2_rdy):
                        if sub == 0 and dt == 0:
                            t.wait_ge(mk.sg.h, w2_rdy)
                            t.wait_ge(mk.sv.h, h_done_sv)
                            t.wait_ge(mk.ss.h, cwb_done)
                        if gj >= 2:
                            t.wait_ge(mk.sv.h, y_sv[gj - 2])
                        for kt in range(NMT):
                            ins = t.matmul(mk.bank(4 + gj % 2)[:, 0:512],
                                           w2s[:, kt * D + dt * 128:kt * D + (dt + 1) * 128],
                                           h16[:, (kt * 2 + sub) * 512:(kt * 2 + sub + 1) * 512],
                                           start=(kt == 0), stop=(kt == 15))
                        y_pe.append(mk.sp.inc(ins))

                    @block.vector
                    def _(v, e=e, sub=sub, dt=dt, gj=gj):
                        v.wait_ge(mk.sp.h, y_pe[gj])
                        aslice = acc[:, dt * CB + sub * 512: dt * CB + (sub + 1) * 512]
                        if e < NE:
                            v.tensor_tensor(tmp[:, (gj % 2) * 512:(gj % 2 + 1) * 512],
                                            mk.bank(4 + gj % 2)[:, 0:512],
                                            cwb[:, sub * 512:(sub + 1) * 512], ALU.mult)
                            ins = v.tensor_tensor(aslice, aslice,
                                                  tmp[:, (gj % 2) * 512:(gj % 2 + 1) * 512],
                                                  ALU.add)
                        else:
                            ins = v.scalar_tensor_tensor(aslice, mk.bank(4 + gj % 2)[:, 0:512],
                                                         1.0 / 3.0, aslice, ALU.mult, ALU.add)
                        y_sv.append(mk.sv.inc(ins))
            y_done_pe = mk.sp.n
            if e == NE:
                last_y_done_pe = y_done_pe

            if e < NE:
                w2_rdy = stream_w2(e + 1, y_done_pe)

        acc_done = mk.sv.n

        @block.scalar
        def _(s, c=c):
            s.wait_ge(mk.sv.h, acc_done)
            ins = s.dma_start(out=x1p_h[:, :, c * CB:(c + 1) * CB].rearrange("dt p c -> p dt c"),
                              in_=acc.rearrange("p (dt c) -> p dt c", dt=8))
            mk.sd.inc(ins, 16)

        if c < NCB - 1:
            w13_rdy = stream_w13(0, last_h_done_pe)
            w2_rdy = stream_w2(0, last_y_done_pe)


def emit_final_wait_p1b(mk, block):
    @block.sync
    def _(q):
        q.wait_ge(mk.sd.h, mk.sd.n)
        q.wait_ge(mk.sg.h, mk.sg.n)


"""P2 emitter: L0 attn output (num/den via block-diag KV), O-proj, x1 = x1p + attn0,
LN1 -> xn1 (bf16) + m1r1 rows.

Chunk = 512 tokens (8 chunks). Banks: 0 den/stats, 1-2 num rotate + stats, 3 rden
bcast, 4-5 O-proj + LN bcast reuse. kvks layout from P1a.
"""

CH = 512
NCH = T // CH


def declare_p2(mk):
    mk.dram_t("W_o", [D, D], F32, "ExternalInput")
    mk.dram_t("b_o", [D], F32, "ExternalInput")
    mk.dram_t("qf", [NDT, 128, T], BF16, "Internal")
    mk.dram_t("x1p", [NDT, 128, T], F32, "Internal")
    mk.dram_t("kvks", [128, 16 * 64 + 16], F32, "Internal")
    mk.dram_t("wo16", [D, D], BF16, "Internal")
    mk.dram_t("x1f", [NDT, 128, T], F32, "Internal")
    mk.dram_t("xn1", [NDT, 128, T], BF16, "Internal")
    mk.dram_t("m1r1", [2, T], F32, "Internal")

    declare_consts(mk)
    mk.sb("onesf", [1, 128], F32)
    mk.sb("onescol", [128, 1], F32)
    mk.sb("cst", [1, 4], F32)
    mk.sb("woT", [128, 8 * 1024], BF16)
    mk.sb("bo", [128, 8], F32)
    mk.sb("KVsb", [128, 16 * 64], F32)
    mk.sb("Kssb", [128, 16], F32)
    mk.sb("KVst", [128, 16 * 128], BF16)
    mk.sb("Ksst", [128, 16 * 16], BF16)
    mk.sb("qfc", [128, 8 * CH], BF16)
    mk.sb("x1pc", [128, 8 * CH], F32)
    mk.sb("den", [16, CH], F32)
    mk.sb("num16", [128, 2 * CH], BF16)
    mk.sb("att16", [128, 8 * CH], BF16)
    mk.sb("x1c", [128, 8 * CH], F32)
    mk.sb("xsq2", [128, 8 * CH], F32)
    mk.sb("stat", [1, 4 * CH], F32)
    mk.sb("xn16", [128, 8 * CH], BF16)
    mk.sb("mrow", [1, T], F32)
    mk.sb("rrow", [1, T], F32)


def emit_p2(mk, block):
    B = mk.sbuf
    qf_h, x1p_h, kvks_h = mk.dram["qf"], mk.dram["x1p"], mk.dram["kvks"]
    x1f_h, xn1_h, m1r1_h = mk.dram["x1f"], mk.dram["xn1"], mk.dram["m1r1"]
    wo16 = mk.dram["wo16"]
    woT, bo = B["woT"], B["bo"]
    KVsb, Kssb, KVst, Ksst = B["KVsb"], B["Kssb"], B["KVst"], B["Ksst"]
    qfc, x1pc, den, num16, att16 = B["qfc"], B["x1pc"], B["den"], B["num16"], B["att16"]
    x1c, xsq, stat, xn16 = B["x1c"], B["xsq2"], B["stat"], B["xn16"]
    mrow, rrow = B["mrow"], B["rrow"]
    onesf, onescol, cst = B["onesf"], B["onescol"], B["cst"]
    sel16 = B["c_sel16"]

    consts_d = emit_load_consts(mk, block)

    @block.gpsimd
    def _(g):
        g.memset(onesf[:], 1.0)
        g.memset(onescol[:], 1.0)
        g.memset(cst[0:1, 0:1], LN_EPS)
        g.memset(cst[0:1, 1:2], EPS0)
        g.memset(KVst[:], 0.0)
        g.memset(Ksst[:], 0.0)
        ins = g.dma_start(out=wo16[:, :], in_=mk.dram["W_o"][:, :])
        mk.sg.inc(ins, 16)
    gp_pre = mk.sg.n

    @block.sync
    def _(q):
        ins = q.dma_start(out=KVsb[:], in_=kvks_h[:, 0:16 * 64])
        mk.sd.inc(ins, 16)
        ins = q.dma_start(out=Kssb[:], in_=kvks_h[:, 16 * 64:16 * 64 + 16])
        mk.sd.inc(ins, 16)
        ins = q.dma_start(out=bo[:, 0:8],
                          in_=mk.dram["b_o"].rearrange("(dt p) -> p dt", p=128),
                          allow_slow_non_contiguous=True)
        mk.sd.inc(ins, 16)
        q.wait_ge(mk.sg.h, gp_pre)
        for kt in range(NDT):
            ins = q.dma_start_transpose(out=woT[:, kt * 1024:(kt + 1) * 1024],
                                        in_=wo16[:, kt * 128:(kt + 1) * 128])
            mk.sd.inc(ins, 16)
    pre_d = mk.sd.n

    @block.vector
    def _(v):
        v.wait_ge(mk.sd.h, pre_d)
        v.wait_ge(mk.sg.h, gp_pre)
        for b in range(BL):
            for dt in range(NDT):
                slot = dt * BL + b
                tile = (b * NDT + dt) * 128
                v.tensor_copy(KVst[0:64, tile:tile + 64], KVsb[0:64, slot * 64:(slot + 1) * 64])
                v.tensor_copy(KVst[64:128, tile + 64:tile + 128],
                              KVsb[64:128, slot * 64:(slot + 1) * 64])
                kt16 = (b * NDT + dt) * 16
                v.tensor_copy(Ksst[0:64, kt16 + 2 * dt:kt16 + 2 * dt + 1],
                              Kssb[0:64, slot:slot + 1])
                v.tensor_copy(Ksst[64:128, kt16 + 2 * dt + 1:kt16 + 2 * dt + 2],
                              Kssb[64:128, slot:slot + 1])
        ins = v.engine_nop()
        mk.sv.inc(ins)
    stats_built = mk.sv.n

    for c in range(NCH):
        b = c // 4

        prev_pe = mk.sp.n
        prev_sd = mk.sd.n

        @block.sync
        def _(q, c=c):
            q.wait_ge(mk.sp.h, prev_pe)
            q.wait_ge(mk.sd.h, prev_sd)
            ins = q.dma_start(out=qfc.rearrange("p (dt c) -> p dt c", dt=8),
                              in_=qf_h[:, :, c * CH:(c + 1) * CH].rearrange("dt p c -> p dt c"))
            mk.sd.inc(ins, 16)

        @block.scalar
        def _(s, c=c):
            s.wait_ge(mk.sp.h, prev_pe)
            s.wait_ge(mk.sd.h, prev_sd)
            ins = s.dma_start(out=x1pc.rearrange("p (dt c) -> p dt c", dt=8),
                              in_=x1p_h[:, :, c * CH:(c + 1) * CH].rearrange("dt p c -> p dt c"))
            mk.sd.inc(ins, 16)
        loads_d = mk.sd.n

        den_read = mk.sv.n

        @block.tensor
        def _(t, c=c, b=b):
            t.wait_ge(mk.sd.h, loads_d)
            t.wait_ge(mk.sv.h, stats_built if c == 0 else den_read)
            for dt in range(NDT):
                ins = t.matmul(mk.bank(0)[0:16, 0:CH],
                               Ksst[:, (b * NDT + dt) * 16:(b * NDT + dt + 1) * 16],
                               qfc[:, dt * CH:(dt + 1) * CH],
                               start=(dt == 0), stop=(dt == 7))
            mk.sp.inc(ins)
        den_pe = mk.sp.n

        @block.vector
        def _(v):
            v.wait_ge(mk.sp.h, den_pe)
            ins = v.tensor_scalar_add(den[:], mk.bank(0)[0:16, 0:CH], EPS0)
            mk.sv.inc(ins)
            v.wait_ge(mk.sv.h, mk.sv.n)
            ins = v.reciprocal(den[:], den[:])
            mk.sv.inc(ins)
            v.wait_ge(mk.sv.h, mk.sv.n)
            ins = v.engine_nop()
            mk.sv.inc(ins)
        rden_v = mk.sv.n

        num_pe, num_sc, bc_pe, att_sv = [], [], [], []
        for dt in range(NDT):
            @block.tensor
            def _(t, dt=dt, b=b):
                if dt >= 2:
                    t.wait_ge(mk.ss.h, num_sc[dt - 2])
                if dt >= 1:
                    t.wait_ge(mk.sv.h, att_sv[dt - 1])
                else:
                    t.wait_ge(mk.sv.h, rden_v)
                ins = t.matmul(mk.bank(1 + dt % 2)[:, 0:CH],
                               KVst[:, (b * NDT + dt) * 128:(b * NDT + dt + 1) * 128],
                               qfc[:, dt * CH:(dt + 1) * CH], start=True, stop=True)
                num_pe.append(mk.sp.inc(ins))
                ins = t.matmul(mk.bank(3)[:, 0:CH], sel16[:, dt * 128:(dt + 1) * 128],
                               den[:, :], start=True, stop=True)
                bc_pe.append(mk.sp.inc(ins))

            @block.scalar
            def _(s, dt=dt):
                s.wait_ge(mk.sp.h, num_pe[dt])
                if dt >= 2:
                    s.wait_ge(mk.sv.h, att_sv[dt - 2])
                ins = s.copy(num16[:, (dt % 2) * CH:(dt % 2 + 1) * CH],
                             mk.bank(1 + dt % 2)[:, 0:CH])
                num_sc.append(mk.ss.inc(ins))

            @block.vector
            def _(v, dt=dt):
                v.wait_ge(mk.ss.h, num_sc[dt])
                v.wait_ge(mk.sp.h, bc_pe[dt])
                ins = v.tensor_tensor(att16[:, dt * CH:(dt + 1) * CH],
                                      num16[:, (dt % 2) * CH:(dt % 2 + 1) * CH],
                                      mk.bank(3)[:, 0:CH], ALU.mult)
                att_sv.append(mk.sv.inc(ins))
        att_done = mk.sv.n

        o_pe, o_sv = [], []
        for mt in range(NDT):
            @block.tensor
            def _(t, mt=mt, c=c):
                if mt == 0:
                    t.wait_ge(mk.sv.h, att_done)
                    if c == 0:
                        t.wait_ge(mk.sd.h, pre_d)
                if mt >= 2:
                    t.wait_ge(mk.sv.h, o_sv[mt - 2])
                for kt in range(NDT):
                    ins = t.matmul(mk.bank(4 + mt % 2)[:, 0:CH],
                                   woT[:, kt * 1024 + mt * 128:kt * 1024 + (mt + 1) * 128],
                                   att16[:, kt * CH:(kt + 1) * CH],
                                   start=(kt == 0), stop=(kt == 7))
                o_pe.append(mk.sp.inc(ins))

            @block.vector
            def _(v, mt=mt):
                v.wait_ge(mk.sp.h, o_pe[mt])
                ins = v.scalar_tensor_tensor(x1c[:, mt * CH:(mt + 1) * CH],
                                             mk.bank(4 + mt % 2)[:, 0:CH],
                                             bo[:, mt:mt + 1],
                                             x1pc[:, mt * CH:(mt + 1) * CH],
                                             ALU.add, ALU.add)
                o_sv.append(mk.sv.inc(ins))
        x1_done = mk.sv.n

        @block.gpsimd
        def _(g, c=c):
            g.wait_ge(mk.sv.h, x1_done)
            ins = g.dma_start(out=x1f_h[:, :, c * CH:(c + 1) * CH].rearrange("dt p c -> p dt c"),
                              in_=x1c.rearrange("p (dt c) -> p dt c", dt=8))
            mk.sd.inc(ins, 16)

        @block.scalar
        def _(s):
            s.wait_ge(mk.sv.h, x1_done)
            for dt in range(NDT):
                ins = s.activation(xsq[:, dt * CH:(dt + 1) * CH],
                                   x1c[:, dt * CH:(dt + 1) * CH], AF.Square)
                mk.ss.inc(ins)
        sq_done = mk.ss.n

        @block.tensor
        def _(t):
            t.wait_ge(mk.ss.h, sq_done)
            for dt in range(NDT):
                t.matmul(mk.bank(0)[0:1, 0:CH], onescol[:], x1c[:, dt * CH:(dt + 1) * CH],
                         start=(dt == 0), stop=(dt == 7))
            for dt in range(NDT):
                ins = t.matmul(mk.bank(1)[0:1, 0:CH], onescol[:], xsq[:, dt * CH:(dt + 1) * CH],
                               start=(dt == 0), stop=(dt == 7))
            mk.sp.inc(ins)
        stats_pe = mk.sp.n

        @block.vector
        def _(v):
            v.wait_ge(mk.sp.h, stats_pe)
            v.tensor_scalar_mul(stat[:, 0:CH], mk.bank(0)[0:1, 0:CH], 1.0 / D)
            v.tensor_mul(stat[:, CH:2 * CH], stat[:, 0:CH], stat[:, 0:CH])
            ins = v.scalar_tensor_tensor(stat[:, 2 * CH:3 * CH], mk.bank(1)[0:1, 0:CH],
                                         1.0 / D, stat[:, CH:2 * CH], ALU.mult, ALU.subtract)
            mk.sv.inc(ins)
        var_v = mk.sv.n

        @block.scalar
        def _(s):
            s.wait_ge(mk.sv.h, var_v)
            s.activation(stat[:, 3 * CH:4 * CH], stat[:, 2 * CH:3 * CH], AF.Ln,
                         bias=cst[0:1, 0:1], scale=1.0)
            ins = s.activation(stat[:, 3 * CH:4 * CH], stat[:, 3 * CH:4 * CH], AF.Exp,
                               bias=0.0, scale=-0.5)
            mk.ss.inc(ins)
        rstd_s = mk.ss.n

        @block.vector
        def _(v, c=c):
            v.wait_ge(mk.ss.h, rstd_s)
            v.tensor_copy(mrow[:, c * CH:(c + 1) * CH], stat[:, 0:CH])
            ins = v.tensor_copy(rrow[:, c * CH:(c + 1) * CH], stat[:, 3 * CH:4 * CH])
            mk.sv.inc(ins)

        @block.tensor
        def _(t):
            t.wait_ge(mk.ss.h, rstd_s)
            t.matmul(mk.bank(2)[:, 0:CH], onesf[:], stat[0:1, 0:CH], start=True, stop=True)
            ins = t.matmul(mk.bank(3)[:, 0:CH], onesf[:], stat[0:1, 3 * CH:4 * CH],
                           start=True, stop=True)
            mk.sp.inc(ins)
        bc_pe2 = mk.sp.n

        @block.vector
        def _(v):
            v.wait_ge(mk.sp.h, bc_pe2)
            for dt in range(NDT):
                v.tensor_tensor(xsq[:, dt * CH:(dt + 1) * CH], x1c[:, dt * CH:(dt + 1) * CH],
                                mk.bank(2)[:, 0:CH], ALU.subtract)
                ins = v.tensor_tensor(xn16[:, dt * CH:(dt + 1) * CH],
                                      xsq[:, dt * CH:(dt + 1) * CH],
                                      mk.bank(3)[:, 0:CH], ALU.mult)
                mk.sv.inc(ins)
        xn_done = mk.sv.n

        @block.gpsimd
        def _(g, c=c):
            g.wait_ge(mk.sv.h, xn_done)
            ins = g.dma_start(out=xn1_h[:, :, c * CH:(c + 1) * CH].rearrange("dt p c -> p dt c"),
                              in_=xn16.rearrange("p (dt c) -> p dt c", dt=8))
            mk.sd.inc(ins, 16)

    fin_v = mk.sv.n

    @block.sync
    def _(q):
        q.wait_ge(mk.sv.h, fin_v)
        ins = q.dma_start(out=m1r1_h[0:1, :], in_=mrow[0:1, :])
        mk.sd.inc(ins, 16)
        ins = q.dma_start(out=m1r1_h[1:2, :], in_=rrow[0:1, :])
        mk.sd.inc(ins, 16)



"""P3 emitter: L1_attn (2-level dwconv wavelet + linear attention + LN(g,b)),
x2p = x1 + attn1.

Per batch (2): convs per dt (scalar first tap + vector stt taps), lin-attn
len 512 via PE transposes + block-diag matmuls, folded upsample-conv
synthesis, final LN(gamma,beta) per 512-chunk. y streamed via HBM scratch.
"""

CH = 512


def declare_p3(mk):
    mk.dram_t("h_filter", [4, D], F32, "ExternalInput")
    mk.dram_t("g_filter", [4, D], F32, "ExternalInput")
    mk.dram_t("h_synth", [4, D], F32, "ExternalInput")
    mk.dram_t("g_synth", [4, D], F32, "ExternalInput")
    mk.dram_t("gamma", [D], F32, "ExternalInput")
    mk.dram_t("beta", [D], F32, "ExternalInput")
    mk.dram_t("xn1", [NDT, 128, T], BF16, "Internal")
    mk.dram_t("x1f", [NDT, 128, T], F32, "Internal")
    mk.dram_t("yh", [NDT, 128, T], F32, "Internal")
    mk.dram_t("x2p", [NDT, 128, T], F32, "Internal")

    declare_consts(mk)
    mk.sb("onesf", [1, 128], F32)
    mk.sb("onescol", [128, 1], F32)
    mk.sb("cst", [1, 4], F32)
    mk.sb("gbuf", [128, 16], F32)
    mk.sb("fbuf", [128, 4 * 32], F32)
    mk.sb("fcomb", [128, 8 * 10], F32)
    mk.sb("xnb", [128, 8 * NSEQ], BF16)
    mk.sb("px", [128, NSEQ + 4], F32)
    mk.sb("pl", [128, 1024 + 4], F32)
    mk.sb("ps1", [128, 512 + 2], F32)
    mk.sb("ps2", [128, 512 + 2], F32)
    mk.sb("pt1", [128, 1024 + 2], F32)
    mk.sb("pt2", [128, 1024 + 2], F32)
    mk.sb("csc1", [128, 1024], F32)
    mk.sb("h1b", [128, 8 * 1024], BF16)
    mk.sb("h2b", [128, 8 * 512], BF16)
    mk.sb("l2b", [128, 8 * 512], F32)
    mk.sb("qe", [128, 8 * 512], F32)
    mk.sb("qe16", [128, 8 * 512], BF16)
    mk.sb("v16b", [128, 8 * 512], BF16)
    mk.sb("zsb", [128, 8], F32)
    mk.sb("kfT2", [128, 512], BF16)
    mk.sb("vT2", [128, 512], BF16)
    mk.sb("kvat", [128, 8 * 64], F32)
    mk.sb("KVst2", [128, 8 * 128], BF16)
    mk.sb("Zst2", [128, 8 * 16], BF16)
    mk.sb("latb", [128, 8 * 512], F32)
    mk.sb("low1", [128, 1024], F32)
    mk.sb("ydt", [128, NSEQ], F32)
    mk.sb("den2", [16, 512], F32)
    mk.sb("num16b", [128, 2 * 512], BF16)
    mk.sb("xsq3", [128, 8 * CH], F32)
    mk.sb("stat3", [1, 4 * CH], F32)


def emit_p3(mk, block):
    B = mk.sbuf
    xn1_h, x1f_h, yh_h, x2p_h = (mk.dram["xn1"], mk.dram["x1f"], mk.dram["yh"],
                                 mk.dram["x2p"])
    onesf, onescol, cst = B["onesf"], B["onescol"], B["cst"]
    gbuf, fbuf, fcomb = B["gbuf"], B["fbuf"], B["fcomb"]
    xnb, px, pl = B["xnb"], B["px"], B["pl"]
    ps1, ps2, pt1, pt2, cs1 = B["ps1"], B["ps2"], B["pt1"], B["pt2"], B["csc1"]
    h1b, h2b, l2b = B["h1b"], B["h2b"], B["l2b"]
    qe, qe16, v16 = B["qe"], B["qe16"], B["v16b"]
    zsb, kfT2, vT2, kvat = B["zsb"], B["kfT2"], B["vT2"], B["kvat"]
    KVst2, Zst2, latb, low1, ydt = B["KVst2"], B["Zst2"], B["latb"], B["low1"], B["ydt"]
    den2, num16 = B["den2"], B["num16b"]
    xsq, stat = B["xsq3"], B["stat3"]
    ycc, x1cc, x2pc = B["qe"], B["l2b"], B["latb"]
    idb, sel16 = B["c_idb"], B["c_sel16"]

    consts_d = emit_load_consts(mk, block)

    @block.gpsimd
    def _(g):
        g.memset(onesf[:], 1.0)
        g.memset(onescol[:], 1.0)
        g.memset(cst[0:1, 0:1], LN_EPS)
        g.memset(cst[0:1, 1:2], ATTN_EPS)
        g.memset(px[:], 0.0)
        g.memset(pl[:], 0.0)
        g.memset(ps1[:], 0.0)
        g.memset(ps2[:], 0.0)
        g.memset(pt1[:], 0.0)
        g.memset(pt2[:], 0.0)
        g.memset(KVst2[:], 0.0)
        ins = g.memset(Zst2[:], 0.0)
        mk.sg.inc(ins)
    gp_pre = mk.sg.n

    @block.sync
    def _(q):
        for i, nm in enumerate(["h_filter", "g_filter", "h_synth", "g_synth"]):
            ins = q.dma_start(out=fbuf[:, i * 32:(i + 1) * 32].rearrange("p (k dt) -> p k dt", k=4),
                              in_=mk.dram[nm].rearrange("k (dt p) -> p k dt", p=128),
                              allow_slow_non_contiguous=True)
            mk.sd.inc(ins, 16)
        ins = q.dma_start(out=gbuf[:, 0:8],
                          in_=mk.dram["gamma"].rearrange("(dt p) -> p dt", p=128),
                          allow_slow_non_contiguous=True)
        mk.sd.inc(ins, 16)
        ins = q.dma_start(out=gbuf[:, 8:16],
                          in_=mk.dram["beta"].rearrange("(dt p) -> p dt", p=128),
                          allow_slow_non_contiguous=True)
        mk.sd.inc(ins, 16)
    filt_d = mk.sd.n

    @block.vector
    def _(v):
        v.wait_ge(mk.sd.h, filt_d)
        for dt in range(NDT):
            for fi, base in [(2, 0), (3, 5)]:
                f0 = fbuf[:, fi * 32 + 0 * 8 + dt: fi * 32 + 0 * 8 + dt + 1]
                f1 = fbuf[:, fi * 32 + 1 * 8 + dt: fi * 32 + 1 * 8 + dt + 1]
                f2 = fbuf[:, fi * 32 + 2 * 8 + dt: fi * 32 + 2 * 8 + dt + 1]
                f3 = fbuf[:, fi * 32 + 3 * 8 + dt: fi * 32 + 3 * 8 + dt + 1]
                o = fcomb[:, dt * 10 + base: dt * 10 + base + 5]
                v.tensor_copy(o[:, 0:1], f0)
                v.tensor_add(o[:, 1:2], f1, f2)
                v.tensor_copy(o[:, 2:3], f3)
                v.tensor_add(o[:, 3:4], f0, f1)
                v.tensor_add(o[:, 4:5], f2, f3)
        ins = v.engine_nop()
        mk.sv.inc(ins)
    fcomb_v = mk.sv.n

    def tapsum(out_ap, terms, wait_sv=None, wait_ss=None):
        in0, c0 = terms[0]

        @block.scalar
        def _(s):
            if wait_sv is not None:
                s.wait_ge(mk.sv.h, wait_sv)
            if wait_ss is not None:
                s.wait_ge(mk.ss.h, wait_ss)
            ins = s.activation(out_ap, in0, AF.Identity, bias=0.0, scale=c0)
            mk.ss.inc(ins)
        t0 = mk.ss.n

        @block.vector
        def _(v):
            v.wait_ge(mk.ss.h, t0)
            for in_ap, c_ap in terms[1:]:
                ins = v.scalar_tensor_tensor(out_ap, in_ap, c_ap, out_ap,
                                             ALU.mult, ALU.add)
            mk.sv.inc(ins)
        return mk.sv.n

    def ftap(f, dt, i):
        return fbuf[:, f * 32 + i * 8 + dt: f * 32 + i * 8 + dt + 1]

    def fc(dt, j):
        return fcomb[:, dt * 10 + j: dt * 10 + j + 1]

    for b in range(BL):
        n0 = b * NSEQ
        prev_pe = mk.sp.n
        prev_sv = mk.sv.n
        prev_ss = mk.ss.n

        @block.sync
        def _(q, b=b, n0=n0):
            q.wait_ge(mk.sp.h, prev_pe)
            q.wait_ge(mk.sv.h, prev_sv)
            q.wait_ge(mk.ss.h, prev_ss)
            ins = q.dma_start(out=xnb.rearrange("p (dt c) -> p dt c", dt=8),
                              in_=xn1_h[:, :, n0:n0 + NSEQ].rearrange("dt p c -> p dt c"))
            mk.sd.inc(ins, 16)
        xnb_d = mk.sd.n

        for dt in range(NDT):
            @block.vector
            def _(v, dt=dt):
                if dt == 0:
                    v.wait_ge(mk.sd.h, xnb_d)
                    v.wait_ge(mk.sv.h, fcomb_v)
                    v.wait_ge(mk.sg.h, gp_pre)
                    v.wait_ge(mk.sd.h, consts_d)
                v.wait_ge(mk.ss.h, mk.ss.n)
                ins = v.tensor_copy(px[:, 1:NSEQ + 1], xnb[:, dt * NSEQ:(dt + 1) * NSEQ])
                mk.sv.inc(ins)
            pxv = mk.sv.n

            tapsum(pl[:, 1:1025],
                   [(px[:, i:i + NSEQ:2], ftap(0, dt, i)) for i in range(4)],
                   wait_sv=pxv)
            h1_sv = tapsum(cs1[:, 0:1024],
                           [(px[:, i:i + NSEQ:2], ftap(1, dt, i)) for i in range(4)],
                           wait_sv=pxv)

            @block.scalar
            def _(s, dt=dt):
                s.wait_ge(mk.sv.h, h1_sv)
                ins = s.copy(h1b[:, dt * 1024:(dt + 1) * 1024], cs1[:, 0:1024])
                mk.ss.inc(ins)

            l1_sv = mk.sv.n
            l2_sv = tapsum(l2b[:, dt * 512:(dt + 1) * 512],
                           [(pl[:, i:i + 1024:2], ftap(0, dt, i)) for i in range(4)],
                           wait_sv=l1_sv)
            h2_sv = tapsum(cs1[:, 0:512],
                           [(pl[:, i:i + 1024:2], ftap(1, dt, i)) for i in range(4)],
                           wait_sv=None, wait_ss=None)

            @block.scalar
            def _(s, dt=dt):
                s.wait_ge(mk.sv.h, h2_sv)
                ins = s.copy(h2b[:, dt * 512:(dt + 1) * 512], cs1[:, 0:512])
                mk.ss.inc(ins)
        convs_sv = mk.sv.n
        convs_ss = mk.ss.n

        @block.vector
        def _(v):
            v.wait_ge(mk.ss.h, convs_ss)
            ins = v.tensor_scalar_min(qe[:], l2b[:], 0.0)
            mk.sv.inc(ins)
        qe_min = mk.sv.n

        @block.scalar
        def _(s):
            s.wait_ge(mk.sv.h, qe_min)
            ins = s.activation(qe[:], qe[:], AF.Exp)
            mk.ss.inc(ins)
        qe_exp = mk.ss.n

        @block.vector
        def _(v):
            v.wait_ge(mk.ss.h, qe_exp)
            ins = v.scalar_tensor_tensor(qe[:], l2b[:], 0.0, qe[:], ALU.max, ALU.add)
            mk.sv.inc(ins)
            v.wait_ge(mk.sv.h, mk.sv.n)
            v.tensor_copy(qe16[:], qe[:])
            ins = v.tensor_copy(v16[:], l2b[:])
            mk.sv.inc(ins)
            for dt in range(NDT):
                ins = v.reduce_sum(zsb[:, dt:dt + 1], qe[:, dt * 512:(dt + 1) * 512],
                                   axis=AX.X)
            mk.sv.inc(ins)
        qe_done = mk.sv.n

        for dt in range(NDT):
            prev_kv_v = mk.sv.n

            @block.tensor
            def _(t, dt=dt):
                t.wait_ge(mk.sv.h, prev_kv_v if dt > 0 else qe_done)
                for g4 in range(4):
                    t.transpose(mk.bank(7)[:, g4 * 128:(g4 + 1) * 128],
                                qe16[:, dt * 512 + g4 * 128:dt * 512 + (g4 + 1) * 128], idb[:])
                for g4 in range(4):
                    ins = t.transpose(mk.bank(7)[:, 512 + g4 * 128:512 + (g4 + 1) * 128],
                                      v16[:, dt * 512 + g4 * 128:dt * 512 + (g4 + 1) * 128], idb[:])
                mk.sp.inc(ins)
            tr_pe = mk.sp.n

            @block.vector
            def _(v, dt=dt):
                v.wait_ge(mk.sp.h, tr_pe)
                v.tensor_copy(kfT2[:], mk.bank(7)[:, 0:512])
                ins = v.tensor_copy(vT2[:], mk.bank(7)[:, 512:1024])
                mk.sv.inc(ins)
            tr_ev = mk.sv.n

            @block.tensor
            def _(t, dt=dt):
                t.wait_ge(mk.sv.h, tr_ev)
                for g4 in range(4):
                    t.matmul(mk.bank(6)[0:64, 0:64], kfT2[:, g4 * 128:g4 * 128 + 64],
                             vT2[:, g4 * 128:g4 * 128 + 64], start=(g4 == 0), stop=(g4 == 3))
                for g4 in range(4):
                    ins = t.matmul(mk.bank(6)[64:128, 0:64],
                                   kfT2[:, g4 * 128 + 64:g4 * 128 + 128],
                                   vT2[:, g4 * 128 + 64:g4 * 128 + 128],
                                   start=(g4 == 0), stop=(g4 == 3))
                mk.sp.inc(ins)
            kv_pe = mk.sp.n

            @block.vector
            def _(v, dt=dt):
                v.wait_ge(mk.sp.h, kv_pe)
                ins = v.tensor_copy(kvat[:, dt * 64:(dt + 1) * 64], mk.bank(6)[:, 0:64])
                mk.sv.inc(ins)
        kv_done_v = mk.sv.n

        @block.vector
        def _(v):
            v.wait_ge(mk.sv.h, kv_done_v)
            v.wait_ge(mk.sg.h, gp_pre)
            for dt in range(NDT):
                v.tensor_copy(KVst2[0:64, dt * 128:dt * 128 + 64],
                              kvat[0:64, dt * 64:(dt + 1) * 64])
                v.tensor_copy(KVst2[64:128, dt * 128 + 64:dt * 128 + 128],
                              kvat[64:128, dt * 64:(dt + 1) * 64])
                v.tensor_copy(Zst2[0:64, dt * 16 + 2 * dt:dt * 16 + 2 * dt + 1],
                              zsb[0:64, dt:dt + 1])
                v.tensor_copy(Zst2[64:128, dt * 16 + 2 * dt + 1:dt * 16 + 2 * dt + 2],
                              zsb[64:128, dt:dt + 1])
            ins = v.engine_nop()
            mk.sv.inc(ins)
        st_built = mk.sv.n

        @block.tensor
        def _(t):
            t.wait_ge(mk.sv.h, st_built)
            for dt in range(NDT):
                ins = t.matmul(mk.bank(0)[0:16, 0:512],
                               Zst2[:, dt * 16:(dt + 1) * 16],
                               qe16[:, dt * 512:(dt + 1) * 512],
                               start=(dt == 0), stop=(dt == 7))
            mk.sp.inc(ins)
        den_pe = mk.sp.n

        @block.vector
        def _(v):
            v.wait_ge(mk.sp.h, den_pe)
            ins = v.tensor_scalar_add(den2[:], mk.bank(0)[0:16, 0:512], ATTN_EPS)
            mk.sv.inc(ins); v.wait_ge(mk.sv.h, mk.sv.n)
            ins = v.reciprocal(den2[:], den2[:])
            mk.sv.inc(ins); v.wait_ge(mk.sv.h, mk.sv.n)
            ins = v.engine_nop()
            mk.sv.inc(ins)
        rden_v = mk.sv.n

        num_pe, num_sc, bc_pe, att_sv = [], [], [], []
        for dt in range(NDT):
            @block.tensor
            def _(t, dt=dt):
                if dt >= 2:
                    t.wait_ge(mk.ss.h, num_sc[dt - 2])
                if dt >= 1:
                    t.wait_ge(mk.sv.h, att_sv[dt - 1])
                else:
                    t.wait_ge(mk.sv.h, rden_v)
                ins = t.matmul(mk.bank(1 + dt % 2)[:, 0:512],
                               KVst2[:, dt * 128:(dt + 1) * 128],
                               qe16[:, dt * 512:(dt + 1) * 512], start=True, stop=True)
                num_pe.append(mk.sp.inc(ins))
                ins = t.matmul(mk.bank(3)[:, 0:512], sel16[:, dt * 128:(dt + 1) * 128],
                               den2[:, :], start=True, stop=True)
                bc_pe.append(mk.sp.inc(ins))

            @block.scalar
            def _(s, dt=dt):
                s.wait_ge(mk.sp.h, num_pe[dt])
                if dt >= 2:
                    s.wait_ge(mk.sv.h, att_sv[dt - 2])
                ins = s.copy(num16[:, (dt % 2) * 512:(dt % 2 + 1) * 512],
                             mk.bank(1 + dt % 2)[:, 0:512])
                num_sc.append(mk.ss.inc(ins))

            @block.vector
            def _(v, dt=dt):
                v.wait_ge(mk.ss.h, num_sc[dt])
                v.wait_ge(mk.sp.h, bc_pe[dt])
                ins = v.tensor_tensor(latb[:, dt * 512:(dt + 1) * 512],
                                      num16[:, (dt % 2) * 512:(dt % 2 + 1) * 512],
                                      mk.bank(3)[:, 0:512], ALU.mult)
                att_sv.append(mk.sv.inc(ins))
        lat_done = mk.sv.n

        for dt in range(NDT):
            @block.vector
            def _(v, dt=dt):
                if dt == 0:
                    v.wait_ge(mk.sv.h, lat_done)
                v.wait_ge(mk.ss.h, mk.ss.n)
                v.tensor_copy(ps1[:, 1:513], latb[:, dt * 512:(dt + 1) * 512])
                ins = v.tensor_copy(ps2[:, 1:513], h2b[:, dt * 512:(dt + 1) * 512])
                mk.sv.inc(ins)
            pads1 = mk.sv.n

            tapsum(low1[:, 0:1024:2],
                   [(ps1[:, 0:512], fc(dt, 0)), (ps1[:, 1:513], fc(dt, 1)),
                    (ps1[:, 2:514], fc(dt, 2)),
                    (ps2[:, 0:512], fc(dt, 5)), (ps2[:, 1:513], fc(dt, 6)),
                    (ps2[:, 2:514], fc(dt, 7))],
                   wait_sv=pads1)
            s1_sv = tapsum(low1[:, 1:1024:2],
                           [(ps1[:, 1:513], fc(dt, 3)), (ps1[:, 2:514], fc(dt, 4)),
                            (ps2[:, 1:513], fc(dt, 8)), (ps2[:, 2:514], fc(dt, 9))],
                           wait_sv=None)

            @block.vector
            def _(v, dt=dt):
                v.wait_ge(mk.sv.h, s1_sv)
                v.wait_ge(mk.ss.h, mk.ss.n)
                v.wait_ge(mk.sd.h, mk.sd.n)
                v.tensor_copy(pt1[:, 1:1025], low1[:, 0:1024])
                ins = v.tensor_copy(pt2[:, 1:1025], h1b[:, dt * 1024:(dt + 1) * 1024])
                mk.sv.inc(ins)
            pads2 = mk.sv.n

            tapsum(ydt[:, 0:2048:2],
                   [(pt1[:, 0:1024], fc(dt, 0)), (pt1[:, 1:1025], fc(dt, 1)),
                    (pt1[:, 2:1026], fc(dt, 2)),
                    (pt2[:, 0:1024], fc(dt, 5)), (pt2[:, 1:1025], fc(dt, 6)),
                    (pt2[:, 2:1026], fc(dt, 7))],
                   wait_sv=pads2)
            s2_sv = tapsum(ydt[:, 1:2048:2],
                           [(pt1[:, 1:1025], fc(dt, 3)), (pt1[:, 2:1026], fc(dt, 4)),
                            (pt2[:, 1:1025], fc(dt, 8)), (pt2[:, 2:1026], fc(dt, 9))],
                           wait_sv=None)

            @block.vector
            def _(v, dt=dt):
                v.wait_ge(mk.sv.h, s2_sv)
                ins = v.tensor_tensor(ydt[:], ydt[:], xnb[:, dt * NSEQ:(dt + 1) * NSEQ],
                                      ALU.add)
                mk.sv.inc(ins)
            y_v = mk.sv.n

            @block.sync
            def _(q, dt=dt, n0=n0):
                q.wait_ge(mk.sv.h, y_v)
                ins = q.dma_start(out=yh_h[dt, :, n0:n0 + NSEQ], in_=ydt[:])
                mk.sd.inc(ins, 16)
        y_stored = mk.sd.n

        for cc in range(4):
            c = b * 4 + cc
            prev2_pe = mk.sp.n
            prev2_sd = mk.sd.n

            @block.sync
            def _(q, c=c):
                q.wait_ge(mk.sv.h, mk.sv.n)
                q.wait_ge(mk.ss.h, mk.ss.n)
                q.wait_ge(mk.sd.h, prev2_sd)
                ins = q.dma_start(out=ycc.rearrange("p (dt c) -> p dt c", dt=8),
                                  in_=yh_h[:, :, c * CH:(c + 1) * CH].rearrange("dt p c -> p dt c"))
                mk.sd.inc(ins, 16)
                ins = q.dma_start(out=x1cc.rearrange("p (dt c) -> p dt c", dt=8),
                                  in_=x1f_h[:, :, c * CH:(c + 1) * CH].rearrange("dt p c -> p dt c"))
                mk.sd.inc(ins, 16)
            ld2 = mk.sd.n

            @block.scalar
            def _(s):
                s.wait_ge(mk.sd.h, ld2)
                for dt in range(NDT):
                    ins = s.activation(xsq[:, dt * CH:(dt + 1) * CH],
                                       ycc[:, dt * CH:(dt + 1) * CH], AF.Square)
                    mk.ss.inc(ins)
            sq_done = mk.ss.n

            @block.tensor
            def _(t):
                t.wait_ge(mk.ss.h, sq_done)
                t.wait_ge(mk.sv.h, mk.sv.n)
                for dt in range(NDT):
                    t.matmul(mk.bank(0)[0:1, 0:CH], onescol[:], ycc[:, dt * CH:(dt + 1) * CH],
                             start=(dt == 0), stop=(dt == 7))
                for dt in range(NDT):
                    ins = t.matmul(mk.bank(1)[0:1, 0:CH], onescol[:], xsq[:, dt * CH:(dt + 1) * CH],
                                   start=(dt == 0), stop=(dt == 7))
                mk.sp.inc(ins)
            stats_pe = mk.sp.n

            @block.vector
            def _(v):
                v.wait_ge(mk.sp.h, stats_pe)
                v.tensor_scalar_mul(stat[:, 0:CH], mk.bank(0)[0:1, 0:CH], 1.0 / D)
                v.tensor_mul(stat[:, CH:2 * CH], stat[:, 0:CH], stat[:, 0:CH])
                ins = v.scalar_tensor_tensor(stat[:, 2 * CH:3 * CH], mk.bank(1)[0:1, 0:CH],
                                             1.0 / D, stat[:, CH:2 * CH], ALU.mult, ALU.subtract)
                mk.sv.inc(ins)
            var_v = mk.sv.n

            @block.scalar
            def _(s):
                s.wait_ge(mk.sv.h, var_v)
                s.activation(stat[:, 3 * CH:4 * CH], stat[:, 2 * CH:3 * CH], AF.Ln,
                             bias=cst[0:1, 0:1], scale=1.0)
                ins = s.activation(stat[:, 3 * CH:4 * CH], stat[:, 3 * CH:4 * CH], AF.Exp,
                                   bias=0.0, scale=-0.5)
                mk.ss.inc(ins)
            rstd_s = mk.ss.n

            @block.tensor
            def _(t):
                t.wait_ge(mk.ss.h, rstd_s)
                t.matmul(mk.bank(2)[:, 0:CH], onesf[:], stat[0:1, 0:CH], start=True, stop=True)
                ins = t.matmul(mk.bank(3)[:, 0:CH], onesf[:], stat[0:1, 3 * CH:4 * CH],
                               start=True, stop=True)
                mk.sp.inc(ins)
            bc2_pe = mk.sp.n

            @block.vector
            def _(v, c=c):
                v.wait_ge(mk.sp.h, bc2_pe)
                for dt in range(NDT):
                    v.tensor_tensor(xsq[:, dt * CH:(dt + 1) * CH], ycc[:, dt * CH:(dt + 1) * CH],
                                    mk.bank(2)[:, 0:CH], ALU.subtract)
                    v.tensor_tensor(xsq[:, dt * CH:(dt + 1) * CH], xsq[:, dt * CH:(dt + 1) * CH],
                                    mk.bank(3)[:, 0:CH], ALU.mult)
                    v.tensor_scalar(xsq[:, dt * CH:(dt + 1) * CH], xsq[:, dt * CH:(dt + 1) * CH],
                                    gbuf[:, dt:dt + 1], gbuf[:, 8 + dt:9 + dt],
                                    ALU.mult, ALU.add)
                    ins = v.tensor_tensor(x2pc[:, dt * CH:(dt + 1) * CH],
                                          x1cc[:, dt * CH:(dt + 1) * CH],
                                          xsq[:, dt * CH:(dt + 1) * CH], ALU.add)
                    mk.sv.inc(ins)
            x2p_v = mk.sv.n

            @block.sync
            def _(q, c=c):
                q.wait_ge(mk.sv.h, x2p_v)
                ins = q.dma_start(out=x2p_h[:, :, c * CH:(c + 1) * CH].rearrange("dt p c -> p dt c"),
                                  in_=x2pc.rearrange("p (dt c) -> p dt c", dt=8))
                mk.sd.inc(ins, 16)


"""P4 emitter: L1_ffn fp4 quant-dequant SwiGLU + final output.

Weight prep: quantize w_gate/w_up/w_down (blocks of 16 along last dim),
bf16 -> HBM; wgT/wuT via transpose-DMA -> HBM (streamed per I-chunk);
wdT via transpose-DMA -> resident SBUF.
Main loop per 512-chunk: xn1f=(x1-m)r, fp32-transpose, x-quant, bf16-transpose
back; gate/up with xq stationary (token-major hidden); h-quant; bf16-transpose;
down-proj; + x2p; fp32-transpose out; store token-major output.
"""

CH = 512
NCH = T // CH
DEBUG = False
ICS = [512, 512, 512, 512, 512, 256]


def declare_p4(mk):
    mk.dram_t("w_gate", [I1, D], F32, "ExternalInput")
    mk.dram_t("w_up", [I1, D], F32, "ExternalInput")
    mk.dram_t("w_down", [D, I1], F32, "ExternalInput")
    mk.dram_t("x1f", [NDT, 128, T], F32, "Internal")
    mk.dram_t("x2p", [NDT, 128, T], F32, "Internal")
    mk.dram_t("m1r1", [2, T], F32, "Internal")
    mk.dram_t("wgq", [I1, D], BF16, "Internal")
    mk.dram_t("wuq", [I1, D], BF16, "Internal")
    mk.dram_t("wdq", [D, I1], BF16, "Internal")
    mk.dram_t("wgT", [NDT, 128, I1], BF16, "Internal")
    mk.dram_t("wuT", [NDT, 128, I1], BF16, "Internal")
    mk.dram_t("out", [BL, NSEQ, D], F32, "ExternalOutput")
    if DEBUG:
        mk.dram_t("dbg_xq", [128, 8 * CH], BF16, "ExternalOutput")
        mk.dram_t("dbg_hq", [128, NIT * CH], BF16, "ExternalOutput")
        mk.dram_t("dbg_xn", [128, 8 * CH], F32, "ExternalOutput")
        mk.dram_t("dbg_xtm", [128, 1024], F32, "ExternalOutput")
        mk.dram_t("dbg_xqtm", [128, 1024], BF16, "ExternalOutput")

    declare_consts(mk)
    mk.sb("onesf", [1, 128], F32)
    mk.sb("wdT", [128, NIT * 1024], BF16)
    mk.sb("wgub", [128, 2 * 2 * 8 * 512], BF16)
    mk.sb("x1fc", [128, 8 * CH], F32)
    mk.sb("xn1f", [128, 8 * CH], F32)
    mk.sb("x2pc", [128, 8 * CH], F32)
    mk.sb("mr", [1, 2 * CH], F32)
    mk.sb("xtm", [128, 1024], F32)
    mk.sb("xqtm", [128, 1024], BF16)
    mk.sb("xqfm", [128, 8 * CH], BF16)
    mk.sb("htm", [128, 2 * 512], F32)
    mk.sb("hqtm", [128, 2 * 512], BF16)
    mk.sb("hqfm", [128, NIT * CH], BF16)
    mk.sb("sgb", [128, 2 * 512], F32)
    mk.sb("xab", [128, 1024], F32)
    mk.sb("qb", [128, 1024], F32)
    mk.sb("c2b", [128, 1024], F32)
    mk.sb("sgn", [128, 1024], F32)
    mk.sb("amax", [128, 64], F32)
    mk.sb("rs", [128, 64], F32)
    mk.sb("sc2", [128, 64], F32)
    mk.sb("otm", [128, 1024], F32)
    mk.sbuf["wq_ld"] = mk.sbuf["x1fc"]
    mk.sbuf["wq_16"] = mk.sbuf["xqfm"]


def emit_quant(mk, block, in_ap, out_ap, L, wait_sv=None, wait_ss=None, wait_sd=None,
               out_wait_sp=None):
    B = mk.sbuf
    xab, qb, c2b, sgn = B["xab"], B["qb"], B["c2b"], B["sgn"]
    amax, rs, sc2 = B["amax"], B["rs"], B["sc2"]
    nb = L // 16
    scratch_free = mk.sv.n

    @block.scalar
    def _(s):
        s.wait_ge(mk.sv.h, scratch_free)
        if wait_ss is not None:
            s.wait_ge(mk.ss.h, wait_ss)
        if wait_sv is not None:
            s.wait_ge(mk.sv.h, wait_sv)
        if wait_sd is not None:
            s.wait_ge(mk.sd.h, wait_sd)
        s.activation(xab[:, 0:L], in_ap, AF.Abs)
        ins = s.sign(sgn[:, 0:L], in_ap)
        mk.ss.inc(ins)
    abs_s = mk.ss.n

    @block.vector
    def _(v):
        v.wait_ge(mk.ss.h, abs_s)
        ins = v.reduce_max(amax[:, 0:nb], xab[:, 0:L].rearrange("p (b k) -> p b k", k=16),
                           axis=AX.X)
        mk.sv.inc(ins)
        v.wait_ge(mk.sv.h, mk.sv.n)
        ins = v.tensor_scalar_max(amax[:, 0:nb], amax[:, 0:nb], 6e-12)
        mk.sv.inc(ins)
        v.wait_ge(mk.sv.h, mk.sv.n)
        ins = v.reciprocal(rs[:, 0:nb], amax[:, 0:nb])
        mk.sv.inc(ins)
        v.wait_ge(mk.sv.h, mk.sv.n)
        v.tensor_scalar_mul(rs[:, 0:nb], rs[:, 0:nb], 6.0)
        ins = v.tensor_scalar_mul(sc2[:, 0:nb], amax[:, 0:nb], 1.0 / 12.0)
        mk.sv.inc(ins)
        v.wait_ge(mk.sv.h, mk.sv.n)
        a3 = xab[:, 0:L].rearrange("p (b k) -> p b k", k=16)
        rs3 = rs[:, 0:nb].rearrange("p (b o) -> p b o", o=1).broadcast_to([128, nb, 16])
        sc3 = sc2[:, 0:nb].rearrange("p (b o) -> p b o", o=1).broadcast_to([128, nb, 16])
        v.tensor_tensor(a3, a3, rs3, ALU.mult)
        a = xab[:, 0:L]
        q = qb[:, 0:L]
        c2 = c2b[:, 0:L]
        v.tensor_single_scalar(q, a, 0.25, ALU.is_gt)
        v.scalar_tensor_tensor(q, a, 0.75, q, ALU.is_gt, ALU.add)
        v.scalar_tensor_tensor(q, a, 1.25, q, ALU.is_gt, ALU.add)
        v.scalar_tensor_tensor(q, a, 1.75, q, ALU.is_gt, ALU.add)
        v.tensor_single_scalar(c2, a, 2.5, ALU.is_gt)
        v.scalar_tensor_tensor(c2, a, 3.5, c2, ALU.is_gt, ALU.add)
        v.scalar_tensor_tensor(q, c2, 2.0, q, ALU.mult, ALU.add)
        v.tensor_single_scalar(c2, a, 5.0, ALU.is_gt)
        v.scalar_tensor_tensor(q, c2, 4.0, q, ALU.mult, ALU.add)
        v.tensor_tensor(q, q, sgn[:, 0:L], ALU.mult)
        if out_wait_sp is not None:
            v.wait_ge(mk.sp.h, out_wait_sp)
        q3 = qb[:, 0:L].rearrange("p (b k) -> p b k", k=16)
        o3 = out_ap.rearrange("p (b k) -> p b k", k=16)
        ins = v.tensor_tensor(o3, q3, sc3, ALU.mult)
        mk.sv.inc(ins)
    return mk.sv.n, abs_s


def emit_p4_weights(mk, block):
    B = mk.sbuf
    wq_ld, wq_16 = B["wq_ld"], B["wq_16"]
    wdT = B["wdT"]

    for wi, (src, dst) in enumerate([("w_gate", "wgq"), ("w_up", "wuq"),
                                     ("w_down", "wdq")]):
        nrow_t = NIT if wi < 2 else NDT
        L = D if wi < 2 else I1
        for it in range(nrow_t):
            rows = 128
            prev_q = mk.sv.n

            @block.sync
            def _(q, src=src, it=it, rows=rows, L=L):
                q.wait_ge(mk.sv.h, prev_q)
                q.wait_ge(mk.sd.h, mk.sd.n)
                ins = q.dma_start(out=wq_ld[:, 0:L],
                                  in_=mk.dram[src][it * 128:(it + 1) * 128, :])
                mk.sd.inc(ins, 16)
            ld_d = mk.sd.n

            qdone = None
            for off in range(0, L, 1024):
                LL = min(1024, L - off)
                qdone, _ = emit_quant(mk, block, wq_ld[:, off:off + LL],
                                      wq_16[:, off:off + LL], LL, wait_sd=ld_d)

            @block.sync
            def _(q, dst=dst, it=it, L=L, qdone=qdone):
                q.wait_ge(mk.sv.h, qdone)
                ins = q.dma_start(out=mk.dram[dst][it * 128:(it + 1) * 128, :],
                                  in_=wq_16[:, 0:L])
                mk.sd.inc(ins, 16)
    wq_done = mk.sd.n

    @block.sync
    def _(q):
        q.wait_ge(mk.sd.h, wq_done)
        for kt in range(NDT):
            ins = q.dma_start_transpose(out=mk.sbuf["wq_16"][:, 0:I1],
                                        in_=mk.dram["wgq"][:, kt * 128:(kt + 1) * 128])
            mk.sd.inc(ins, 16)
            q.wait_ge(mk.sd.h, mk.sd.n)
            ins = q.dma_start(out=mk.dram["wgT"][kt], in_=mk.sbuf["wq_16"][:, 0:I1])
            mk.sd.inc(ins, 16)
            q.wait_ge(mk.sd.h, mk.sd.n)
            ins = q.dma_start_transpose(out=mk.sbuf["wq_16"][:, 0:I1],
                                        in_=mk.dram["wuq"][:, kt * 128:(kt + 1) * 128])
            mk.sd.inc(ins, 16)
            q.wait_ge(mk.sd.h, mk.sd.n)
            ins = q.dma_start(out=mk.dram["wuT"][kt], in_=mk.sbuf["wq_16"][:, 0:I1])
            mk.sd.inc(ins, 16)
            q.wait_ge(mk.sd.h, mk.sd.n)
        for kt in range(NIT):
            ins = q.dma_start_transpose(out=wdT[:, kt * 1024:(kt + 1) * 1024],
                                        in_=mk.dram["wdq"][:, kt * 128:(kt + 1) * 128])
            mk.sd.inc(ins, 16)
    return mk.sd.n


def emit_p4(mk, block):
    B = mk.sbuf
    x1f_h, x2p_h, m1r1_h, out_h = (mk.dram["x1f"], mk.dram["x2p"],
                                   mk.dram["m1r1"], mk.dram["out"])
    wdT, wgub = B["wdT"], B["wgub"]
    x1fc, xn1f, x2pc, mr = B["x1fc"], B["xn1f"], B["x2pc"], B["mr"]
    xtm, xqtm, xqfm = B["xtm"], B["xqtm"], B["xqfm"]
    htm, hqtm, hqfm, sgb = B["htm"], B["hqtm"], B["hqfm"], B["sgb"]
    otm = B["otm"]
    onesf = B["onesf"]
    idf, idb = B["c_idf"], B["c_idb"]
    outc = xn1f

    @block.gpsimd
    def _(g):
        ins = g.memset(onesf[:], 1.0)
        mk.sg.inc(ins)
    ones_g = mk.sg.n

    wprep_d = emit_p4_weights(mk, block)

    def stream_wgu(ic, buf, wait_pe):
        @block.sync
        def _(q, ic=ic, buf=buf):
            q.wait_ge(mk.sp.h, wait_pe)
            L = ICS[ic]
            i0 = ic * 512
            ins = q.dma_start(
                out=wgub[:, buf * 4096:(buf + 1) * 4096].rearrange("p (kt l) -> p kt l", kt=8)[:, :, 0:L],
                in_=mk.dram["wgT"][:, :, i0:i0 + L].rearrange("kt p l -> p kt l"))
            mk.sd.inc(ins, 16)
            ins = q.dma_start(
                out=wgub[:, 8192 + buf * 4096:8192 + (buf + 1) * 4096].rearrange("p (kt l) -> p kt l", kt=8)[:, :, 0:L],
                in_=mk.dram["wuT"][:, :, i0:i0 + L].rearrange("kt p l -> p kt l"))
            mk.sd.inc(ins, 16)
        return mk.sd.n

    def wg_ap(buf, kt, L):
        return wgub[:, buf * 4096 + kt * 512: buf * 4096 + kt * 512 + L]

    def wu_ap(buf, kt, L):
        return wgub[:, 8192 + buf * 4096 + kt * 512: 8192 + buf * 4096 + kt * 512 + L]

    wgu_rdy = stream_wgu(0, 0, 0)

    for c in range(NCH):
        b = c // 4
        prev_pe = mk.sp.n
        prev_sd = mk.sd.n
        prev_sv = mk.sv.n

        @block.sync
        def _(q, c=c):
            q.wait_ge(mk.sp.h, prev_pe)
            q.wait_ge(mk.sv.h, prev_sv)
            q.wait_ge(mk.sd.h, prev_sd)
            ins = q.dma_start(out=x1fc.rearrange("p (dt c) -> p dt c", dt=8),
                              in_=x1f_h[:, :, c * CH:(c + 1) * CH].rearrange("dt p c -> p dt c"))
            mk.sd.inc(ins, 16)
            ins = q.dma_start(out=mr[0:1, 0:CH], in_=m1r1_h[0:1, c * CH:(c + 1) * CH])
            mk.sd.inc(ins, 16)
            ins = q.dma_start(out=mr[0:1, CH:2 * CH], in_=m1r1_h[1:2, c * CH:(c + 1) * CH])
            mk.sd.inc(ins, 16)

        @block.scalar
        def _(s, c=c):
            s.wait_ge(mk.sp.h, prev_pe)
            s.wait_ge(mk.sv.h, prev_sv)
            s.wait_ge(mk.sd.h, prev_sd)
            ins = s.dma_start(out=x2pc.rearrange("p (dt c) -> p dt c", dt=8),
                              in_=x2p_h[:, :, c * CH:(c + 1) * CH].rearrange("dt p c -> p dt c"))
            mk.sd.inc(ins, 16)
        loads_d = mk.sd.n

        @block.tensor
        def _(t, c=c):
            t.wait_ge(mk.sd.h, loads_d)
            if c == 0:
                t.wait_ge(mk.sg.h, ones_g)
            t.wait_ge(mk.sv.h, mk.sv.n)
            t.wait_ge(mk.ss.h, mk.ss.n)
            t.matmul(mk.bank(4)[:, 0:CH], onesf[:], mr[0:1, 0:CH], start=True, stop=True)
            ins = t.matmul(mk.bank(5)[:, 0:CH], onesf[:], mr[0:1, CH:2 * CH],
                           start=True, stop=True)
            mk.sp.inc(ins)
        mrb_pe = mk.sp.n

        @block.vector
        def _(v):
            v.wait_ge(mk.sp.h, mrb_pe)
            for dt in range(NDT):
                v.tensor_tensor(xn1f[:, dt * CH:(dt + 1) * CH], x1fc[:, dt * CH:(dt + 1) * CH],
                                mk.bank(4)[:, 0:CH], ALU.subtract)
                ins = v.tensor_tensor(xn1f[:, dt * CH:(dt + 1) * CH],
                                      xn1f[:, dt * CH:(dt + 1) * CH],
                                      mk.bank(5)[:, 0:CH], ALU.mult)
                mk.sv.inc(ins)
        xn_v = mk.sv.n

        trx_pe_l, trx_sc_l = [], []
        for g in range(4):
            for dt in range(8):
                gi = g * 8 + dt

                @block.tensor
                def _(t, g=g, dt=dt, gi=gi):
                    if gi == 0:
                        t.wait_ge(mk.sv.h, xn_v)
                        t.wait_ge(mk.ss.h, mk.ss.n)
                    if gi >= 2:
                        t.wait_ge(mk.ss.h, trx_sc_l[gi - 2])
                    ins = t.transpose(mk.bank(4 + gi % 2)[:, 0:128],
                                      xn1f[:, dt * CH + g * 128:dt * CH + (g + 1) * 128],
                                      idf[:])
                    trx_pe_l.append(mk.sp.inc(ins))

                @block.scalar
                def _(s, g=g, dt=dt, gi=gi):
                    s.wait_ge(mk.sp.h, trx_pe_l[gi])
                    ins = s.copy(xtm[:, dt * 128:(dt + 1) * 128], mk.bank(4 + gi % 2)[:, 0:128])
                    trx_sc_l.append(mk.ss.inc(ins))
            xtm_s = mk.ss.n

            q_sv, _ = emit_quant(mk, block, xtm[:, 0:1024], xqtm[:, 0:1024], 1024,
                                 wait_ss=xtm_s)
            if DEBUG and c == 0 and g == 0:
                @block.sync
                def _(q):
                    q.wait_ge(mk.sv.h, q_sv)
                    ins = q.dma_start(out=mk.dram["dbg_xn"][:, :], in_=xn1f[:])
                    mk.sd.inc(ins, 16)
                    ins = q.dma_start(out=mk.dram["dbg_xtm"][:, :], in_=xtm[:])
                    mk.sd.inc(ins, 16)
                    ins = q.dma_start(out=mk.dram["dbg_xqtm"][:, :], in_=xqtm[:])
                    mk.sd.inc(ins, 16)

            @block.tensor
            def _(t, g=g):
                t.wait_ge(mk.sv.h, q_sv)
                t.wait_ge(mk.sv.h, mk.sv.n)
                for dt in range(8):
                    ins = t.transpose(mk.bank(7)[:, dt * 128:(dt + 1) * 128],
                                      xqtm[:, dt * 128:(dt + 1) * 128], idb[:])
                mk.sp.inc(ins)
            trq_pe = mk.sp.n

            @block.vector
            def _(v, g=g):
                v.wait_ge(mk.sp.h, trq_pe)
                for kt in range(8):
                    ins = v.tensor_copy(xqfm[:, kt * CH + g * 128:kt * CH + (g + 1) * 128],
                                        mk.bank(7)[:, kt * 128:(kt + 1) * 128])
                mk.sv.inc(ins)
        xq_done = mk.sv.n

        GRP = [(ic, g) for ic in range(6) for g in range(4)]
        gu_pe_l, silu_l, hmul_l, absm_l, qm_l, trm_l, cpm_l = {}, {}, {}, {}, {}, {}, {}

        def emit_tr(j):
            ic_, g_ = GRP[j]
            nt_ = ICS[ic_] // 128
            p_ = j % 2

            @block.tensor
            def _(t, j=j, nt_=nt_, p_=p_):
                t.wait_ge(mk.sv.h, qm_l[j])
                if j >= 2:
                    t.wait_ge(mk.sv.h, cpm_l[j - 2])
                for jj in range(nt_):
                    ins = t.transpose(mk.bank(7)[:, jj * 128:(jj + 1) * 128],
                                      hqtm[:, p_ * 512 + jj * 128:p_ * 512 + (jj + 1) * 128],
                                      idb[:])
                trm_l[j] = mk.sp.inc(ins)

            @block.vector
            def _(v, j=j, ic_=ic_, g_=g_, nt_=nt_):
                v.wait_ge(mk.sp.h, trm_l[j])
                for jj in range(nt_):
                    it = ic_ * 4 + jj
                    ins = v.tensor_copy(hqfm[:, it * CH + g_ * 128:it * CH + (g_ + 1) * 128],
                                        mk.bank(7)[:, jj * 128:(jj + 1) * 128])
                cpm_l[j] = mk.sv.inc(ins)

        for i, (ic, g) in enumerate(GRP):
            L = ICS[ic]
            p = i % 2

            @block.tensor
            def _(t, i=i, ic=ic, g=g, L=L, wgu_rdy=wgu_rdy):
                if g == 0:
                    t.wait_ge(mk.sd.h, wgu_rdy)
                    if ic == 0:
                        t.wait_ge(mk.sv.h, xq_done)
                if i >= 1:
                    t.wait_ge(mk.ss.h, silu_l[i - 1])
                    t.wait_ge(mk.sv.h, hmul_l[i - 1])
                buf = ic % 2
                for kt in range(8):
                    t.matmul(mk.bank(0)[:, 0:L], xqfm[:, kt * CH + g * 128:kt * CH + (g + 1) * 128],
                             wg_ap(buf, kt, L), start=(kt == 0), stop=(kt == 7))
                for kt in range(8):
                    ins = t.matmul(mk.bank(1)[:, 0:L], xqfm[:, kt * CH + g * 128:kt * CH + (g + 1) * 128],
                                   wu_ap(buf, kt, L), start=(kt == 0), stop=(kt == 7))
                gu_pe_l[i] = mk.sp.inc(ins)

            if i >= 2:
                emit_tr(i - 2)

            @block.scalar
            def _(s, i=i, L=L, p=p):
                s.wait_ge(mk.sp.h, gu_pe_l[i])
                if i >= 2:
                    s.wait_ge(mk.sv.h, hmul_l[i - 2])
                ins = s.activation(sgb[:, p * 512:p * 512 + L], mk.bank(0)[:, 0:L],
                                   AF.Silu)
                silu_l[i] = mk.ss.inc(ins)

            @block.vector
            def _(v, i=i, L=L, p=p):
                v.wait_ge(mk.ss.h, silu_l[i])
                if i >= 2:
                    v.wait_ge(mk.ss.h, absm_l[i - 2])
                ins = v.tensor_tensor(htm[:, p * 512:p * 512 + L],
                                      sgb[:, p * 512:p * 512 + L],
                                      mk.bank(1)[:, 0:L], ALU.mult)
                hmul_l[i] = mk.sv.inc(ins)

            qm_l[i], absm_l[i] = emit_quant(
                mk, block, htm[:, p * 512:p * 512 + L],
                hqtm[:, p * 512:p * 512 + L], L, wait_sv=hmul_l[i],
                out_wait_sp=(trm_l[i - 2] if i >= 2 else None))

            if g == 3:
                if ic < 5:
                    wgu_rdy = stream_wgu(ic + 1, (ic + 1) % 2, mk.sp.n)
                elif c < NCH - 1:
                    wgu_rdy = stream_wgu(0, 0, mk.sp.n)
        emit_tr(22)
        emit_tr(23)
        hq_done = mk.sv.n

        if DEBUG and c == 0:
            @block.sync
            def _(q):
                q.wait_ge(mk.sv.h, hq_done)
                ins = q.dma_start(out=mk.dram["dbg_xq"][:, :], in_=xqfm[:])
                mk.sd.inc(ins, 16)
                ins = q.dma_start(out=mk.dram["dbg_hq"][:, :], in_=hqfm[:])
                mk.sd.inc(ins, 16)

        d_pe, d_sv = [], []
        for mt in range(NDT):
            @block.tensor
            def _(t, mt=mt):
                if mt == 0:
                    t.wait_ge(mk.sv.h, hq_done)
                    t.wait_ge(mk.sd.h, wprep_d)
                if mt >= 2:
                    t.wait_ge(mk.sv.h, d_sv[mt - 2])
                for kt in range(NIT):
                    ins = t.matmul(mk.bank(2 + mt % 2)[:, 0:CH],
                                   wdT[:, kt * 1024 + mt * 128:kt * 1024 + (mt + 1) * 128],
                                   hqfm[:, kt * CH:(kt + 1) * CH],
                                   start=(kt == 0), stop=(kt == NIT - 1))
                d_pe.append(mk.sp.inc(ins))

            @block.vector
            def _(v, mt=mt):
                v.wait_ge(mk.sp.h, d_pe[mt])
                ins = v.tensor_tensor(outc[:, mt * CH:(mt + 1) * CH],
                                      mk.bank(2 + mt % 2)[:, 0:CH],
                                      x2pc[:, mt * CH:(mt + 1) * CH], ALU.add)
                d_sv.append(mk.sv.inc(ins))
        out_done = mk.sv.n

        tro_pe_l, tro_sc_l = [], []
        for g in range(4):
            for dt in range(8):
                gi = g * 8 + dt

                @block.tensor
                def _(t, g=g, dt=dt, gi=gi):
                    if gi == 0:
                        t.wait_ge(mk.sv.h, out_done)
                    if gi >= 2:
                        t.wait_ge(mk.ss.h, tro_sc_l[gi - 2])
                    ins = t.transpose(mk.bank(4 + gi % 2)[:, 128:256],
                                      outc[:, dt * CH + g * 128:dt * CH + (g + 1) * 128],
                                      idf[:])
                    tro_pe_l.append(mk.sp.inc(ins))

                @block.scalar
                def _(s, g=g, dt=dt, gi=gi):
                    s.wait_ge(mk.sp.h, tro_pe_l[gi])
                    if dt == 0:
                        s.wait_ge(mk.sd.h, mk.sd.n)
                    ins = s.copy(otm[:, dt * 128:(dt + 1) * 128],
                                 mk.bank(4 + gi % 2)[:, 128:256])
                    tro_sc_l.append(mk.ss.inc(ins))
            otm_s = mk.ss.n

            @block.scalar
            def _(s, c=c, g=g, b=b):
                n0 = (c % 4) * CH + g * 128
                ins = s.dma_start(out=out_h[b, n0:n0 + 128, :], in_=otm[:])
                mk.sd.inc(ins, 16)




_NCORES = 8


def emit_final_wait_all(mk, block):
    @block.sync
    def _(q):
        q.wait_ge(mk.sd.h, mk.sd.n)
        q.wait_ge(mk.sg.h, mk.sg.n)

    @block.gpsimd
    def _(g):
        g.wait_ge(mk.sd.h, mk.sd.n)
        g.wait_ge(mk.sg.h, mk.sg.n)


def _build_fused():
    nc = bass.Bass("TRN2", target_bir_lowering=False)
    ctx = ExitStack()
    mk = MK(nc, ctx, phase_io=None)
    with ctx:
        mk.psum_init()
        block = ctx.enter_context(nc.Block())
        declare_p1a(mk)
        emit_preload_tbl(mk, block, TBL_LN_EXP)
        emit_const_prep(mk, block)
        emit_p0_qkv(mk, block)
        emit_dec_bcast_p1a(mk, block)
        emit_p1a(mk, block)
        emit_barrier(mk, block)
        mk.free_phase_sbuf()
        declare_p1b(mk)
        emit_preload_tbl(mk, block, TBL_SILU)
        emit_p1b(mk, block)
        emit_barrier(mk, block)
        mk.free_phase_sbuf()
        declare_p2(mk)
        emit_preload_tbl(mk, block, TBL_LN_EXP)
        emit_p2(mk, block)
        emit_barrier(mk, block)
        mk.free_phase_sbuf()
        declare_p3(mk)
        emit_preload_tbl(mk, block, TBL_LN_EXP)
        emit_p3(mk, block)
        emit_barrier(mk, block)
        mk.free_phase_sbuf()
        declare_p4(mk)
        emit_preload_tbl(mk, block, TBL_SILU)
        emit_load_consts(mk, block)
        emit_p4(mk, block)
        emit_final_wait_all(mk, block)
    return nc


def kernel(**inputs):
    consts = build_consts()
    f32 = lambda a: np.ascontiguousarray(a, dtype=np.float32)
    x_full = f32(inputs["L0_attn_x"])
    nc = _build_fused()
    im = []
    for i in range(_NCORES):
        d = {
            "x0": np.ascontiguousarray(x_full[2 * i:2 * i + 2]),
            "W_q": f32(inputs["L0_attn_W_q"]), "W_k": f32(inputs["L0_attn_W_k"]),
            "W_v": f32(inputs["L0_attn_W_v"]), "W_o": f32(inputs["L0_attn_W_o"]),
            "b_q": f32(inputs["L0_attn_b_q"]), "b_k": f32(inputs["L0_attn_b_k"]),
            "b_v": f32(inputs["L0_attn_b_v"]), "b_o": f32(inputs["L0_attn_b_o"]),
            "w_router": f32(inputs["L0_ffn_w_router"]),
            "w1": f32(inputs["L0_ffn_w1"]), "w3": f32(inputs["L0_ffn_w3"]),
            "w2": f32(inputs["L0_ffn_w2"]),
            "ws1": f32(inputs["L0_ffn_ws1"]), "ws3": f32(inputs["L0_ffn_ws3"]),
            "ws2": f32(inputs["L0_ffn_ws2"]),
            "h_filter": f32(inputs["L1_attn_h_filter"]),
            "g_filter": f32(inputs["L1_attn_g_filter"]),
            "h_synth": f32(inputs["L1_attn_h_synth"]),
            "g_synth": f32(inputs["L1_attn_g_synth"]),
            "gamma": f32(inputs["L1_attn_norm_gamma"]),
            "beta": f32(inputs["L1_attn_norm_beta"]),
            "w_gate": f32(inputs["L1_ffn_w_gate"]),
            "w_up": f32(inputs["L1_ffn_w_up"]),
            "w_down": f32(inputs["L1_ffn_w_down"]),
        }
        d.update(consts)
        im.append(d)
    res = run_bass_kernel_spmd(nc, im, core_ids=list(range(_NCORES)))
    out = np.concatenate([np.asarray(r["out"]) for r in res.results], axis=0)
    return out.astype(np.float32)
